# version 11
# speedup vs baseline: 1.1282x; 1.0882x over previous
import contextlib
import numpy as np

import concourse.bass as bass
import concourse.mybir as mybir
from concourse.bass_utils import run_bass_kernel_spmd

F32 = mybir.dt.float32
BF16 = mybir.dt.bfloat16
F16 = mybir.dt.float16
F8E4 = mybir.dt.float8e4
I32 = mybir.dt.int32
AF = mybir.ActivationFunctionType
ALU = mybir.AluOpType
AX = mybir.AxisListType

T = 2048
NTI = 16
D = 1024
DCH = 8
INNER = 4096
ICH = 32
H = 8
HD = 128
CK = 128
NCH = 16
MAGIC = 12582912.0
MAGIC64 = 6755399441055744.0
W_COL, M_COL = 96, 97
NEG = -1e30

ST_MU, ST_RS, ST_A, ST_B = 0, 16, 32, 48
ST_M, ST_L = 64, 65
SCL = 66
WS = 82
SWG = 84
RRC = 66



class _EngGuard:

    _IN_POS = {
        "tensor_copy": (1,), "tensor_scalar": (1, 2, 3), "tensor_tensor": (1, 2),
        "scalar_tensor_tensor": (1, 2, 3), "tensor_tensor_scan": (1, 2, 3),
        "reduce_sum": (1,), "reduce_max": (1,), "tensor_reduce": (1,),
        "reciprocal": (1,), "select": (1, 2, 3), "activation": (1,),
        "memset": (), "memzero": (),
    }

    def __init__(self, kind, sem, state):
        self._e = None
        self._kind = kind
        self._sem = sem
        self._st = state

    def _rng(self, a):
        try:
            n = a.name
            n = n() if callable(n) else n
            off = a.offset
            off = off() if callable(off) else off
            try:
                span = a.free_size
                span = span() if callable(span) else span
            except Exception:
                span = 10 ** 9
            return (str(n), int(off), int(off) + int(span))
        except Exception:
            return ("?unknown?", 0, 10 ** 9)

    @staticmethod
    def _ovl(x, y):
        return x[0] == y[0] and x[1] < y[2] and y[1] < x[2]

    def __getattr__(self, m):
        f = getattr(self._e, m)
        if m not in self._IN_POS:
            return f
        inpos = self._IN_POS[m]

        def call(*args, **kw):
            acc = []
            for i in inpos:
                if i < len(args) and hasattr(args[i], "shape"):
                    acc.append(self._rng(args[i]))
            for k in ("bias", "scale", "alpha"):
                if hasattr(kw.get(k), "shape"):
                    acc.append(self._rng(kw[k]))
            wr = []
            if args and hasattr(args[0], "shape"):
                wr.append(self._rng(args[0]))
            if hasattr(kw.get("accum_out"), "shape"):
                wr.append(self._rng(kw["accum_out"]))
            st = self._st
            hit = None
            for ent in st["hist"]:
                if (any(self._ovl(w, r) for w in ent[0] for r in acc + wr)
                        or any(self._ovl(er, w) for er in ent[1] for w in wr)):
                    hit = ent
            if hit is not None:
                if hit[4] is not None:
                    self._e.wait_ge(hit[4][0], hit[4][1])
                else:
                    if hit[3] is None:
                        st["ctr"] += 1
                        hit[2].then_inc(self._sem, 1)
                        hit[3] = st["ctr"]
                    self._e.wait_ge(self._sem, st["ctr"])
                idx = st["hist"].index(hit)
                st["hist"] = st["hist"][idx + 1:]
            r = f(*args, **kw)
            st["hist"].append([wr, acc, r, None, None])
            if len(st["hist"]) > 64:
                st["hist"] = st["hist"][-64:]
            return r
        return call

    def note_step_inc(self, instr, sem, val):
        for ent in self._st["hist"]:
            if ent[2] is instr:
                ent[4] = (sem, val)


def build_program(upto=9, dbg=""):
    nc = bass.Bass("TRN2", target_bir_lowering=False)

    def din(name, shape, dt=F32):
        return nc.dram_tensor(name, shape, dt, kind="ExternalInput").ap()

    x_own = din("x_own", [T, D]); x_prev = din("x_prev", [T, D])
    wq0 = din("wq0", [D, D]); wk0 = din("wk0", [D, D]); wv0 = din("wv0", [D, D])
    wgT = din("wgT", [D, INNER]); wuT = din("wuT", [D, INNER]); wdT = din("wdT", [INNER, D])
    qw = din("qw", [D, D]); kw = din("kw", [D, D]); vw = din("vw", [D, D])
    gw = din("gw", [D, D]); oww = din("oww", [D, D])
    bw = din("bw", [D, H]); aw = din("aw", [D, H])
    fcw = din("fcw", [D, INNER]); pww = din("pww", [INNER, D])
    w1T = din("w1T", [D, INNER]); w2T = din("w2T", [INNER, D])
    ln1row = din("ln1row", [128, D]); ln2row = din("ln2row", [128, D])
    onormrow = din("onormrow", [128, D]); b1fm_in = din("b1fm_in", [128, ICH])
    b2row_in = din("b2row_in", [128, D])
    alog_col = din("alog_col", [128, 1]); dtb_col = din("dtb_col", [128, 1])
    identf_in = din("identf_in", [128, 128])
    causal_in = din("causal_in", [128, 128])
    musU_in = din("musU_in", [128, 128]); musL_in = din("musL_in", [128, 128])
    muiU_in = din("muiU_in", [128, 128])
    maskattn = din("maskattn", [128, 1]); maskstate = din("maskstate", [128, 1])
    sel8_in = din("sel8_in", [8, 8 * 128])
    gather_off = din("gather_off", [128, 8], I32)

    out = nc.dram_tensor("out", [T, D], F32, kind="ExternalOutput").ap()
    dbg_out = nc.dram_tensor("dbg", [128, 1672], F32, kind="ExternalOutput").ap()

    kT_d = nc.dram_tensor("kT_d", [D, 2 * T], BF16).ap()
    v_d = nc.dram_tensor("v_d", [2 * T, D], BF16).ap()
    tg_d = nc.dram_tensor("tg_d", [D, INNER], BF16).ap()
    tu_d = nc.dram_tensor("tu_d", [D, INNER], BF16).ap()
    td_d = nc.dram_tensor("td_d", [INNER, D], BF16).ap()
    qf_d = nc.dram_tensor("qf_d", [D, T], BF16).ap()
    aT_d = nc.dram_tensor("aT_d", [NCH * H * 128, 128], BF16).ap()
    bC_d = nc.dram_tensor("bC_d", [NCH * H * 128, 128], F32).ap()
    oT_d = nc.dram_tensor("oT_d", [NCH * H * 128, 128], F32).ap()
    m2_d = nc.dram_tensor("m2_d", [NCH * H * 128, 128], BF16).ap()
    x1_d = nc.dram_tensor("x1_d", [T, D], F32).ap()
    cc_in = nc.dram_tensor("cc_in", [H * HD, HD], F32)
    cc_out = nc.dram_tensor("cc_out", [8 * H * HD, HD], F32)

    es = contextlib.ExitStack()

    def sb(stk, name, shape, dt=F32):
        return stk.enter_context(nc.sbuf_tensor(name, shape, dt))

    def pst(stk, name, shape, dt=F32):
        return stk.enter_context(nc.psum_tensor(name, shape, dt))

    with es:
        res = sb(es, "res", [128, NTI * D])
        stats = sb(es, "stats", [128, 104])
        ident = sb(es, "ident", [128, 128], BF16)
        identf = sb(es, "identf", [128, 128])
        causal = sb(es, "causal", [128, 128])
        musU = sb(es, "musU", [128, 128], BF16)
        musL = sb(es, "musL", [128, 128], BF16)
        muiU = sb(es, "muiU", [128, 128], BF16)
        mcA = sb(es, "mcA", [128, 1]); mcS = sb(es, "mcS", [128, 1])
        sel8 = sb(es, "sel8", [8, 8 * 128])
        goff = sb(es, "goff", [128, 8], I32)
        cvec = sb(es, "cvec", [128, 8])
        dump = sb(es, "dump", [128, 1024])
        ltile = sb(es, "ltile", [128, 1024])
        ltile2 = sb(es, "ltile2", [128, 1024])
        stg = sb(es, "stg", [128, 1024], BF16)
        stg2 = sb(es, "stg2", [128, 1024], BF16)
        dbgb = sb(es, "dbgb", [128, 1672 if dbg else 2])
        spc = sb(es, "spc", [128, 8])

        sems = {}
        for n in ["L", "G", "Te", "V", "S", "O", "Cc"]:
            sems[n] = [es.enter_context(nc.semaphore("s" + n)), 0]
        rawd = {n: [es.enter_context(nc.semaphore("d" + n)), 0, False]
                for n in ["L", "G"]}

        block = es.enter_context(nc.Block())
        ENG = {"L": block.sync, "G": block.gpsimd, "Te": block.tensor,
               "V": block.vector, "S": block.scalar}

        ilsem = {"V": es.enter_context(nc.semaphore("ilV")),
                 "S": es.enter_context(nc.semaphore("ilS"))}
        ilstate = {"V": {"hist": [], "ctr": 0}, "S": {"hist": [], "ctr": 0}}
        guards = {k: _EngGuard(k, ilsem[k], ilstate[k]) for k in ("V", "S")}

        def _drain(d):
            rw = rawd[d]
            pub = sems[d]
            if not rw[2]:
                return
            rw[2] = False
            pub[1] += 1

            def dbody(e):
                e.wait_ge(rw[0], rw[1])
                e.sem_inc(pub[0], 1)
            ENG[d](dbody)

        rawx = {}

        def dma_sem(name):
            if name not in rawx:
                rawx[name] = [es.enter_context(nc.semaphore("dx" + name)), 0]
            return rawx[name]

        def step(eng, fn, deps=(), dma=False, sem=None):
            for d in deps:
                if isinstance(d, str) and d in rawd:
                    _drain(d)
            if eng in rawd and not dma and sem is None:
                _drain(eng)
            dw = []
            for d in deps:
                if isinstance(d, str):
                    if sems[d][1] > 0:
                        dw.append((sems[d][0], sems[d][1]))
                elif d is not None:
                    dw.append((d[0], d[1]))
            me = sems[eng]
            tok = [None]

            def body(e):
                for s_, v_ in dw:
                    e.wait_ge(s_, v_)
                if eng in ("V", "S"):
                    guards[eng]._e = e
                    r = fn(guards[eng])
                else:
                    r = fn(e)
                if r is None:
                    return
                if dma:
                    tgt = dma_sem(sem) if sem else rawd[eng]
                    for ins in (r if isinstance(r, list) else [r]):
                        ins.then_inc(tgt[0], 16)
                        tgt[1] += 16
                    if sem:
                        tok[0] = (tgt[0], tgt[1])
                    else:
                        tgt[2] = True
                else:
                    r.then_inc(me[0], 1)
                    me[1] += 1
                    tok[0] = (me[0], me[1])
                    if eng in ("V", "S"):
                        guards[eng].note_step_inc(r, me[0], me[1])
            ENG[eng](body)
            return tok[0]

        def barrier():
            xtok = [(s_[0], s_[1]) for s_ in rawx.values() if s_[1] > 0]
            for e in ["L", "G", "Te", "V", "S"]:
                step(e, lambda _e: None,
                     deps=[d for d in ["L", "G", "Te", "V", "S", "O", "Cc"]
                           if d != e] + xtok)

        def DBG(tag, ap, col0=0):
            if dbg != tag:
                return
            def fd(v, ap=ap, col0=col0):
                n = ap.shape[-1] if len(ap.shape) == 2 else ap.shape[-1]
                return v.tensor_copy(dbgb[0:ap.shape[0], col0:col0 + n], ap)
            step("V", fd, deps=["S", "Te", "G", "L"])

        def fin():
            nc._sem_counts = {k: v[1] for k, v in sems.items()}
            def fo2(g):
                return g.dma_start(out=dbg_out[:, 0:dbgb.shape[1]], in_=dbgb[:, :])
            step("G", fo2, deps=["V", "S", "Te", "L"], dma=True)
            def fo(g):
                r = []
                orr = out.rearrange("(t p) d -> t p d", p=128)
                for t in range(NTI):
                    r.append(g.dma_start(out=orr[t], in_=res[:, t * D:(t + 1) * D]))
                return r
            step("G", fo, deps=["V", "S", "Te", "L"], dma=True)
            step("G", lambda g: None, deps=["G"])

        def ln_tile(src_ap, mu_j, rs_j, eps=1e-5, nd=D, dep0=()):
            def fv(v):
                v.reduce_sum(stats[:, ST_A:ST_A + 1], src_ap, AX.X)
                return v.tensor_scalar(stats[:, mu_j:mu_j + 1], stats[:, ST_A:ST_A + 1],
                                       1.0 / nd, None, ALU.mult)
            step("V", fv, deps=list(dep0))
            def fs(s):
                return s.activation(dump[:, 0:nd], src_ap, AF.Square,
                                    accum_out=stats[:, ST_B:ST_B + 1])
            ln_tile.src_stok = step("S", fs, deps=["V"])
            def fv2(v):
                v.tensor_tensor(stats[:, ST_A:ST_A + 1], stats[:, mu_j:mu_j + 1],
                                stats[:, mu_j:mu_j + 1], ALU.mult)
                v.tensor_scalar(stats[:, ST_B:ST_B + 1], stats[:, ST_B:ST_B + 1],
                                1.0 / nd, None, ALU.mult)
                v.tensor_tensor(stats[:, ST_A:ST_A + 1], stats[:, ST_B:ST_B + 1],
                                stats[:, ST_A:ST_A + 1], ALU.subtract)
                return v.tensor_scalar(stats[:, ST_A:ST_A + 1], stats[:, ST_A:ST_A + 1],
                                       eps, None, ALU.add)
            step("V", fv2, deps=["S"])
            def fs2(s):
                return s.activation(stats[:, ST_B:ST_B + 1], stats[:, ST_A:ST_A + 1],
                                    AF.Ln)
            step("S", fs2, deps=["V"])
            def fv3(v):
                return v.tensor_scalar(stats[:, ST_B + 1:ST_B + 2],
                                       stats[:, ST_B:ST_B + 1], -0.5, None, ALU.mult)
            step("V", fv3, deps=["S"])
            def fs3(s):
                return s.activation(stats[:, rs_j:rs_j + 1], stats[:, ST_B + 1:ST_B + 2],
                                    AF.Exp)
            ln_tile.end_stok = step("S", fs3, deps=["V"])

        def rsqrt_cols(col, n, dstcol):
            def fs(s):
                return s.activation(stats[:, ST_B:ST_B + n], stats[:, col:col + n],
                                    AF.Ln)
            step("S", fs, deps=["V"])
            def fv(v):
                return v.tensor_scalar(stats[:, ST_B:ST_B + n], stats[:, ST_B:ST_B + n],
                                       -0.5, None, ALU.mult)
            step("V", fv, deps=["S"])
            def fs2b(s):
                return s.activation(stats[:, dstcol:dstcol + n], stats[:, ST_B:ST_B + n],
                                    AF.Exp)
            step("S", fs2b, deps=["V"])

        def load_weight_bf16(wap, dst, dep=("Te",)):
            rows, cols = wap.shape
            wr = wap.rearrange("(c p) d -> c p d", p=128)
            def fg(g):
                return [g.dma_start(out=dst[:, ci * cols:(ci + 1) * cols], in_=wr[ci])
                        for ci in range(rows // 128)]
            step("G", fg, deps=list(dep), dma=True)

        def residual_add_ln(block_out_ap, t):
            ln_tile(block_out_ap, ST_A + 8, ST_A + 9)
            def fr(v):
                v.tensor_scalar(dump[:, 0:D], block_out_ap,
                                stats[:, ST_A + 8:ST_A + 9], stats[:, ST_A + 9:ST_A + 10],
                                ALU.subtract, ALU.mult)
                return v.tensor_tensor(res[:, t * D:(t + 1) * D],
                                       res[:, t * D:(t + 1) * D], dump[:, 0:D], ALU.add)
            step("V", fr, deps=["S"])

        def p0(sy):
            r = []
            xr = x_own.rearrange("(t p) d -> t p d", p=128)
            for t in range(NTI):
                r.append(sy.dma_start(out=res[:, t * D:(t + 1) * D], in_=xr[t]))
            for dst, src in [(causal, causal_in), (identf, identf_in),
                             (mcA, maskattn), (mcS, maskstate)]:
                r.append(sy.dma_start(out=dst[:, :], in_=src[:, :]))
            r.append(sy.dma_start(out=sel8[:, :], in_=sel8_in[:, :]))
            r.append(sy.dma_start(out=stats[:, 94:95], in_=alog_col[:, :]))
            r.append(sy.dma_start(out=stats[:, 95:96], in_=dtb_col[:, :]))
            return r
        step("L", p0, dma=True)

        def p0g(g):
            r = []
            for dst, src in [(ident, identf_in), (musU, musU_in), (musL, musL_in),
                             (muiU, muiU_in)]:
                r.append(g.dma_start(out=dst[:, :], in_=src[:, :]))
            r.append(g.dma_start(out=goff[:, :], in_=gather_off[:, :]))
            return r
        step("G", p0g, dma=True)

        step("S", lambda s: s.activation(cvec[:, 0:1], stats[:, 94:95], AF.Exp),
             deps=["L"])

        def p0v(v):
            v.memset(dbgb[:, :], 0.0)
            v.memset(spc[:, :], 0.0)
            v.memset(stats[:, 0:94], 0.0)
            v.memset(dump[:, :], 0.0)
            v.memset(ltile[:, :], 0.0)
            v.memset(ltile2[:, :], 0.0)
            v.tensor_scalar(cvec[:, 0:1], cvec[:, 0:1], -1.0, None, ALU.mult)
            v.tensor_copy(cvec[:, 1:2], stats[:, 95:96])
            v.memset(stats[:, W_COL:W_COL + 1], 0.75)
            v.tensor_scalar(stats[:, W_COL:W_COL + 1], stats[:, W_COL:W_COL + 1],
                            MAGIC64, None, ALU.add)
            v.tensor_scalar(stats[:, W_COL:W_COL + 1], stats[:, W_COL:W_COL + 1],
                            -MAGIC64, None, ALU.add)
            return v.tensor_scalar(stats[:, M_COL:M_COL + 1],
                                   stats[:, W_COL:W_COL + 1],
                                   MAGIC64, MAGIC, ALU.mult, ALU.add)
        step("V", p0v, deps=["S"])
        barrier()
        if upto <= 0:
            fin()
            return nc

        p1 = contextlib.ExitStack()
        with p1:
            qfm = sb(p1, "qfm", [128, DCH * T], BF16)
            pT1 = pst(p1, "pT1", [128, 1024], BF16)
            pA = pst(p1, "pA", [128, 512])
            pB = pst(p1, "pB", [128, 512])

            p1h = contextlib.ExitStack()
            with p1h:
                hfm = sb(p1h, "hfm", [128, DCH * T], BF16)

                def build_h(half):
                    src = x_prev if half == 0 else x_own
                    xr = src.rearrange("(t p) d -> t p d", p=128)
                    for t in range(NTI):
                        if half == 0:
                            def fl(sy, t=t, xr=xr):
                                return sy.dma_start(out=ltile[:, 0:D], in_=xr[t])
                            step("L", fl, deps=["V", "S"], dma=True)
                            src_ap, mu, rs = ltile[:, 0:D], ST_MU, ST_RS
                            ln_tile(src_ap, mu, rs, dep0=("L",))
                        else:
                            src_ap = res[:, t * D:(t + 1) * D]
                            mu, rs = ST_MU + t, ST_RS + t
                            ln_tile(src_ap, mu, rs)
                        def fap(v, src_ap=src_ap, mu=mu, rs=rs):
                            return v.tensor_scalar(stg[:, 0:D], src_ap,
                                                   stats[:, mu:mu + 1],
                                                   stats[:, rs:rs + 1],
                                                   ALU.subtract, ALU.mult)
                        step("V", fap, deps=["S"])
                        if half == 1 and t == 0:
                            DBG("h0", stg[:, 0:D])
                            DBG("h0stats", stats[:, 0:96])
                        def ftp(te, t=t):
                            for i in range(DCH):
                                r = te.transpose(pT1[:, i * 128:(i + 1) * 128],
                                                 stg[:, i * 128:(i + 1) * 128],
                                                 ident[:, :])
                            return r
                        step("Te", ftp, deps=["V"])
                        def fev(v, t=t):
                            for i in range(DCH):
                                r = v.tensor_copy(
                                    hfm[:, i * T + t * 128:i * T + (t + 1) * 128],
                                    pT1[:, i * 128:(i + 1) * 128])
                            return r
                        step("V", fev, deps=["Te"])

                def proj_kv(half):
                    key0 = 0 if half == 0 else T
                    pw2 = contextlib.ExitStack()
                    with pw2:
                        wkb = sb(pw2, "wkb%d" % half, [128, DCH * D], BF16)
                        wvb = sb(pw2, "wvb%d" % half, [128, DCH * D], BF16)
                        load_weight_bf16(wk0, wkb)
                        load_weight_bf16(wv0, wvb)
                        barrier()
                        for of in range(DCH):
                            for w in range(4):
                                def fm(te, of=of, w=w):
                                    for ci in range(DCH):
                                        r = te.matmul(
                                            pA[:, 0:512],
                                            wkb[:, ci * D + of * 128:ci * D + (of + 1) * 128],
                                            hfm[:, ci * T + w * 512:ci * T + (w + 1) * 512],
                                            start=(ci == 0), stop=(ci == DCH - 1))
                                    return r
                                step("Te", fm, deps=["V", "L"])
                                def fe(v):
                                    return v.tensor_copy(stg[:, 0:512], pA[:, 0:512])
                                step("V", fe, deps=["Te", "L"])
                                def fd(sy, of=of, w=w, key0=key0):
                                    return sy.dma_start(
                                        out=kT_d[of * 128:(of + 1) * 128,
                                                 key0 + w * 512:key0 + (w + 1) * 512],
                                        in_=stg[:, 0:512])
                                step("L", fd, deps=["V"], dma=True)
                        for t in range(NTI):
                            for nn2 in range(2):
                                def fm(te, t=t, nn2=nn2):
                                    for ci in range(DCH):
                                        r = te.matmul(
                                            pB[:, 0:512],
                                            hfm[:, ci * T + t * 128:ci * T + (t + 1) * 128],
                                            wvb[:, ci * D + nn2 * 512:ci * D + (nn2 + 1) * 512],
                                            start=(ci == 0), stop=(ci == DCH - 1))
                                    return r
                                step("Te", fm, deps=["V", "L"])
                                def fe(v, nn2=nn2):
                                    return v.tensor_copy(stg2[:, nn2 * 512:(nn2 + 1) * 512],
                                                         pB[:, 0:512])
                                step("V", fe, deps=["Te", "L"])
                            def fd(sy, t=t, key0=key0):
                                return sy.dma_start(
                                    out=v_d[key0 + t * 128:key0 + (t + 1) * 128, :],
                                    in_=stg2[:, 0:D])
                            step("L", fd, deps=["V"], dma=True)

                build_h(0)
                proj_kv(0)
                build_h(1)
                proj_kv(1)

                pq = contextlib.ExitStack()
                with pq:
                    wqb = sb(pq, "wqb", [128, DCH * D], BF16)
                    load_weight_bf16(wq0, wqb)
                    barrier()
                    for of in range(DCH):
                        for w in range(4):
                            def fm(te, of=of, w=w):
                                for ci in range(DCH):
                                    r = te.matmul(
                                        pA[:, 0:512],
                                        wqb[:, ci * D + of * 128:ci * D + (of + 1) * 128],
                                        hfm[:, ci * T + w * 512:ci * T + (w + 1) * 512],
                                        start=(ci == 0), stop=(ci == DCH - 1))
                                return r
                            step("Te", fm, deps=["V", "L"])
                            def fe(v, of=of, w=w):
                                return v.tensor_scalar(
                                    qfm[:, of * T + w * 512:of * T + (w + 1) * 512],
                                    pA[:, 0:512], 1.0 / 32.0, None, ALU.mult)
                            step("V", fe, deps=["Te"])
                barrier()

            fl1 = contextlib.ExitStack()
            with fl1:
                kst = sb(fl1, "kst", [128, 2 * DCH * 512], BF16)
                vst = sb(fl1, "vst", [128, 2 * 4 * D], BF16)
                sco = sb(fl1, "sco", [128, 4096])
                pbuf = sb(fl1, "pbuf", [128, 4096], BF16)
                pC = pst(fl1, "pC", [128, 512])
                pD = pst(fl1, "pD", [128, 512])

                for qt in range(NTI):
                    ext = T + (qt + 1) * 128
                    nwin = (ext + 511) // 512
                    for w in range(nwin):
                        k0w = w * 512
                        kn = min(ext, k0w + 512) - k0w
                        buf = w % 2
                        def fk(sy, k0w=k0w, kn=kn, buf=buf):
                            return sy.dma_start(
                                out=kst[:, buf * DCH * 512:buf * DCH * 512 + DCH * kn]
                                    .rearrange("p (c n) -> p c n", c=DCH),
                                in_=kT_d[:, k0w:k0w + kn]
                                    .rearrange("(c p) n -> c p n", p=128).transpose([1, 0, 2]))
                        step("L", fk, deps=["Te"], dma=True)
                        def fm(te, kn=kn, buf=buf, qt=qt):
                            for ci in range(DCH):
                                r = te.matmul(
                                    pA[:, 0:kn],
                                    qfm[:, ci * T + qt * 128:ci * T + (qt + 1) * 128],
                                    kst[:, buf * DCH * 512 + ci * kn:
                                        buf * DCH * 512 + (ci + 1) * kn],
                                    start=(ci == 0), stop=(ci == DCH - 1))
                            return r
                        step("Te", fm, deps=["L", "V"])
                        def fe(v, w=w, k0w=k0w, kn=kn, qt=qt):
                            r = None
                            col = k0w
                            while col < k0w + kn:
                                ce = min(col + 128, k0w + kn)
                                if ce <= T:
                                    r = v.tensor_scalar(sco[:, col:ce],
                                                        pA[:, col - k0w:ce - k0w],
                                                        mcA[:, 0:1], None, ALU.add)
                                elif (col - T) // 128 == qt:
                                    r = v.tensor_tensor(sco[:, col:ce],
                                                        pA[:, col - k0w:ce - k0w],
                                                        causal[:, 0:ce - col], ALU.add)
                                else:
                                    r = v.tensor_copy(sco[:, col:ce],
                                                      pA[:, col - k0w:ce - k0w])
                                col = ce
                            return r
                        step("V", fe, deps=["Te"])
                    if qt == 0:
                        DBG("sco0", sco[:, 0:1672])
                    def fmx(v, ext=ext):
                        return v.reduce_max(stats[:, ST_M:ST_M + 1], sco[:, 0:ext],
                                            AX.X, negate=True)
                    step("V", fmx)
                    def fex(s, ext=ext):
                        return s.activation(pbuf[:, 0:ext], sco[:, 0:ext], AF.Exp,
                                            bias=stats[:, ST_M:ST_M + 1],
                                            accum_out=stats[:, ST_L:ST_L + 1])
                    step("S", fex, deps=["V"])
                    def frc(v):
                        return v.reciprocal(stats[:, ST_L + 1:ST_L + 2],
                                            stats[:, ST_L:ST_L + 1])
                    step("V", frc, deps=["S"])
                    nkt = ext // 128
                    first = True
                    for g0 in range(0, nkt, 4):
                        g1 = min(nkt, g0 + 4)
                        buf = (g0 // 4) % 2
                        def fv(sy, g0=g0, g1=g1, buf=buf):
                            return sy.dma_start(
                                out=vst[:, buf * 4 * D:buf * 4 * D + (g1 - g0) * D]
                                    .rearrange("p (a b) -> p a b", b=D),
                                in_=v_d[g0 * 128:g1 * 128, :]
                                    .rearrange("(a p) b -> a p b", p=128).transpose([1, 0, 2]))
                        step("L", fv, deps=["Te"], dma=True)
                        for kt in range(g0, g1):
                            def ftp(te, kt=kt):
                                return te.transpose(pT1[:, 0:128],
                                                    pbuf[:, kt * 128:(kt + 1) * 128],
                                                    ident[:, :])
                            step("Te", ftp, deps=["V", "S"])
                            def fpe(v):
                                return v.tensor_copy(stg[:, 0:128], pT1[:, 0:128])
                            step("V", fpe, deps=["Te"])
                            def fav(te, kt=kt, buf=buf, g0=g0, first=first, nkt=nkt):
                                te.matmul(pC[:, 0:512], stg[:, 0:128],
                                          vst[:, buf * 4 * D + (kt - g0) * D:
                                              buf * 4 * D + (kt - g0) * D + 512],
                                          start=first, stop=(kt == nkt - 1))
                                return te.matmul(
                                    pD[:, 0:512], stg[:, 0:128],
                                    vst[:, buf * 4 * D + (kt - g0) * D + 512:
                                        buf * 4 * D + (kt - g0 + 1) * D],
                                    start=first, stop=(kt == nkt - 1))
                            step("Te", fav, deps=["V", "L"])
                            first = False
                    def ff(v):
                        v.tensor_scalar(ltile[:, 0:512], pC[:, 0:512],
                                        stats[:, ST_L + 1:ST_L + 2], None, ALU.mult)
                        return v.tensor_scalar(ltile[:, 512:1024], pD[:, 0:512],
                                               stats[:, ST_L + 1:ST_L + 2], None,
                                               ALU.mult)
                    step("V", ff, deps=["Te"])
                    if qt == 0:
                        DBG("acc0", ltile[:, 0:D])
                        DBG("pb0", pbuf[:, 0:1672])
                        DBG("l0", stats[:, 64:67])
                    residual_add_ln(ltile[:, 0:D], qt)
            barrier()
        if upto <= 1:
            fin()
            return nc

        p2 = contextlib.ExitStack()
        with p2:
            scratch = sb(p2, "scratch", [128, NTI * 1024])
            nfm = sb(p2, "nfm", [128, DCH * T], BF16)
            wstr = sb(p2, "wstr", [128, 2 * 512])
            tstr = sb(p2, "tstr", [128, 512], BF16)
            s2bc = sb(p2, "s2bc", [128, 512])
            prow = sb(p2, "prow", [128, 512])
            n2t = sb(p2, "n2t", [128, 256], BF16)
            iv2 = sb(p2, "iv2", [128, 8])
            tbuf = sb(p2, "tbuf", [128, 2 * DCH * 128], BF16)
            tbuf2 = sb(p2, "tbuf2", [128, 2 * DCH * 128], BF16)
            pA2 = pst(p2, "pA2", [128, 512])
            pB2 = pst(p2, "pB2", [128, 512])
            pR2 = pst(p2, "pR2", [128, 512])
            pD2 = pst(p2, "pD2", [128, 512])
            pT2 = pst(p2, "pT2", [128, 1024], BF16)

            DBG("p2res", res[:, 0:1024], 0)
            DBG("p2res", res[:, 15 * D:15 * D + 512], 1024)
            for t in range(NTI):
                ln_tile(res[:, t * D:(t + 1) * D], ST_MU + t, ST_RS + t)
                def fap(v, t=t):
                    v.tensor_scalar(ltile[:, 0:D], res[:, t * D:(t + 1) * D],
                                    stats[:, ST_MU + t:ST_MU + t + 1],
                                    stats[:, ST_RS + t:ST_RS + t + 1],
                                    ALU.subtract, ALU.mult)
                    v.reduce_max(stats[:, ST_A:ST_A + 1], ltile[:, 0:D], AX.X,
                                 apply_absolute_value=True)
                    v.tensor_scalar(stats[:, ST_A:ST_A + 1], stats[:, ST_A:ST_A + 1],
                                    1e-5, None, ALU.max)
                    v.tensor_scalar(stats[:, SCL + t:SCL + t + 1],
                                    stats[:, ST_A:ST_A + 1], 1.0 / 127.0, None, ALU.mult)
                    v.reciprocal(stats[:, ST_B:ST_B + 1], stats[:, ST_A:ST_A + 1])
                    v.tensor_scalar(stats[:, ST_B:ST_B + 1], stats[:, ST_B:ST_B + 1],
                                    127.0, None, ALU.mult)
                    v.tensor_scalar(ltile2[:, 0:D], ltile[:, 0:D],
                                    stats[:, ST_B:ST_B + 1],
                                    stats[:, M_COL:M_COL + 1],
                                    ALU.mult, ALU.add)
                    return v.tensor_scalar(stg[:, 0:D], ltile2[:, 0:D],
                                           stats[:, M_COL:M_COL + 1], None,
                                           ALU.subtract)
                step("V", fap, deps=["S"])
                if t == 0:
                    DBG("p2res", stg[:, 0:128], 1536)
                def ftp(te, t=t):
                    for i in range(DCH):
                        r = te.transpose(pT2[:, i * 128:(i + 1) * 128],
                                         stg[:, i * 128:(i + 1) * 128], ident[:, :])
                    return r
                step("Te", ftp, deps=["V"])
                def fev(v, t=t):
                    for i in range(DCH):
                        r = v.tensor_copy(nfm[:, i * T + t * 128:i * T + (t + 1) * 128],
                                          pT2[:, i * 128:(i + 1) * 128])
                    return r
                step("V", fev, deps=["Te"])

            for t5 in range(16):
                DBG("p2r2", res[:, t5 * D:t5 * D + 64], t5 * 64)
            DBG("p2st", stats[:, 0:96], 0)
            for i5 in range(8):
                DBG("p2st", nfm[:, i5 * T + 15 * 128:i5 * T + 16 * 128], 96 + i5 * 128)
            DBG("p2st", nfm[:, 0:128], 1120)
            for wi, (wap, tdram) in enumerate([(wgT, tg_d), (wuT, tu_d), (wdT, td_d)]):
                rows, cols = wap.shape
                wr = wap.rearrange("(a p) (b q) -> a p b q", p=128, q=512)
                nb = cols // 512
                nch2 = (rows // 128) * nb
                step("V", lambda v: v.memset(stats[:, WS:WS + 1], 0.0), deps=["S"])
                for ch in range(nch2):
                    def fl(sy, ch=ch, wr=wr):
                        return sy.dma_start(
                            out=wstr[:, (ch % 2) * 512:(ch % 2) * 512 + 512],
                            in_=wr[ch // nb, :, ch % nb, :])
                    step("L", fl, deps=["S"], dma=True)
                    def fs(s, ch=ch):
                        return s.activation(dump[:, 0:512],
                                            wstr[:, (ch % 2) * 512:(ch % 2) * 512 + 512],
                                            AF.Abs, accum_out=stats[:, ST_A:ST_A + 1])
                    step("S", fs, deps=["L", "V"])
                    def fa(v):
                        return v.tensor_tensor(stats[:, WS:WS + 1], stats[:, WS:WS + 1],
                                               stats[:, ST_A:ST_A + 1], ALU.add)
                    step("V", fa, deps=["S"])
                def fbf_t(te):
                    return te.transpose(pA2[0:1, 0:128], stats[:, WS:WS + 1],
                                        identf[:, :])
                step("Te", fbf_t, deps=["S", "V"])
                def fbf_r(v, rows=rows, cols=cols):
                    v.reduce_sum(stats[0:1, 90:91], pA2[0:1, 0:128], AX.X)
                    v.tensor_scalar(stats[0:1, 92:93], stats[0:1, 90:91],
                                    1.0 / (rows * cols), None, ALU.mult)
                    v.tensor_scalar(stats[0:1, 92:93],
                                    stats[0:1, 92:93], 1e-5, None, ALU.max)
                    return v.reciprocal(stats[0:1, 93:94], stats[0:1, 92:93])
                step("V", fbf_r, deps=["Te"])
                def fbf_b(te):
                    te.matmul(pA2[:, 0:1], sel8[0:1, 0:128], stats[0:1, 93:94],
                              start=True, stop=True)
                    return te.matmul(pA2[:, 1:2], sel8[0:1, 0:128],
                                     stats[0:1, 92:93], start=True, stop=True)
                step("Te", fbf_b, deps=["V"])
                def fbf_e(v, wi=wi):
                    v.tensor_copy(stats[:, WS + 1:WS + 2], pA2[:, 0:1])
                    v.tensor_copy(stats[:, SWG + wi:SWG + wi + 1], pA2[:, 0:1])
                    return v.tensor_copy(stats[:, SWG + 3 + wi:SWG + 4 + wi],
                                         pA2[:, 1:2])
                step("V", fbf_e, deps=["Te"])
                DBG("p2w", stats[:, 80:96], wi * 16)
                DBG("p2w", pA2[:, 0:2], 48 + wi * 2)
                DBG("p2w", stats[0:1, WS:WS + 4], 54 + wi * 4)
                tdr = tdram.rearrange("(a p) (b q) -> a p b q", p=128, q=512)
                for ch in range(nch2):
                    def fl(sy, ch=ch, wr=wr):
                        return sy.dma_start(
                            out=wstr[:, (ch % 2) * 512:(ch % 2) * 512 + 512],
                            in_=wr[ch // nb, :, ch % nb, :])
                    step("L", fl, deps=["V", "G"], dma=True)
                    def ft(v, ch=ch):
                        sl = wstr[:, (ch % 2) * 512:(ch % 2) * 512 + 512]
                        v.tensor_scalar(dump[:, 0:512], sl, stats[:, WS + 1:WS + 2],
                                        stats[:, M_COL:M_COL + 1],
                                        ALU.mult, ALU.add)
                        v.tensor_scalar(dump[:, 512:1024], dump[:, 0:512],
                                        stats[:, M_COL:M_COL + 1], 1.0,
                                        ALU.subtract, ALU.min)
                        return v.tensor_scalar(tstr[:, 0:512], dump[:, 512:1024],
                                               -1.0, None, ALU.max)
                    step("V", ft, deps=["L", "G"])
                    def fst(g, ch=ch, tdr=tdr):
                        return g.dma_start(out=tdr[ch // nb, :, ch % nb, :],
                                           in_=tstr[:, 0:512])
                    step("G", fst, deps=["V"], dma=True)
            barrier()

            tgr_p = tg_d.rearrange("(c p) o -> c p o", p=128).transpose([1, 0, 2])
            tur_p = tu_d.rearrange("(c p) o -> c p o", p=128).transpose([1, 0, 2])
            tdr_p = td_d.rearrange("(c p) o -> c p o", p=128).transpose([1, 0, 2])
            for tb in range(4):
                tok0 = tb * 512
                def fbt(te, tb=tb):
                    return te.transpose(pR2[0:4, 0:128],
                                        stats[:, SCL + 4 * tb:SCL + 4 * tb + 4],
                                        identf[:, :])
                step("Te", fbt, deps=["V", "S"])
                def fbe(v):
                    v.memset(ltile[0:8, 0:128], 0.0)
                    return v.tensor_copy(ltile[0:4, 0:128], pR2[0:4, 0:128])
                step("V", fbe, deps=["Te"])
                def fbm(te):
                    for i in range(4):
                        r = te.matmul(pR2[:, i * 128:(i + 1) * 128],
                                      sel8[:, i * 128:i * 128 + 128],
                                      ltile[0:8, 0:128],
                                      start=True, stop=True)
                    return r
                step("Te", fbm, deps=["V"])
                def fbc(v):
                    return v.tensor_copy(s2bc[:, 0:512], pR2[:, 0:512])
                step("V", fbc, deps=["Te"])

                for oc in range(ICH):
                    b = oc % 2
                    def flg(sy, oc=oc, b=b):
                        r1 = sy.dma_start(
                            out=tbuf[:, b * DCH * 128:(b + 1) * DCH * 128]
                                .rearrange("p (c n) -> p c n", c=DCH),
                            in_=tgr_p[:, :, oc * 128:(oc + 1) * 128])
                        r2 = sy.dma_start(
                            out=tbuf2[:, b * DCH * 128:(b + 1) * DCH * 128]
                                .rearrange("p (c n) -> p c n", c=DCH),
                            in_=tur_p[:, :, oc * 128:(oc + 1) * 128])
                        return [r1, r2]
                    step("L", flg, deps=["Te"], dma=True)
                    def fmg(te, b=b, tok0=tok0):
                        for ci in range(DCH):
                            te.matmul(pA2[:, 0:512],
                                      tbuf[:, b * DCH * 128 + ci * 128:
                                           b * DCH * 128 + (ci + 1) * 128],
                                      nfm[:, ci * T + tok0:ci * T + tok0 + 512],
                                      start=(ci == 0), stop=(ci == DCH - 1))
                        for ci in range(DCH):
                            r = te.matmul(pB2[:, 0:512],
                                          tbuf2[:, b * DCH * 128 + ci * 128:
                                                b * DCH * 128 + (ci + 1) * 128],
                                          nfm[:, ci * T + tok0:ci * T + tok0 + 512],
                                          start=(ci == 0), stop=(ci == DCH - 1))
                        return r
                    step("Te", fmg, deps=["L", "V", "S"])
                    def feg(v):
                        return v.scalar_tensor_tensor(ltile[:, 0:512], pA2[:, 0:512],
                                                      stats[:, SWG + 3:SWG + 4],
                                                      s2bc[:, 0:512],
                                                      ALU.mult, ALU.mult)
                    step("V", feg, deps=["Te"])
                    def fsi(s):
                        return s.activation(ltile2[:, 0:512], ltile[:, 0:512], AF.Silu)
                    step("S", fsi, deps=["V"])
                    def fup(v, oc=oc):
                        v.scalar_tensor_tensor(ltile[:, 0:512], pB2[:, 0:512],
                                               stats[:, SWG + 4:SWG + 5],
                                               s2bc[:, 0:512], ALU.mult, ALU.mult)
                        return v.tensor_tensor(scratch[:, oc * 512:(oc + 1) * 512],
                                               ltile2[:, 0:512], ltile[:, 0:512],
                                               ALU.mult)
                    step("V", fup, deps=["S"])
                    if tb == 0 and oc == 0:
                        DBG("p2oc", ltile2[:, 0:512], 0)
                        DBG("p2oc", ltile[:, 0:512], 512)
                        DBG("p2oc", scratch[:, 0:512], 1024)
                        DBG("p2oc", stats[:, 80:96], 1536)
                        DBG("p2oc", s2bc[:, 0:64], 1552)
                def fq1(v):
                    v.tensor_tensor(prow[:, 0:512], scratch[:, 0:512],
                                    scratch[:, 512:1024], ALU.max)
                    v.tensor_tensor(ltile2[:, 0:512], scratch[:, 0:512],
                                    scratch[:, 512:1024], ALU.min)
                    for ic in range(2, ICH):
                        v.tensor_tensor(prow[:, 0:512], prow[:, 0:512],
                                        scratch[:, ic * 512:(ic + 1) * 512], ALU.max)
                        v.tensor_tensor(ltile2[:, 0:512], ltile2[:, 0:512],
                                        scratch[:, ic * 512:(ic + 1) * 512], ALU.min)
                    v.tensor_scalar(ltile2[:, 0:512], ltile2[:, 0:512], -1.0, None,
                                    ALU.mult)
                    return v.tensor_tensor(prow[:, 0:512], prow[:, 0:512],
                                           ltile2[:, 0:512], ALU.max)
                step("V", fq1)
                def fi2(te):
                    for i in range(4):
                        r = te.transpose(pR2[:, i * 128:(i + 1) * 128],
                                         prow[:, i * 128:(i + 1) * 128], identf[:, :])
                    return r
                step("Te", fi2, deps=["V"])
                def fi2e(v):
                    for i in range(4):
                        v.reduce_max(iv2[:, i:i + 1], pR2[:, i * 128:(i + 1) * 128],
                                     AX.X)
                    v.tensor_scalar(iv2[:, 0:4], iv2[:, 0:4], 1e-5, None, ALU.max)
                    v.reciprocal(iv2[:, 4:8], iv2[:, 0:4])
                    v.tensor_scalar(iv2[:, 0:4], iv2[:, 0:4], 1.0 / 127.0, None,
                                    ALU.mult)
                    return v.tensor_scalar(iv2[:, 4:8], iv2[:, 4:8], 127.0, None,
                                           ALU.mult)
                step("V", fi2e, deps=["Te"])
                def fi3(te):
                    for i in range(4):
                        r = te.transpose(pD2[0:1, i * 128:(i + 1) * 128],
                                         iv2[:, 4 + i:5 + i], identf[:, :])
                    return r
                step("Te", fi3, deps=["V"])
                def fi3e(v):
                    return v.tensor_copy(ltile[0:1, 0:512], pD2[0:1, 0:512])
                step("V", fi3e, deps=["Te"])
                def fi4(te):
                    for i in range(4):
                        r = te.matmul(pD2[:, i * 128:(i + 1) * 128],
                                      sel8[0:1, 0:128],
                                      ltile[0:1, i * 128:(i + 1) * 128],
                                      start=True, stop=True)
                    return r
                step("Te", fi4, deps=["V"])
                def fi4e(v):
                    return v.tensor_copy(s2bc[:, 0:512], pD2[:, 0:512])
                step("V", fi4e, deps=["Te"])
                if tb == 0:
                    DBG("p2mix", scratch[:, 0:512], 0)
                    DBG("p2mix", s2bc[:, 0:512], 512)
                    DBG("p2mix", iv2[:, 0:8], 1024)
                    DBG("p2mix", prow[:, 0:512], 1032)
                    DBG("p2mix", nfm[:, 0:128], 1544)
                for ph in range(2):
                    c0 = ph * 256
                    for ic in range(ICH):
                        def fn2(v, ic=ic, c0=c0):
                            v.tensor_tensor(ltile2[:, 0:256],
                                            scratch[:, ic * 512 + c0:ic * 512 + c0 + 256],
                                            s2bc[:, c0:c0 + 256], ALU.mult)
                            v.tensor_scalar(ltile2[:, 256:512], ltile2[:, 0:256],
                                            stats[:, M_COL:M_COL + 1], None, ALU.add)
                            return v.tensor_scalar(n2t[:, 0:256], ltile2[:, 256:512],
                                                   stats[:, M_COL:M_COL + 1], None,
                                                   ALU.subtract)
                        step("V", fn2, deps=["Te"])
                        def fld(sy, ic=ic):
                            return sy.dma_start(
                                out=tbuf[:, 0:1024],
                                in_=td_d[ic * 128:(ic + 1) * 128, :])
                        step("L", fld, deps=["Te"], dma=True)
                        def fmd(te, ic=ic):
                            for ti in range(2):
                                for nn2 in range(2):
                                    pp = [pA2, pB2, pR2, pD2][ti * 2 + nn2]
                                    r = te.matmul(pp[:, 0:512],
                                                  n2t[:, ti * 128:(ti + 1) * 128],
                                                  tbuf[:, nn2 * 512:(nn2 + 1) * 512],
                                                  start=(ic == 0), stop=(ic == ICH - 1))
                            return r
                        step("Te", fmd, deps=["V", "L"])
                    for ti in range(2):
                        t_abs = tb * 4 + ph * 2 + ti
                        def fde(v, ti=ti, ph=ph, t_abs=t_abs):
                            v.tensor_scalar(ltile[:, 0:512],
                                            [pA2, pB2, pR2, pD2][ti * 2][:, 0:512],
                                            iv2[:, ph * 2 + ti:ph * 2 + ti + 1],
                                            stats[:, SWG + 5:SWG + 6],
                                            ALU.mult, ALU.mult)
                            return v.tensor_scalar(ltile[:, 512:1024],
                                                   [pA2, pB2, pR2, pD2][ti * 2 + 1][:, 0:512],
                                                   iv2[:, ph * 2 + ti:ph * 2 + ti + 1],
                                                   stats[:, SWG + 5:SWG + 6],
                                                   ALU.mult, ALU.mult)
                        step("V", fde, deps=["Te"])
                        residual_add_ln(ltile[:, 0:D], t_abs)
            barrier()
        if upto <= 2:
            fin()
            return nc

        p3 = contextlib.ExitStack()
        with p3:
            for t in range(NTI):
                ln_tile(res[:, t * D:(t + 1) * D], ST_MU + t, ST_RS + t)
            def frr(v):
                v.tensor_tensor(stats[:, ST_A:ST_A + NTI], stats[:, ST_RS:ST_RS + NTI],
                                stats[:, ST_RS:ST_RS + NTI], ALU.mult)
                return v.tensor_scalar(stats[:, ST_A:ST_A + NTI],
                                       stats[:, ST_A:ST_A + NTI],
                                       -1e-5, 1.000001, ALU.mult, ALU.add)
            step("V", frr, deps=["S"])
            def frr2(s):
                return s.activation(stats[:, ST_B:ST_B + NTI], stats[:, ST_A:ST_A + NTI],
                                    AF.Ln)
            step("S", frr2, deps=["V"])
            def frr2v(v):
                return v.tensor_scalar(stats[:, ST_A:ST_A + NTI],
                                       stats[:, ST_B:ST_B + NTI], -0.5, None, ALU.mult)
            step("V", frr2v, deps=["S"])
            def frr2b(s):
                return s.activation(stats[:, ST_B:ST_B + NTI], stats[:, ST_A:ST_A + NTI],
                                    AF.Exp)
            step("S", frr2b, deps=["V"])
            def frr3(v):
                return v.tensor_tensor(stats[:, RRC:RRC + NTI], stats[:, ST_RS:ST_RS + NTI],
                                       stats[:, ST_B:ST_B + NTI], ALU.mult)
            step("V", frr3, deps=["S"])

            ln1r = sb(p3, "ln1r", [128, D])
            zr = sb(p3, "zr", [8, T])
            zr2 = sb(p3, "zr2", [8, T])
            zrone = sb(p3, "zrone", [8, 128])
            acol = sb(p3, "acol", [128, NTI * H])
            lbacol = sb(p3, "lbacol", [128, NTI * H])
            lam = sb(p3, "lam", [128, NTI * H])
            kcoef = sb(p3, "kcoef", [128, NTI * H])
            lamC = sb(p3, "lamC", [128, NTI * H])
            bwb = sb(p3, "bwb", [128, DCH * H], BF16)
            awb = sb(p3, "awb", [128, DCH * H], BF16)
            step("L", lambda sy: sy.dma_start(out=ln1r[:, :], in_=ln1row[:, :]),
                 deps=[], dma=True)
            load_weight_bf16(bw, bwb)
            load_weight_bf16(aw, awb)
            barrier()

            p3a = contextlib.ExitStack()
            with p3a:
                hblk = sb(p3a, "hblk", [128, DCH * 512], BF16)
                wst3 = sb(p3a, "wst3", [128, DCH * D], BF16)
                qtm = sb(p3a, "qtm", [128, 4 * D], BF16)
                ktm = sb(p3a, "ktm", [128, 4 * D], BF16)
                vtm = sb(p3a, "vtm", [128, 4 * D], BF16)
                qfb = sb(p3a, "qfb", [128, DCH * 512], BF16)
                kfb = sb(p3a, "kfb", [128, DCH * 512], BF16)
                dT = sb(p3a, "dT", [128, 128], BF16)
                dU = sb(p3a, "dU", [128, 128], BF16)
                tpow = sb(p3a, "tpow", [128, 7 * 128], BF16)
                nbuf = sb(p3a, "nbuf", [128, 2 * 128], BF16)
                ybuf = sb(p3a, "ybuf", [128, 256], BF16)
                kp = sb(p3a, "kp", [128, 128], BF16)
                pmt = sb(p3a, "pmt", [128, 128], BF16)
                abuf = sb(p3a, "abuf", [128, 128], BF16)
                pA3 = pst(p3a, "pA3", [128, 512])
                pB3 = pst(p3a, "pB3", [128, 512])
                pC3 = pst(p3a, "pC3", [128, 512])
                pD3 = pst(p3a, "pD3", [128, 512])
                pE3 = pst(p3a, "pE3", [128, 512])
                pT3 = pst(p3a, "pT3", [128, 1024], BF16)

                def build_hblk(bk, dst):
                    for tt in range(4):
                        t = bk * 4 + tt
                        def fh(v, t=t):
                            v.tensor_scalar(ltile[:, 0:D], res[:, t * D:(t + 1) * D],
                                            stats[:, ST_MU + t:ST_MU + t + 1],
                                            stats[:, RRC + t:RRC + t + 1],
                                            ALU.subtract, ALU.mult)
                            return v.tensor_tensor(stg[:, 0:D], ltile[:, 0:D],
                                                   ln1r[:, :], ALU.mult)
                        step("V", fh, deps=["S", "Te"])
                        def fht(te, tt=tt):
                            for i in range(DCH):
                                r = te.transpose(pT3[:, i * 128:(i + 1) * 128],
                                                 stg[:, i * 128:(i + 1) * 128],
                                                 ident[:, :])
                            return r
                        step("Te", fht, deps=["V"])
                        def fhe(v, tt=tt, dst=dst):
                            for i in range(DCH):
                                r = v.tensor_copy(
                                    dst[:, i * 512 + tt * 128:i * 512 + (tt + 1) * 128],
                                    pT3[:, i * 128:(i + 1) * 128])
                            return r
                        step("V", fhe, deps=["Te"])

                for bk in range(4):
                    tok0 = bk * 512
                    build_hblk(bk, hblk)
                    def fz(te, tok0=tok0):
                        for ci in range(DCH):
                            te.matmul(pA3[0:H, 0:512],
                                      bwb[:, ci * H:(ci + 1) * H],
                                      hblk[:, ci * 512:(ci + 1) * 512],
                                      start=(ci == 0), stop=(ci == DCH - 1))
                        for ci in range(DCH):
                            r = te.matmul(pB3[0:H, 0:512],
                                          awb[:, ci * H:(ci + 1) * H],
                                          hblk[:, ci * 512:(ci + 1) * 512],
                                          start=(ci == 0), stop=(ci == DCH - 1))
                        return r
                    step("Te", fz, deps=["V"])
                    def fga0(v):
                        v.tensor_scalar(ltile[0:H, 0:512], pB3[0:H, 0:512],
                                        cvec[0:H, 1:2], None, ALU.add)
                        return v.tensor_scalar(ltile[0:H, 512:1024], pA3[0:H, 0:512],
                                               -1.0, None, ALU.mult)
                    step("V", fga0, deps=["Te"])
                    def fga1(s):
                        return s.activation(dump[0:H, 0:1024], ltile[0:H, 0:1024],
                                            AF.Abs)
                    step("S", fga1, deps=["V"])
                    def fga2(s):
                        return s.activation(ltile2[0:H, 0:1024], dump[0:H, 0:1024],
                                            AF.Exp, scale=-1.0)
                    step("S", fga2, deps=["S"])
                    def fga3(v):
                        return v.tensor_scalar(ltile2[0:H, 0:1024], ltile2[0:H, 0:1024],
                                               1.0, None, ALU.add)
                    step("V", fga3, deps=["S"])
                    def fga4(s):
                        return s.activation(dump[0:H, 0:1024], ltile2[0:H, 0:1024],
                                            AF.Ln)
                    step("S", fga4, deps=["V"])
                    def fga5(v):
                        v.tensor_scalar(ltile2[0:H, 0:1024], ltile[0:H, 0:1024],
                                        0.0, None, ALU.max)
                        return v.tensor_tensor(dump[0:H, 0:1024], ltile2[0:H, 0:1024],
                                               dump[0:H, 0:1024], ALU.add)
                    step("V", fga5, deps=["S"])
                    if bk == 0:
                        step("V", lambda v: v.memset(zrone[:, :], 1.0), deps=["S"])
                    def fgv2(v, tok0=tok0):
                        v.tensor_scalar(ltile[0:H, 0:512], dump[0:H, 0:512],
                                        cvec[0:H, 0:1], None, ALU.mult)
                        for cc2 in range(4):
                            v.tensor_tensor_scan(
                                zr[0:H, tok0 + cc2 * 128:tok0 + (cc2 + 1) * 128],
                                zrone[:, :],
                                ltile[0:H, cc2 * 128:(cc2 + 1) * 128],
                                0.0, ALU.mult, ALU.add)
                        v.tensor_tensor(ltile[0:H, 0:512], dump[0:H, 512:1024],
                                        zr[0:H, tok0:tok0 + 512], ALU.add)
                        return v.tensor_scalar(zr2[0:H, tok0:tok0 + 512],
                                               ltile[0:H, 0:512], -1.0, None, ALU.mult)
                    step("V", fgv2, deps=["S"])
                    for tt in range(4):
                        t = bk * 4 + tt
                        def fac(te, t=t, tok0=tok0, tt=tt):
                            te.transpose(pC3[:, 0:8],
                                         zr[0:8, tok0 + tt * 128:tok0 + (tt + 1) * 128],
                                         identf[0:8, 0:8])
                            return te.transpose(pC3[:, 8:16],
                                                zr2[0:8, tok0 + tt * 128:tok0 + (tt + 1) * 128],
                                                identf[0:8, 0:8])
                        step("Te", fac, deps=["V"])
                        def face(v, t=t):
                            v.tensor_copy(acol[:, t * H:(t + 1) * H], pC3[:, 0:8])
                            return v.tensor_copy(lbacol[:, t * H:(t + 1) * H], pC3[:, 8:16])
                        step("V", face, deps=["Te"])
                    for wv_i, (wap_, kind) in enumerate([(qw, "q"), (kw, "k"), (vw, "v")]):
                        load_weight_bf16(wap_, wst3, dep=("Te", "V"))
                        for tt in range(4):
                            t = bk * 4 + tt
                            for nn2 in range(2):
                                def fpm(te, tt=tt, nn2=nn2):
                                    for ci in range(DCH):
                                        r = te.matmul(
                                            pA3[:, 0:512],
                                            hblk[:, ci * 512 + tt * 128:ci * 512 + (tt + 1) * 128],
                                            wst3[:, ci * D + nn2 * 512:ci * D + (nn2 + 1) * 512],
                                            start=(ci == 0), stop=(ci == DCH - 1))
                                    return r
                                step("Te", fpm, deps=["G", "V", "S"])
                                def fpe(s, tt=tt, nn2=nn2, kind=kind):
                                    dst = {"q": qtm, "k": ktm, "v": vtm}[kind]
                                    return s.activation(
                                        dump[:, nn2 * 512:(nn2 + 1) * 512]
                                        if kind != "v" else
                                        dst[:, tt * D + nn2 * 512:tt * D + (nn2 + 1) * 512],
                                        pA3[:, 0:512], AF.Silu)
                                step("S", fpe, deps=["Te"])
                            if kind != "v":
                                def fl2(v, tt=tt, kind=kind):
                                    v.tensor_tensor(ltile[:, 0:D], dump[:, 0:D],
                                                    dump[:, 0:D], ALU.mult)
                                    for hh in range(H):
                                        v.reduce_sum(stats[:, ST_A + hh:ST_A + hh + 1],
                                                     ltile[:, hh * 128:(hh + 1) * 128],
                                                     AX.X)
                                    return v.tensor_scalar(stats[:, ST_A:ST_A + H],
                                                           stats[:, ST_A:ST_A + H],
                                                           1e-6, None, ALU.add)
                                step("V", fl2, deps=["S"])
                                def fl2b(s):
                                    return s.activation(stats[:, ST_B:ST_B + H],
                                                        stats[:, ST_A:ST_A + H], AF.Ln)
                                step("S", fl2b, deps=["V"])
                                def fl2bv(v):
                                    return v.tensor_scalar(stats[:, ST_B + 8:ST_B + 8 + H],
                                                           stats[:, ST_B:ST_B + H],
                                                           -0.5, None, ALU.mult)
                                step("V", fl2bv, deps=["S"])
                                def fl2bb(s):
                                    return s.activation(stats[:, ST_A + 8:ST_A + 8 + H],
                                                        stats[:, ST_B + 8:ST_B + 8 + H],
                                                        AF.Exp)
                                step("S", fl2bb, deps=["V"])
                                def fl2c(v, tt=tt, kind=kind):
                                    dst = qtm if kind == "q" else ktm
                                    for hh in range(H):
                                        r = v.tensor_scalar(
                                            dst[:, tt * D + hh * 128:tt * D + (hh + 1) * 128],
                                            dump[:, hh * 128:(hh + 1) * 128],
                                            stats[:, ST_A + 8 + hh:ST_A + 9 + hh], None,
                                            ALU.mult)
                                    return r
                                step("V", fl2c, deps=["S"])
                        if kind == "v":
                            continue
                        dstf = qfb if kind == "q" else kfb
                        srct = qtm if kind == "q" else ktm
                        for tt in range(4):
                            def ftf(te, tt=tt, srct=srct):
                                for i in range(DCH):
                                    r = te.transpose(pT3[:, i * 128:(i + 1) * 128],
                                                     srct[:, tt * D + i * 128:tt * D + (i + 1) * 128],
                                                     ident[:, :])
                                return r
                            step("Te", ftf, deps=["V", "S"])
                            def ftfe(v, tt=tt, dstf=dstf):
                                for i in range(DCH):
                                    r = v.tensor_copy(
                                        dstf[:, i * 512 + tt * 128:i * 512 + (tt + 1) * 128],
                                        pT3[:, i * 128:(i + 1) * 128])
                                return r
                            step("V", ftfe, deps=["Te", "L"])
                    def fqd(sy, tok0=tok0):
                        return sy.dma_start(
                            out=qf_d[:, tok0:tok0 + 512]
                                .rearrange("(c p) n -> c p n", p=128).transpose([1, 0, 2]),
                            in_=qfb[:, :].rearrange("p (c n) -> p c n", c=DCH))
                    step("L", fqd, deps=["V"], dma=True)

                    def fexp(s, bk=bk):
                        c0 = bk * 4 * H
                        return s.activation(lam[:, c0:c0 + 4 * H], acol[:, c0:c0 + 4 * H],
                                            AF.Exp)
                    step("S", fexp, deps=["V"])
                    def fac2(te, bk=bk, tok0=tok0):
                        for hh in range(H):
                            r = te.matmul(pC3[:, hh * 4:(hh + 1) * 4],
                                          sel8[:, hh * 128:hh * 128 + 128],
                                          zr[0:8, tok0 + 127:tok0 + 512:128],
                                          start=True, stop=True)
                        return r
                    step("Te", fac2, deps=["S", "V"])
                    def fac3(v, bk=bk):
                        c0 = bk * 4 * H
                        for hh in range(H):
                            for ct2 in range(4):
                                v.tensor_copy(ltile[:, ct2 * H + hh:ct2 * H + hh + 1],
                                              pC3[:, hh * 4 + ct2:hh * 4 + ct2 + 1])
                        return v.tensor_tensor(ltile[:, 32:64], lbacol[:, c0:c0 + 32],
                                               ltile[:, 0:32], ALU.add)
                    step("V", fac3, deps=["Te"])
                    def fac4(s, bk=bk):
                        c0 = bk * 4 * H
                        s.activation(kcoef[:, c0:c0 + 32], ltile[:, 32:64], AF.Exp)
                        return s.activation(lamC[:, c0:c0 + 32], ltile[:, 0:32], AF.Exp)
                    step("S", fac4, deps=["V"])

                    for ct in range(4):
                        c_abs = bk * 4 + ct
                        c0f = ct * 128
                        tok0c = bk * 512 + ct * 128
                        for hh in range(H):
                            ccol = c_abs * H + hh
                            def fx1(te, tok0c=tok0c, hh=hh):
                                te.matmul(pA3[:, 0:128],
                                          sel8[:, hh * 128:(hh + 1) * 128],
                                          zr[0:8, tok0c:tok0c + 128],
                                          start=True, stop=True)
                                return te.matmul(pB3[:, 0:128],
                                                 sel8[:, hh * 128:(hh + 1) * 128],
                                                 zr2[0:8, tok0c:tok0c + 128],
                                                 start=True, stop=True)
                            step("Te", fx1, deps=["V", "S"])
                            def fx2(v, ccol=ccol):
                                v.tensor_scalar(ltile[:, 0:128], pA3[:, 0:128],
                                                lbacol[:, ccol:ccol + 1], 0.0,
                                                ALU.add, ALU.min)
                                return v.tensor_scalar(ltile[:, 128:256], pB3[:, 0:128],
                                                       acol[:, ccol:ccol + 1], 0.0,
                                                       ALU.add, ALU.min)
                            step("V", fx2, deps=["Te"])
                            def fx3(s):
                                s.activation(dT[:, :], ltile[:, 0:128], AF.Exp)
                                return s.activation(dU[:, :], ltile[:, 128:256], AF.Exp)
                            step("S", fx3, deps=["V"])
                            def fkk(te, c0f=c0f, hh=hh):
                                te.matmul(pA3[:, 0:128],
                                          kfb[:, hh * 512 + c0f:hh * 512 + c0f + 128],
                                          kfb[:, hh * 512 + c0f:hh * 512 + c0f + 128],
                                          start=True, stop=True)
                                return te.matmul(pB3[:, 0:128],
                                                 kfb[:, hh * 512 + c0f:hh * 512 + c0f + 128],
                                                 qfb[:, hh * 512 + c0f:hh * 512 + c0f + 128],
                                                 start=True, stop=True)
                            step("Te", fkk, deps=["S", "V"])
                            def fnt(v):
                                v.tensor_tensor(tpow[:, 0:128], dT[:, :], pA3[:, 0:128],
                                                ALU.mult)
                                v.tensor_tensor(tpow[:, 0:128], tpow[:, 0:128],
                                                musU[:, :], ALU.mult)
                                v.tensor_tensor(nbuf[:, 0:128], dU[:, :], pA3[:, 0:128],
                                                ALU.mult)
                                v.tensor_tensor(nbuf[:, 0:128], nbuf[:, 0:128],
                                                musL[:, :], ALU.mult)
                                v.tensor_tensor(pmt[:, :], dT[:, :], pB3[:, 0:128],
                                                ALU.mult)
                                return v.tensor_tensor(pmt[:, :], pmt[:, :],
                                                       muiU[:, :], ALU.mult)
                            step("V", fnt, deps=["Te", "S"])
                            def fy(v, ct=ct, hh=hh, ccol=ccol):
                                v.tensor_copy(ybuf[:, 0:128],
                                              vtm[:, ct * D + hh * 128:ct * D + (hh + 1) * 128])
                                v.tensor_scalar(ybuf[:, 128:256],
                                                ktm[:, ct * D + hh * 128:ct * D + (hh + 1) * 128],
                                                lam[:, ccol:ccol + 1], None, ALU.mult)
                                return v.tensor_scalar(kp[:, :],
                                                       ktm[:, ct * D + hh * 128:ct * D + (hh + 1) * 128],
                                                       kcoef[:, ccol:ccol + 1], None,
                                                       ALU.mult)
                            step("V", fy, deps=["S"])
                            for lvl in range(1, 7):
                                cur = (lvl - 1) % 2
                                nxt = lvl % 2
                                def fsq(te, lvl=lvl, cur=cur):
                                    r = te.matmul(pC3[:, 0:128],
                                                  nbuf[:, cur * 128:(cur + 1) * 128],
                                                  tpow[:, (lvl - 1) * 128:lvl * 128],
                                                  start=True, stop=True)
                                    if lvl < 6:
                                        r = te.matmul(pD3[:, 0:128],
                                                      tpow[:, (lvl - 1) * 128:lvl * 128],
                                                      nbuf[:, cur * 128:(cur + 1) * 128],
                                                      start=True, stop=True)
                                    return r
                                step("Te", fsq, deps=["V"])
                                def fse(v, lvl=lvl, nxt=nxt):
                                    r = v.tensor_copy(tpow[:, lvl * 128:(lvl + 1) * 128],
                                                      pC3[:, 0:128])
                                    if lvl < 6:
                                        r = v.tensor_copy(nbuf[:, nxt * 128:(nxt + 1) * 128],
                                                          pD3[:, 0:128])
                                    return r
                                step("V", fse, deps=["Te"])
                            for lvl in range(6, -1, -1):
                                def fap2(te, lvl=lvl):
                                    return te.matmul(pC3[:, 0:256],
                                                     tpow[:, lvl * 128:(lvl + 1) * 128],
                                                     ybuf[:, 0:256], start=True, stop=True)
                                step("Te", fap2, deps=["V"])
                                def fae(v, lvl=lvl):
                                    return v.tensor_tensor(ybuf[:, 0:256], ybuf[:, 0:256],
                                                           pC3[:, 0:256],
                                                           ALU.add if lvl > 0 else ALU.subtract)
                                step("V", fae, deps=["Te"])
                            def fw2(te):
                                te.matmul(pA3[:, 0:128], ybuf[:, 128:256], kp[:, :],
                                          start=True, stop=True)
                                te.matmul(pB3[:, 0:128], ybuf[:, 128:256], pmt[:, :],
                                          start=True, stop=True)
                                te.matmul(pC3[:, 0:128], kp[:, :], ybuf[:, 0:128],
                                          start=True, stop=True)
                                return te.matmul(pD3[:, 0:128], pmt[:, :], ybuf[:, 0:128],
                                                 start=True, stop=True)
                            step("Te", fw2, deps=["V"])
                            def fw3(v, ccol=ccol):
                                v.tensor_scalar(stg[:, 0:128], ident[:, :],
                                                lamC[:, ccol:ccol + 1], None, ALU.mult)
                                v.tensor_scalar(abuf[:, :], pA3[:, 0:128],
                                                -1.0, None, ALU.mult)
                                v.tensor_tensor(abuf[:, :], abuf[:, :], stg[:, 0:128],
                                                ALU.add)
                                v.tensor_scalar(stg2[:, 0:128], pB3[:, 0:128],
                                                -1.0, None, ALU.mult)
                                v.tensor_copy(ltile[:, 0:128], pC3[:, 0:128])
                                return v.tensor_copy(ltile[:, 128:256], pD3[:, 0:128])
                            step("V", fw3, deps=["Te", "G"])
                            def fw4(g, ccol=ccol):
                                r = []
                                r0 = ccol * 128
                                r.append(g.dma_start(out=aT_d[r0:r0 + 128, :],
                                                     in_=abuf[:, :]))
                                r.append(g.dma_start(out=m2_d[r0:r0 + 128, :],
                                                     in_=stg2[:, 0:128]))
                                r.append(g.dma_start(out=bC_d[r0:r0 + 128, :],
                                                     in_=ltile[:, 0:128]))
                                r.append(g.dma_start(out=oT_d[r0:r0 + 128, :],
                                                     in_=ltile[:, 128:256]))
                                return r
                            step("G", fw4, deps=["V"], dma=True)
                            step("V", lambda v: None, deps=["G"])
            barrier()

            p3c = contextlib.ExitStack()
            with p3c:
                Sst = sb(p3c, "Sst", [128, H * 128], BF16)
                ach = sb(p3c, "ach", [128, H * 128], BF16)
                bch = sb(p3c, "bch", [128, H * 128])
                pA4 = pst(p3c, "pA4", [128, 512])
                step("V", lambda v: v.memset(Sst[:, :], 0.0), deps=["G", "S", "Te"])
                for c_abs in range(NCH):
                    def fca(sy, c_abs=c_abs):
                        r0 = c_abs * H * 128
                        r1 = sy.dma_start(out=ach[:, :].rearrange("p (h n) -> p h n", h=H),
                                          in_=aT_d[r0:r0 + H * 128, :]
                                          .rearrange("(h p) n -> h p n", p=128).transpose([1, 0, 2]))
                        r2 = sy.dma_start(out=bch[:, :].rearrange("p (h n) -> p h n", h=H),
                                          in_=bC_d[r0:r0 + H * 128, :]
                                          .rearrange("(h p) n -> h p n", p=128).transpose([1, 0, 2]))
                        return [r1, r2]
                    step("L", fca, deps=["V"], dma=True)
                    for hh in range(H):
                        def fcm(te, hh=hh):
                            return te.matmul(pA4[:, 0:128],
                                             ach[:, hh * 128:(hh + 1) * 128],
                                             Sst[:, hh * 128:(hh + 1) * 128],
                                             start=True, stop=True)
                        step("Te", fcm, deps=["L", "V"])
                        def fce(v, hh=hh):
                            return v.tensor_tensor(Sst[:, hh * 128:(hh + 1) * 128],
                                                   pA4[:, 0:128],
                                                   bch[:, hh * 128:(hh + 1) * 128],
                                                   ALU.add)
                        step("V", fce, deps=["Te"])
                def fsf(v):
                    return v.tensor_copy(ltile[:, 0:1024], Sst[:, 0:1024])
                step("V", fsf)
                def fsg(g):
                    return g.dma_start(out=cc_in.ap().rearrange("(h p) n -> h p n", p=128).transpose([1, 0, 2]),
                                       in_=ltile[:, 0:1024]
                                       .rearrange("p (h n) -> p h n", h=H))
                step("G", fsg, deps=["V"], dma=True)
                def fccw(g):
                    r = g.collective_compute(
                        "AllGather", ALU.bypass,
                        replica_groups=[list(range(8))],
                        ins=[cc_in.ap().opt()],
                        outs=[cc_out.ap().opt()])
                    r.then_inc(sems["Cc"][0], 1)
                    sems["Cc"][1] += 1
                    return None
                step("G", fccw, deps=["G"])
                IndirectOffsetOnAxis = bass.IndirectOffsetOnAxis
                def frd(g):
                    r = []
                    for hh in range(H):
                        r.append(g.indirect_dma_start(
                            ltile[:, hh * 128:(hh + 1) * 128], None,
                            cc_out.ap(),
                            IndirectOffsetOnAxis(goff[:, hh:hh + 1], 0)))
                    return r
                step("G", frd, deps=["Cc"], dma=True)
                def fmk(v):
                    return v.tensor_scalar(Sst[:, 0:1024], ltile[:, 0:1024],
                                           mcS[:, 0:1], None, ALU.mult)
                step("V", fmk, deps=["G"])

                p3d = contextlib.ExitStack()
                with p3d:
                    gwb = sb(p3d, "gwb", [128, DCH * D], BF16)
                    owb = sb(p3d, "owb", [128, DCH * D], BF16)
                    och = sb(p3d, "och", [128, H * 128])
                    mch = sb(p3d, "mch", [128, H * 128], BF16)
                    qch = sb(p3d, "qch", [128, H * 128], BF16)
                    hfc = sb(p3d, "hfc", [128, DCH * 128], BF16)
                    gatec = sb(p3d, "gatec", [128, D], BF16)
                    ofc = sb(p3d, "ofc", [128, H * 128], BF16)
                    onr = sb(p3d, "onr", [128, D])
                    pA5 = pst(p3d, "pA5", [128, 512])
                    pB5 = pst(p3d, "pB5", [128, 512])
                    pC5 = pst(p3d, "pC5", [128, 512])
                    pD5 = pst(p3d, "pD5", [128, 512])
                    pT5 = pst(p3d, "pT5", [128, 1024], BF16)
                    load_weight_bf16(gw, gwb, dep=("V",))
                    load_weight_bf16(oww, owb, dep=("V",))
                    step("L", lambda sy: sy.dma_start(out=onr[:, :], in_=onormrow[:, :]),
                         dma=True)
                    barrier()
                    for c_abs in range(NCH):
                        tok0c = c_abs * 128
                        def fld2(sy, c_abs=c_abs):
                            r0 = c_abs * H * 128
                            r = []
                            r.append(sy.dma_start(
                                out=ach[:, :].rearrange("p (h n) -> p h n", h=H),
                                in_=aT_d[r0:r0 + H * 128, :]
                                .rearrange("(h p) n -> h p n", p=128).transpose([1, 0, 2])))
                            r.append(sy.dma_start(
                                out=bch[:, :].rearrange("p (h n) -> p h n", h=H),
                                in_=bC_d[r0:r0 + H * 128, :]
                                .rearrange("(h p) n -> h p n", p=128).transpose([1, 0, 2])))
                            r.append(sy.dma_start(
                                out=och[:, :].rearrange("p (h n) -> p h n", h=H),
                                in_=oT_d[r0:r0 + H * 128, :]
                                .rearrange("(h p) n -> h p n", p=128).transpose([1, 0, 2])))
                            r.append(sy.dma_start(
                                out=mch[:, :].rearrange("p (h n) -> p h n", h=H),
                                in_=m2_d[r0:r0 + H * 128, :]
                                .rearrange("(h p) n -> h p n", p=128).transpose([1, 0, 2])))
                            r.append(sy.dma_start(
                                out=qch[:, :].rearrange("p (h n) -> p h n", h=H),
                                in_=qf_d[:, tok0c:tok0c + 128]
                                .rearrange("(h p) n -> h p n", p=128).transpose([1, 0, 2])))
                            return r
                        step("L", fld2, deps=["V", "Te"], dma=True)
                        def fh2(v, c_abs=c_abs):
                            v.tensor_scalar(ltile[:, 0:D],
                                            res[:, c_abs * D:(c_abs + 1) * D],
                                            stats[:, ST_MU + c_abs:ST_MU + c_abs + 1],
                                            stats[:, RRC + c_abs:RRC + c_abs + 1],
                                            ALU.subtract, ALU.mult)
                            return v.tensor_tensor(stg[:, 0:D], ltile[:, 0:D],
                                                   ln1r[:, :], ALU.mult)
                        step("V", fh2, deps=["S", "Te"])
                        def fh3(te):
                            for i in range(DCH):
                                r = te.transpose(pT5[:, i * 128:(i + 1) * 128],
                                                 stg[:, i * 128:(i + 1) * 128],
                                                 ident[:, :])
                            return r
                        step("Te", fh3, deps=["V"])
                        def fh4(v):
                            for i in range(DCH):
                                r = v.tensor_copy(hfc[:, i * 128:(i + 1) * 128],
                                                  pT5[:, i * 128:(i + 1) * 128])
                            return r
                        step("V", fh4, deps=["Te"])
                        def fg5(te):
                            for nn2 in range(2):
                                for ci in range(DCH):
                                    r = te.matmul(
                                        [pA5, pB5][nn2][:, 0:512],
                                        hfc[:, ci * 128:(ci + 1) * 128],
                                        gwb[:, ci * D + nn2 * 512:ci * D + (nn2 + 1) * 512],
                                        start=(ci == 0), stop=(ci == DCH - 1))
                            return r
                        step("Te", fg5, deps=["V", "L"])
                        def fg6(s):
                            s.activation(gatec[:, 0:512], pA5[:, 0:512], AF.Silu)
                            return s.activation(gatec[:, 512:1024], pB5[:, 0:512],
                                                AF.Silu)
                        step("S", fg6, deps=["Te"])
                        for hh in range(H):
                            ccol = c_abs * H + hh
                            def fo1(te, hh=hh):
                                te.matmul(pA5[:, 0:128], qch[:, hh * 128:(hh + 1) * 128],
                                          Sst[:, hh * 128:(hh + 1) * 128],
                                          start=True, stop=True)
                                te.matmul(pB5[:, 0:128], mch[:, hh * 128:(hh + 1) * 128],
                                          Sst[:, hh * 128:(hh + 1) * 128],
                                          start=True, stop=True)
                                return te.matmul(pC5[:, 0:128],
                                                 ach[:, hh * 128:(hh + 1) * 128],
                                                 Sst[:, hh * 128:(hh + 1) * 128],
                                                 start=True, stop=True)
                            step("Te", fo1, deps=["S", "V", "L"])
                            def fo2(v, hh=hh, ccol=ccol):
                                v.tensor_scalar(ltile[:, 0:128], pA5[:, 0:128],
                                                lam[:, ccol:ccol + 1], None, ALU.mult)
                                v.tensor_tensor(ltile[:, 128:256], ltile[:, 0:128],
                                                pB5[:, 0:128], ALU.add)
                                v.tensor_tensor(ltile[:, 0:128], ltile[:, 128:256],
                                                och[:, hh * 128:(hh + 1) * 128], ALU.add)
                                return v.tensor_tensor(Sst[:, hh * 128:(hh + 1) * 128],
                                                       pC5[:, 0:128],
                                                       bch[:, hh * 128:(hh + 1) * 128],
                                                       ALU.add)
                            step("V", fo2, deps=["Te"])
                            def fo3(s):
                                return s.activation(dump[:, 0:128], ltile[:, 0:128],
                                                    AF.Square,
                                                    accum_out=stats[:, ST_A:ST_A + 1])
                            step("S", fo3, deps=["V"])
                            def fo4(v):
                                return v.tensor_scalar(stats[:, ST_A + 2:ST_A + 3],
                                                       stats[:, ST_A:ST_A + 1],
                                                       1.0 / 128.0, 1e-6,
                                                       ALU.mult, ALU.add)
                            step("V", fo4, deps=["S"])
                            def fo5(s):
                                return s.activation(stats[:, ST_B:ST_B + 1],
                                                    stats[:, ST_A + 2:ST_A + 3], AF.Ln)
                            step("S", fo5, deps=["V"])
                            def fo5v(v):
                                return v.tensor_scalar(stats[:, ST_B + 1:ST_B + 2],
                                                       stats[:, ST_B:ST_B + 1],
                                                       -0.5, None, ALU.mult)
                            step("V", fo5v, deps=["S"])
                            def fo5b(s):
                                return s.activation(stats[:, ST_A + 1:ST_A + 2],
                                                    stats[:, ST_B + 1:ST_B + 2], AF.Exp)
                            step("S", fo5b, deps=["V"])
                            def fo6(v, hh=hh):
                                v.tensor_scalar(ltile[:, 128:256], ltile[:, 0:128],
                                                stats[:, ST_A + 1:ST_A + 2], None,
                                                ALU.mult)
                                v.tensor_tensor(ltile[:, 128:256], ltile[:, 128:256],
                                                onr[:, hh * 128:(hh + 1) * 128], ALU.mult)
                                return v.tensor_tensor(stg2[:, 0:128], ltile[:, 128:256],
                                                       gatec[:, hh * 128:(hh + 1) * 128],
                                                       ALU.mult)
                            step("V", fo6, deps=["S"])
                            def fo7(te):
                                return te.transpose(pT5[:, 0:128], stg2[:, 0:128],
                                                    ident[:, :])
                            step("Te", fo7, deps=["V"])
                            def fo8(v, hh=hh):
                                return v.tensor_copy(ofc[:, hh * 128:(hh + 1) * 128],
                                                     pT5[:, 0:128])
                            step("V", fo8, deps=["Te"])
                        def fow(te):
                            for nn2 in range(2):
                                for ci in range(DCH):
                                    r = te.matmul(
                                        [pA5, pB5][nn2][:, 0:512],
                                        ofc[:, ci * 128:(ci + 1) * 128],
                                        owb[:, ci * D + nn2 * 512:ci * D + (nn2 + 1) * 512],
                                        start=(ci == 0), stop=(ci == DCH - 1))
                            return r
                        step("Te", fow, deps=["V"])
                        def fx1(v, c_abs=c_abs):
                            v.tensor_scalar(ltile[:, 0:D],
                                            res[:, c_abs * D:(c_abs + 1) * D],
                                            stats[:, ST_MU + c_abs:ST_MU + c_abs + 1],
                                            stats[:, ST_RS + c_abs:ST_RS + c_abs + 1],
                                            ALU.subtract, ALU.mult)
                            v.scalar_tensor_tensor(ltile2[:, 0:512], pA5[:, 0:512],
                                                   0.5, ltile[:, 0:512],
                                                   ALU.mult, ALU.add)
                            return v.scalar_tensor_tensor(ltile2[:, 512:1024],
                                                          pB5[:, 0:512],
                                                          0.5, ltile[:, 512:1024],
                                                          ALU.mult, ALU.add)
                        step("V", fx1, deps=["Te"])
                        def fx2(sy, c_abs=c_abs):
                            return sy.dma_start(
                                out=x1_d[c_abs * 128:(c_abs + 1) * 128, :],
                                in_=ltile2[:, 0:D])
                        step("L", fx2, deps=["V"], dma=True)
                        step("V", lambda v: None, deps=["L"])
            barrier()

            p3f = contextlib.ExitStack()
            with p3f:
                h2fm = sb(p3f, "h2fm", [128, DCH * T], BF16)
                sil2 = sb(p3f, "sil2", [128, 2 * 1024], BF16)
                acc = sb(p3f, "acc", [128, 8 * 1024])
                fstr = sb(p3f, "fstr", [128, 2 * DCH * 128], BF16)
                pwst = sb(p3f, "pwst", [128, 2 * D], BF16)
                pwsf = sb(p3f, "pwsf", [128, 2 * D])
                ln2r = sb(p3f, "ln2r", [128, D])
                x1t = ltile2
                pp6 = [pst(p3f, "pp6%d" % i, [128, 512]) for i in range(6)]
                p3fb = contextlib.ExitStack()
                pT6 = pst(p3fb, "pT6", [128, 1024], BF16)
                step("L", lambda sy: sy.dma_start(out=ln2r[:, :], in_=ln2row[:, :]),
                     dma=True)
                barrier()
                for t in range(NTI):
                    def fx3(sy, t=t):
                        return sy.dma_start(out=x1t[:, :],
                                            in_=x1_d[t * 128:(t + 1) * 128, :])
                    step("L", fx3, deps=["V", "S"], dma=True)
                    def fr1(s, t=t):
                        return s.activation(dump[:, 0:D], x1t[:, :], AF.Square,
                                            accum_out=stats[:, ST_A:ST_A + 1])
                    step("S", fr1, deps=["L"])
                    def fr2(v):
                        return v.tensor_scalar(stats[:, ST_A + 2:ST_A + 3],
                                               stats[:, ST_A:ST_A + 1],
                                               1.0 / D, 1e-6, ALU.mult, ALU.add)
                    step("V", fr2, deps=["S"])
                    def fr3(s):
                        return s.activation(stats[:, ST_B:ST_B + 1],
                                            stats[:, ST_A + 2:ST_A + 3], AF.Ln)
                    step("S", fr3, deps=["V"])
                    def fr3v(v):
                        return v.tensor_scalar(stats[:, ST_B + 1:ST_B + 2],
                                               stats[:, ST_B:ST_B + 1], -0.5, None,
                                               ALU.mult)
                    step("V", fr3v, deps=["S"])
                    def fr3b(s):
                        return s.activation(stats[:, ST_A + 1:ST_A + 2],
                                            stats[:, ST_B + 1:ST_B + 2], AF.Exp)
                    step("S", fr3b, deps=["V"])
                    def fr4(v):
                        v.tensor_scalar(ltile[:, 0:D], x1t[:, :],
                                        stats[:, ST_A + 1:ST_A + 2], None, ALU.mult)
                        return v.tensor_tensor(stg[:, 0:D], ltile[:, 0:D], ln2r[:, :],
                                               ALU.mult)
                    step("V", fr4, deps=["S"])
                    def fr5(te):
                        for i in range(DCH):
                            r = te.transpose(pT6[:, i * 128:(i + 1) * 128],
                                             stg[:, i * 128:(i + 1) * 128], ident[:, :])
                        return r
                    step("Te", fr5, deps=["V"])
                    def fr6(v, t=t):
                        for i in range(DCH):
                            r = v.tensor_copy(h2fm[:, i * T + t * 128:i * T + (t + 1) * 128],
                                              pT6[:, i * 128:(i + 1) * 128])
                        return r
                    step("V", fr6, deps=["Te"])
                barrier()
                p3fb.close()
                pp7 = [pst(p3f, "pp7%d" % i, [128, 512]) for i in range(2)]
                fcr_p = fcw.rearrange("(c p) o -> c p o", p=128).transpose([1, 0, 2])
                pwr = pww.rearrange("(c p) d -> c p d", p=128)
                use_fc = [None, None]
                sil_tok = [None, None]
                use_pw = [None, None]
                cst_tok = [None, None]
                ac_ev = [None, None]
                fy_tok = None
                rs3_tok = None
                fcps = [[pp6[4], pp6[5]], [pp7[0], pp7[1]]]
                for tbp in range(2):
                    az = step("V", lambda v: v.memset(acc[:, :], 0.0))
                    for icc in range(ICH):
                        b = icc % 2
                        def ffl(g, icc=icc, b=b):
                            return g.dma_start(
                                out=fstr[:, b * DCH * 128:(b + 1) * DCH * 128]
                                    .rearrange("p (c n) -> p c n", c=DCH),
                                in_=fcr_p[:, :, icc * 128:(icc + 1) * 128])
                        lt = step("G", ffl, deps=[use_fc[b]], dma=True,
                                  sem="fc%d" % b)
                        def fpl2(sy, icc=icc, b=b):
                            return sy.dma_start(
                                out=pwsf[:, b * D:(b + 1) * D], in_=pwr[icc])
                        lt2 = step("L", fpl2, deps=[cst_tok[b]], dma=True,
                                   sem="pw%d" % b)
                        def fpc(v, b=b):
                            return v.tensor_copy(pwst[:, b * D:(b + 1) * D],
                                                 pwsf[:, b * D:(b + 1) * D])
                        ct = step("V", fpc, deps=[lt2, use_pw[b]])
                        cst_tok[b] = ct
                        def ffm(te, icc=icc, b=b, tbp=tbp):
                            for sb2 in range(2):
                                tok0 = tbp * 1024 + sb2 * 512
                                for ci in range(DCH):
                                    r = te.matmul(
                                        fcps[b][sb2][:, 0:512],
                                        fstr[:, b * DCH * 128 + ci * 128:
                                             b * DCH * 128 + (ci + 1) * 128],
                                        h2fm[:, ci * T + tok0:ci * T + tok0 + 512],
                                        start=(ci == 0), stop=(ci == DCH - 1))
                            return r
                        mt = step("Te", ffm, deps=[lt, sil_tok[b]])
                        use_fc[b] = mt
                        def ffs(s, b=b):
                            s.activation(sil2[:, b * 1024:b * 1024 + 512],
                                         fcps[b][0][:, 0:512], AF.Silu)
                            return s.activation(sil2[:, b * 1024 + 512:(b + 1) * 1024],
                                                fcps[b][1][:, 0:512], AF.Silu)
                        sil_tok[b] = step("S", ffs, deps=[mt])
                        for tq in range(8):
                            pq = tq % 2
                            deps0 = [ct, sil_tok[b]]
                            if icc == 0 and tq == 0:
                                deps0 += [az]
                            def fpm2(te, icc=icc, b=b, tq=tq, pq=pq):
                                for nn2 in range(2):
                                    r = te.matmul(
                                        pp6[pq * 2 + nn2][:, 0:512],
                                        sil2[:, b * 1024 + tq * 128:
                                             b * 1024 + (tq + 1) * 128],
                                        pwst[:, b * D + nn2 * 512:
                                             b * D + (nn2 + 1) * 512],
                                        start=True, stop=True)
                                return r
                            mtq = step("Te", fpm2, deps=deps0 + [ac_ev[pq]])
                            use_pw[b] = mtq
                            def facc(v, tq=tq, pq=pq):
                                v.tensor_tensor(acc[:, tq * 1024:tq * 1024 + 512],
                                                acc[:, tq * 1024:tq * 1024 + 512],
                                                pp6[pq * 2][:, 0:512], ALU.add)
                                return v.tensor_tensor(
                                    acc[:, tq * 1024 + 512:(tq + 1) * 1024],
                                    acc[:, tq * 1024 + 512:(tq + 1) * 1024],
                                    pp6[pq * 2 + 1][:, 0:512], ALU.add)
                            ac_ev[pq] = step("V", facc, deps=[mtq])
                    for tq in range(8):
                        t_abs = tbp * 8 + tq
                        def fye(sy, t_abs=t_abs):
                            return sy.dma_start(
                                out=x1t[:, 0:D],
                                in_=x1_d[t_abs * 128:(t_abs + 1) * 128, :])
                        xlt = step("L", fye,
                                   deps=[fy_tok] if fy_tok else ["V", "S"],
                                   dma=True, sem="x1t")
                        def fy2(v, tq=tq):
                            v.scalar_tensor_tensor(ltile[:, 0:512],
                                                   acc[:, tq * 1024:tq * 1024 + 512],
                                                   0.5, x1t[:, 0:512],
                                                   ALU.mult, ALU.add)
                            return v.scalar_tensor_tensor(
                                ltile[:, 512:1024],
                                acc[:, tq * 1024 + 512:(tq + 1) * 1024],
                                0.5, x1t[:, 512:1024], ALU.mult, ALU.add)
                        fy_tok = step("V", fy2, deps=[xlt, rs3_tok])
                        residual_add_ln(ltile[:, 0:D], t_abs)
                        rs3_tok = ln_tile.end_stok
            barrier()
        if upto <= 3:
            fin()
            return nc

        p4 = contextlib.ExitStack()
        with p4:
            xfm = sb(p4, "xfm", [128, DCH * T], BF16)
            w1b = sb(p4, "w1b", [128, 2 * DCH * 512], BF16)
            hq = sb(p4, "hq", [128, ICH * 1024], F8E4)
            w2q = sb(p4, "w2q", [128, ICH * 1024], F8E4)
            w2s = sb(p4, "w2s", [128, 2 * 512])
            b2r = sb(p4, "b2r", [128, D])
            b1c = sb(p4, "b1c", [128, ICH])
            yq = sb(p4, "yq", [128, 512], F16)
            pA7 = pst(p4, "pA7", [128, 512])
            pB7 = pst(p4, "pB7", [128, 512])
            pT7 = pst(p4, "pT7", [128, 1024], BF16)

            def p4l(sy):
                r1 = sy.dma_start(out=b2r[:, :], in_=b2row_in[:, :])
                r2 = sy.dma_start(out=b1c[:, :], in_=b1fm_in[:, :])
                return [r1, r2]
            step("L", p4l, dma=True)
            w2r = w2T.rearrange("(c p) o -> c p o", p=128)
            w2ct = [None, None]
            for hc2 in range(2 * ICH):
                hc, hf2 = hc2 // 2, hc2 % 2
                b = hc2 % 2
                def fwl(sy, hc=hc, hf2=hf2, b=b):
                    return sy.dma_start(out=w2s[:, b * 512:(b + 1) * 512],
                                        in_=w2r[hc][:, hf2 * 512:(hf2 + 1) * 512])
                lt = step("L", fwl, deps=[w2ct[b]], dma=True, sem="w2%d" % b)
                def fwc(v, hc=hc, hf2=hf2, b=b):
                    return v.tensor_copy(
                        w2q[:, hc * 1024 + hf2 * 512:hc * 1024 + (hf2 + 1) * 512],
                        w2s[:, b * 512:(b + 1) * 512])
                w2ct[b] = step("V", fwc, deps=[lt])
            for t in range(NTI):
                ln_tile(res[:, t * D:(t + 1) * D], ST_MU + t, ST_RS + t)
                def fap4(v, t=t):
                    return v.tensor_scalar(stg[:, 0:D], res[:, t * D:(t + 1) * D],
                                           stats[:, ST_MU + t:ST_MU + t + 1],
                                           stats[:, ST_RS + t:ST_RS + t + 1],
                                           ALU.subtract, ALU.mult)
                step("V", fap4, deps=["S"])
                def ftp4(te):
                    for i in range(DCH):
                        r = te.transpose(pT7[:, i * 128:(i + 1) * 128],
                                         stg[:, i * 128:(i + 1) * 128], ident[:, :])
                    return r
                step("Te", ftp4, deps=["V"])
                def fev4(v, t=t):
                    for i in range(DCH):
                        r = v.tensor_copy(xfm[:, i * T + t * 128:i * T + (t + 1) * 128],
                                          pT7[:, i * 128:(i + 1) * 128])
                    return r
                step("V", fev4, deps=["Te"])
            barrier()
            w1r4 = w1T.rearrange("(c p) (q o) -> q c p o", p=128, o=1024)
            fe_tok = None
            rs_tok = None
            ld_tok = [None, None]
            use_tok = [None, None]
            ev_tok = [None, None]
            for tbp in range(2):
                for qtr in range(4):
                    for hf in range(2):
                        b = (qtr * 2 + hf) % 2
                        def fwl4(g, qtr=qtr, hf=hf, b=b):
                            return [g.dma_start(
                                out=w1b[:, b * DCH * 512 + ci * 512:
                                        b * DCH * 512 + (ci + 1) * 512],
                                in_=w1r4[qtr, ci][:, hf * 512:(hf + 1) * 512])
                                for ci in range(DCH)]
                        ld_tok[b] = step("G", fwl4, deps=[use_tok[b]], dma=True,
                                         sem="w1%d" % b)
                        for tbi in range(2):
                            tok0 = tbp * 1024 + tbi * 512
                            for h4 in range(4):
                                hh2 = hf * 4 + h4
                                hc = qtr * DCH + hh2
                                pb = (tbi * 4 + h4) % 2
                                def fm4(te, h4=h4, b=b, tok0=tok0, pb=pb):
                                    for ci in range(DCH):
                                        r = te.matmul(
                                            [pA7, pB7][pb][:, 0:512],
                                            w1b[:, b * DCH * 512 + ci * 512 + h4 * 128:
                                                b * DCH * 512 + ci * 512 + (h4 + 1) * 128],
                                            xfm[:, ci * T + tok0:ci * T + tok0 + 512],
                                            start=(ci == 0), stop=(ci == DCH - 1))
                                    return r
                                mt = step("Te", fm4,
                                          deps=[ld_tok[b], ev_tok[pb], fe_tok])
                                fe_tok = None
                                def fe4(s, hc=hc, tbi=tbi, pb=pb):
                                    return s.activation(
                                        hq[:, hc * 1024 + tbi * 512:
                                            hc * 1024 + (tbi + 1) * 512],
                                        [pA7, pB7][pb][:, 0:512], AF.Relu,
                                        bias=b1c[:, hc:hc + 1])
                                ev_tok[pb] = step("S", fe4, deps=[mt])
                                use_tok[b] = mt
                for tbi in range(2):
                    for ti in range(4):
                        t_abs = tbp * 8 + tbi * 4 + ti
                        mts = []
                        for nn2 in range(2):
                            def fm5(te, tbi=tbi, ti=ti, nn2=nn2):
                                pp = [pA7, pB7][nn2]
                                for hc in range(ICH):
                                    r = te.matmul(
                                        pp[:, 0:512],
                                        hq[:, hc * 1024 + tbi * 512 + ti * 128:
                                            hc * 1024 + tbi * 512 + (ti + 1) * 128],
                                        w2q[:, hc * 1024 + nn2 * 512:
                                            hc * 1024 + (nn2 + 1) * 512],
                                        start=(hc == 0), stop=(hc == ICH - 1))
                                return r
                            mts.append(step("Te", fm5,
                                            deps=[ev_tok[0], ev_tok[1], fe_tok]))
                        def fe5(v):
                            v.tensor_tensor(ltile[:, 0:512], pA7[:, 0:512],
                                            b2r[:, 0:512], ALU.add)
                            v.tensor_tensor(ltile[:, 512:1024], pB7[:, 0:512],
                                            b2r[:, 512:1024], ALU.add)
                            v.tensor_copy(yq[:, :], ltile[:, 0:512])
                            v.tensor_copy(ltile[:, 0:512], yq[:, :])
                            v.tensor_copy(yq[:, :], ltile[:, 512:1024])
                            return v.tensor_copy(ltile[:, 512:1024], yq[:, :])
                        fe_tok = step("V", fe5, deps=mts + [rs_tok])
                        residual_add_ln(ltile[:, 0:D], t_abs)
                        rs_tok = ln_tile.end_stok
            barrier()
        fin()
    return nc


_CACHE = {}


def kernel(**inputs):
    import os
    upto = int(os.environ.get("KMK_UPTO", "9"))
    dbg = os.environ.get("KMK_DBG", "")
    if _CACHE.get("key") != (upto, dbg):
        _CACHE["nc"] = build_program(upto, dbg)
        _CACHE["key"] = (upto, dbg)
    nc = _CACHE["nc"]
    f32 = np.float32
    x = np.ascontiguousarray(inputs["L0_attn_x"], dtype=f32)

    ident = np.eye(128, dtype=f32)
    causal = np.where(np.tril(np.ones((128, 128), bool)), 0.0, NEG).astype(f32)
    musU_ = np.triu(np.ones((128, 128), f32), 1)
    musL_ = np.tril(np.ones((128, 128), f32), -1)
    muiU_ = np.triu(np.ones((128, 128), f32), 0)
    pad = lambda a: np.pad(np.asarray(a, f32), (0, 128 - len(a)))[:, None]
    shared = {
        "wq0": np.ascontiguousarray(inputs["L0_attn_Wq"], f32),
        "wk0": np.ascontiguousarray(inputs["L0_attn_Wk"], f32),
        "wv0": np.ascontiguousarray(inputs["L0_attn_Wv"], f32),
        "wgT": np.ascontiguousarray(np.asarray(inputs["L0_ffn_w_gate"], f32).T),
        "wuT": np.ascontiguousarray(np.asarray(inputs["L0_ffn_w_up"], f32).T),
        "wdT": np.ascontiguousarray(np.asarray(inputs["L0_ffn_w_down"], f32).T),
        "qw": np.ascontiguousarray(inputs["L1_attn_qw"], f32),
        "kw": np.ascontiguousarray(inputs["L1_attn_kw"], f32),
        "vw": np.ascontiguousarray(inputs["L1_attn_vw"], f32),
        "gw": np.ascontiguousarray(inputs["L1_attn_gw"], f32),
        "oww": np.ascontiguousarray(inputs["L1_attn_ow"], f32),
        "bw": np.ascontiguousarray(inputs["L1_attn_bw"], f32),
        "aw": np.ascontiguousarray(inputs["L1_attn_aw"], f32),
        "fcw": np.ascontiguousarray(inputs["L1_attn_fcw"], f32),
        "pww": np.ascontiguousarray(inputs["L1_attn_pw"], f32),
        "w1T": np.ascontiguousarray(np.asarray(inputs["L1_ffn_w1"], f32).T),
        "w2T": np.ascontiguousarray(np.asarray(inputs["L1_ffn_w2"], f32).T),
        "ln1row": np.broadcast_to(np.asarray(inputs["L1_attn_ln1_w"], f32), (128, D)).copy(),
        "ln2row": np.broadcast_to(np.asarray(inputs["L1_attn_ln2_w"], f32), (128, D)).copy(),
        "onormrow": np.broadcast_to(np.tile(np.asarray(inputs["L1_attn_onorm_w"], f32), H),
                                    (128, D)).copy(),
        "b1fm_in": np.ascontiguousarray(np.asarray(inputs["L1_ffn_b1"], f32)
                                        .reshape(ICH, 128).T),
        "b2row_in": np.broadcast_to(np.asarray(inputs["L1_ffn_b2"], f32), (128, D)).copy(),
        "alog_col": pad(inputs["L1_attn_A_log"]),
        "dtb_col": pad(inputs["L1_attn_dt_bias"]),
        "identf_in": ident,
        "causal_in": causal,
        "musU_in": musU_, "musL_in": musL_, "muiU_in": muiU_,
        "sel8_in": np.concatenate(
            [np.where(np.arange(8)[:, None] == i, 1.0, 0.0)
             * np.ones((1, 128)) for i in range(8)], axis=1).astype(f32),
    }
    in_maps = []
    for c in range(8):
        s, hf = c // 2, c % 2
        m = dict(shared)
        m["x_own"] = np.ascontiguousarray(x[s, hf * T:(hf + 1) * T])
        m["x_prev"] = (np.ascontiguousarray(x[s, 0:T]) if hf == 1
                       else np.zeros((T, D), f32))
        m["maskattn"] = np.full((128, 1), 0.0 if hf == 1 else NEG, f32)
        m["maskstate"] = np.full((128, 1), 1.0 if hf == 1 else 0.0, f32)
        slot = c - 1 if hf == 1 else c
        gidx = (slot * H * HD + np.arange(H)[None, :] * HD
                + np.arange(128)[:, None]).astype(np.int32)
        m["gather_off"] = gidx
        in_maps.append(m)
    r = run_bass_kernel_spmd(nc, in_maps, core_ids=list(range(8)))
    kernel._last = r
    outa = np.empty((4, 4096, D), f32)
    for c in range(8):
        outa[c // 2, (c % 2) * T:(c % 2 + 1) * T] = r.results[c]["out"]
    return outa


# version 12
# speedup vs baseline: 1.2717x; 1.1272x over previous
import contextlib
import numpy as np

import concourse.bass as bass
import concourse.mybir as mybir
from concourse.bass_utils import run_bass_kernel_spmd

F32 = mybir.dt.float32
BF16 = mybir.dt.bfloat16
F16 = mybir.dt.float16
F8E4 = mybir.dt.float8e4
I32 = mybir.dt.int32
AF = mybir.ActivationFunctionType
ALU = mybir.AluOpType
AX = mybir.AxisListType

T = 2048
NTI = 16
D = 1024
DCH = 8
INNER = 4096
ICH = 32
H = 8
HD = 128
CK = 128
NCH = 16
MAGIC = 12582912.0
MAGIC64 = 6755399441055744.0
W_COL, M_COL = 96, 97
NEG = -1e30

ST_MU, ST_RS, ST_A, ST_B = 0, 16, 32, 48
ST_M, ST_L = 64, 65
SCL = 66
WS = 82
SWG = 84
RRC = 66



class _EngGuard:

    _IN_POS = {
        "tensor_copy": (1,), "tensor_scalar": (1, 2, 3), "tensor_tensor": (1, 2),
        "scalar_tensor_tensor": (1, 2, 3), "tensor_tensor_scan": (1, 2, 3),
        "reduce_sum": (1,), "reduce_max": (1,), "tensor_reduce": (1,),
        "reciprocal": (1,), "select": (1, 2, 3), "activation": (1,),
        "memset": (), "memzero": (),
    }

    def __init__(self, kind, sem, state):
        self._e = None
        self._kind = kind
        self._sem = sem
        self._st = state

    def _rng(self, a):
        try:
            n = a.name
            n = n() if callable(n) else n
            off = a.offset
            off = off() if callable(off) else off
            try:
                span = a.free_size
                span = span() if callable(span) else span
            except Exception:
                span = 10 ** 9
            return (str(n), int(off), int(off) + int(span))
        except Exception:
            return ("?unknown?", 0, 10 ** 9)

    @staticmethod
    def _ovl(x, y):
        return x[0] == y[0] and x[1] < y[2] and y[1] < x[2]

    def __getattr__(self, m):
        f = getattr(self._e, m)
        if m not in self._IN_POS:
            return f
        inpos = self._IN_POS[m]

        def call(*args, **kw):
            acc = []
            for i in inpos:
                if i < len(args) and hasattr(args[i], "shape"):
                    acc.append(self._rng(args[i]))
            for k in ("bias", "scale", "alpha"):
                if hasattr(kw.get(k), "shape"):
                    acc.append(self._rng(kw[k]))
            wr = []
            if args and hasattr(args[0], "shape"):
                wr.append(self._rng(args[0]))
            if hasattr(kw.get("accum_out"), "shape"):
                wr.append(self._rng(kw["accum_out"]))
            st = self._st
            hit = None
            for ent in st["hist"]:
                if (any(self._ovl(w, r) for w in ent[0] for r in acc + wr)
                        or any(self._ovl(er, w) for er in ent[1] for w in wr)):
                    hit = ent
            if hit is not None:
                if hit[4] is not None:
                    self._e.wait_ge(hit[4][0], hit[4][1])
                else:
                    if hit[3] is None:
                        st["ctr"] += 1
                        hit[2].then_inc(self._sem, 1)
                        hit[3] = st["ctr"]
                    self._e.wait_ge(self._sem, st["ctr"])
                idx = st["hist"].index(hit)
                st["hist"] = st["hist"][idx + 1:]
            r = f(*args, **kw)
            st["hist"].append([wr, acc, r, None, None])
            if len(st["hist"]) > 64:
                st["hist"] = st["hist"][-64:]
            return r
        return call

    def note_step_inc(self, instr, sem, val):
        for ent in self._st["hist"]:
            if ent[2] is instr:
                ent[4] = (sem, val)


def build_program(upto=9, dbg=""):
    nc = bass.Bass("TRN2", target_bir_lowering=False)

    def din(name, shape, dt=F32):
        return nc.dram_tensor(name, shape, dt, kind="ExternalInput").ap()

    x_own = din("x_own", [T, D]); x_prev = din("x_prev", [T, D])
    wq0 = din("wq0", [D, D]); wk0 = din("wk0", [D, D]); wv0 = din("wv0", [D, D])
    wgT = din("wgT", [D, INNER]); wuT = din("wuT", [D, INNER]); wdT = din("wdT", [INNER, D])
    qw = din("qw", [D, D]); kw = din("kw", [D, D]); vw = din("vw", [D, D])
    gw = din("gw", [D, D]); oww = din("oww", [D, D])
    bw = din("bw", [D, H]); aw = din("aw", [D, H])
    fcw = din("fcw", [D, INNER]); pww = din("pww", [INNER, D])
    w1T = din("w1T", [D, INNER]); w2T = din("w2T", [INNER, D])
    ln1row = din("ln1row", [128, D]); ln2row = din("ln2row", [128, D])
    onormrow = din("onormrow", [128, D]); b1fm_in = din("b1fm_in", [128, ICH])
    b2row_in = din("b2row_in", [128, D])
    alog_col = din("alog_col", [128, 1]); dtb_col = din("dtb_col", [128, 1])
    identf_in = din("identf_in", [128, 128])
    causal_in = din("causal_in", [128, 128])
    musU_in = din("musU_in", [128, 128]); musL_in = din("musL_in", [128, 128])
    muiU_in = din("muiU_in", [128, 128])
    maskattn = din("maskattn", [128, 1]); maskstate = din("maskstate", [128, 1])
    sel8_in = din("sel8_in", [8, 8 * 128])
    gather_off = din("gather_off", [128, 8], I32)

    out = nc.dram_tensor("out", [T, D], F32, kind="ExternalOutput").ap()
    dbg_out = nc.dram_tensor("dbg", [128, 1672], F32, kind="ExternalOutput").ap()

    kT_d = nc.dram_tensor("kT_d", [D, 2 * T], BF16).ap()
    v_d = nc.dram_tensor("v_d", [2 * T, D], BF16).ap()
    tg_d = nc.dram_tensor("tg_d", [D, INNER], BF16).ap()
    tu_d = nc.dram_tensor("tu_d", [D, INNER], BF16).ap()
    td_d = nc.dram_tensor("td_d", [INNER, D], BF16).ap()
    qf_d = nc.dram_tensor("qf_d", [D, T], BF16).ap()
    aT_d = nc.dram_tensor("aT_d", [NCH * H * 128, 128], BF16).ap()
    bC_d = nc.dram_tensor("bC_d", [NCH * H * 128, 128], F32).ap()
    oT_d = nc.dram_tensor("oT_d", [NCH * H * 128, 128], F32).ap()
    m2_d = nc.dram_tensor("m2_d", [NCH * H * 128, 128], BF16).ap()
    x1_d = nc.dram_tensor("x1_d", [T, D], F32).ap()
    cc_in = nc.dram_tensor("cc_in", [H * HD, HD], F32)
    cc_out = nc.dram_tensor("cc_out", [8 * H * HD, HD], F32)

    es = contextlib.ExitStack()

    def sb(stk, name, shape, dt=F32):
        return stk.enter_context(nc.sbuf_tensor(name, shape, dt))

    def pst(stk, name, shape, dt=F32):
        return stk.enter_context(nc.psum_tensor(name, shape, dt))

    with es:
        res = sb(es, "res", [128, NTI * D])
        stats = sb(es, "stats", [128, 104])
        ident = sb(es, "ident", [128, 128], BF16)
        identf = sb(es, "identf", [128, 128])
        causal = sb(es, "causal", [128, 128])
        musU = sb(es, "musU", [128, 128], BF16)
        musL = sb(es, "musL", [128, 128], BF16)
        muiU = sb(es, "muiU", [128, 128], BF16)
        mcA = sb(es, "mcA", [128, 1]); mcS = sb(es, "mcS", [128, 1])
        sel8 = sb(es, "sel8", [8, 8 * 128])
        goff = sb(es, "goff", [128, 8], I32)
        cvec = sb(es, "cvec", [128, 8])
        dump = sb(es, "dump", [128, 1024])
        ltile = sb(es, "ltile", [128, 1024])
        ltile2 = sb(es, "ltile2", [128, 1024])
        stg = sb(es, "stg", [128, 1024], BF16)
        stg2 = sb(es, "stg2", [128, 1024], BF16)
        dbgb = sb(es, "dbgb", [128, 1672 if dbg else 2])
        spc = sb(es, "spc", [128, 8])

        sems = {}
        for n in ["L", "G", "Te", "V", "S", "O", "Cc"]:
            sems[n] = [es.enter_context(nc.semaphore("s" + n)), 0]
        rawd = {n: [es.enter_context(nc.semaphore("d" + n)), 0, False]
                for n in ["L", "G"]}

        block = es.enter_context(nc.Block())
        ENG = {"L": block.sync, "G": block.gpsimd, "Te": block.tensor,
               "V": block.vector, "S": block.scalar}

        ilsem = {"V": es.enter_context(nc.semaphore("ilV")),
                 "S": es.enter_context(nc.semaphore("ilS"))}
        ilstate = {"V": {"hist": [], "ctr": 0}, "S": {"hist": [], "ctr": 0}}
        guards = {k: _EngGuard(k, ilsem[k], ilstate[k]) for k in ("V", "S")}

        def _drain(d):
            rw = rawd[d]
            pub = sems[d]
            if not rw[2]:
                return
            rw[2] = False
            pub[1] += 1

            def dbody(e):
                e.wait_ge(rw[0], rw[1])
                e.sem_inc(pub[0], 1)
            ENG[d](dbody)

        rawx = {}

        def dma_sem(name):
            if name not in rawx:
                rawx[name] = [es.enter_context(nc.semaphore("dx" + name)), 0]
            return rawx[name]

        def step(eng, fn, deps=(), dma=False, sem=None):
            for d in deps:
                if isinstance(d, str) and d in rawd:
                    _drain(d)
            if eng in rawd and not dma and sem is None:
                _drain(eng)
            dw = []
            for d in deps:
                if isinstance(d, str):
                    if sems[d][1] > 0:
                        dw.append((sems[d][0], sems[d][1]))
                elif d is not None:
                    dw.append((d[0], d[1]))
            me = sems[eng]
            tok = [None]

            def body(e):
                for s_, v_ in dw:
                    e.wait_ge(s_, v_)
                if eng in ("V", "S"):
                    guards[eng]._e = e
                    r = fn(guards[eng])
                else:
                    r = fn(e)
                if r is None:
                    return
                if dma:
                    tgt = dma_sem(sem) if sem else rawd[eng]
                    for ins in (r if isinstance(r, list) else [r]):
                        ins.then_inc(tgt[0], 16)
                        tgt[1] += 16
                    if sem:
                        tok[0] = (tgt[0], tgt[1])
                    else:
                        tgt[2] = True
                else:
                    r.then_inc(me[0], 1)
                    me[1] += 1
                    tok[0] = (me[0], me[1])
                    if eng in ("V", "S"):
                        guards[eng].note_step_inc(r, me[0], me[1])
            ENG[eng](body)
            return tok[0]

        def barrier():
            xtok = [(s_[0], s_[1]) for s_ in rawx.values() if s_[1] > 0]
            for e in ["L", "G", "Te", "V", "S"]:
                step(e, lambda _e: None,
                     deps=[d for d in ["L", "G", "Te", "V", "S", "O", "Cc"]
                           if d != e] + xtok)

        def DBG(tag, ap, col0=0):
            if dbg != tag:
                return
            def fd(v, ap=ap, col0=col0):
                n = ap.shape[-1] if len(ap.shape) == 2 else ap.shape[-1]
                return v.tensor_copy(dbgb[0:ap.shape[0], col0:col0 + n], ap)
            step("V", fd, deps=["S", "Te", "G", "L"])

        def fin():
            nc._sem_counts = {k: v[1] for k, v in sems.items()}
            def fo2(g):
                return g.dma_start(out=dbg_out[:, 0:dbgb.shape[1]], in_=dbgb[:, :])
            step("G", fo2, deps=["V", "S", "Te", "L"], dma=True)
            def fo(g):
                r = []
                orr = out.rearrange("(t p) d -> t p d", p=128)
                for t in range(NTI):
                    r.append(g.dma_start(out=orr[t], in_=res[:, t * D:(t + 1) * D]))
                return r
            step("G", fo, deps=["V", "S", "Te", "L"], dma=True)
            step("G", lambda g: None, deps=["G"])

        def ln_tile(src_ap, mu_j, rs_j, eps=1e-5, nd=D, dep0=()):
            def fv(v):
                v.reduce_sum(stats[:, ST_A:ST_A + 1], src_ap, AX.X)
                return v.tensor_scalar(stats[:, mu_j:mu_j + 1], stats[:, ST_A:ST_A + 1],
                                       1.0 / nd, None, ALU.mult)
            step("V", fv, deps=list(dep0))
            def fs(s):
                return s.activation(dump[:, 0:nd], src_ap, AF.Square,
                                    accum_out=stats[:, ST_B:ST_B + 1])
            ln_tile.src_stok = step("S", fs, deps=["V"])
            def fv2(v):
                v.tensor_tensor(stats[:, ST_A:ST_A + 1], stats[:, mu_j:mu_j + 1],
                                stats[:, mu_j:mu_j + 1], ALU.mult)
                v.tensor_scalar(stats[:, ST_B:ST_B + 1], stats[:, ST_B:ST_B + 1],
                                1.0 / nd, None, ALU.mult)
                v.tensor_tensor(stats[:, ST_A:ST_A + 1], stats[:, ST_B:ST_B + 1],
                                stats[:, ST_A:ST_A + 1], ALU.subtract)
                return v.tensor_scalar(stats[:, ST_A:ST_A + 1], stats[:, ST_A:ST_A + 1],
                                       eps, None, ALU.add)
            step("V", fv2, deps=["S"])
            def fs2(s):
                return s.activation(stats[:, ST_B:ST_B + 1], stats[:, ST_A:ST_A + 1],
                                    AF.Ln)
            step("S", fs2, deps=["V"])
            def fv3(v):
                return v.tensor_scalar(stats[:, ST_B + 1:ST_B + 2],
                                       stats[:, ST_B:ST_B + 1], -0.5, None, ALU.mult)
            step("V", fv3, deps=["S"])
            def fs3(s):
                return s.activation(stats[:, rs_j:rs_j + 1], stats[:, ST_B + 1:ST_B + 2],
                                    AF.Exp)
            ln_tile.end_stok = step("S", fs3, deps=["V"])

        def rsqrt_cols(col, n, dstcol):
            def fs(s):
                return s.activation(stats[:, ST_B:ST_B + n], stats[:, col:col + n],
                                    AF.Ln)
            step("S", fs, deps=["V"])
            def fv(v):
                return v.tensor_scalar(stats[:, ST_B:ST_B + n], stats[:, ST_B:ST_B + n],
                                       -0.5, None, ALU.mult)
            step("V", fv, deps=["S"])
            def fs2b(s):
                return s.activation(stats[:, dstcol:dstcol + n], stats[:, ST_B:ST_B + n],
                                    AF.Exp)
            step("S", fs2b, deps=["V"])

        def load_weight_bf16(wap, dst, dep=("Te",)):
            rows, cols = wap.shape
            wr = wap.rearrange("(c p) d -> c p d", p=128)
            def fg(g):
                return [g.dma_start(out=dst[:, ci * cols:(ci + 1) * cols], in_=wr[ci])
                        for ci in range(rows // 128)]
            step("G", fg, deps=list(dep), dma=True)

        def residual_add_ln(block_out_ap, t):
            ln_tile(block_out_ap, ST_A + 8, ST_A + 9)
            def fr(v):
                v.tensor_scalar(dump[:, 0:D], block_out_ap,
                                stats[:, ST_A + 8:ST_A + 9], stats[:, ST_A + 9:ST_A + 10],
                                ALU.subtract, ALU.mult)
                return v.tensor_tensor(res[:, t * D:(t + 1) * D],
                                       res[:, t * D:(t + 1) * D], dump[:, 0:D], ALU.add)
            step("V", fr, deps=["S"])

        def p0(sy):
            r = []
            xr = x_own.rearrange("(t p) d -> t p d", p=128)
            for t in range(NTI):
                r.append(sy.dma_start(out=res[:, t * D:(t + 1) * D], in_=xr[t]))
            for dst, src in [(causal, causal_in), (identf, identf_in),
                             (mcA, maskattn), (mcS, maskstate)]:
                r.append(sy.dma_start(out=dst[:, :], in_=src[:, :]))
            r.append(sy.dma_start(out=sel8[:, :], in_=sel8_in[:, :]))
            r.append(sy.dma_start(out=stats[:, 94:95], in_=alog_col[:, :]))
            r.append(sy.dma_start(out=stats[:, 95:96], in_=dtb_col[:, :]))
            return r
        step("L", p0, dma=True)

        def p0g(g):
            r = []
            for dst, src in [(ident, identf_in), (musU, musU_in), (musL, musL_in),
                             (muiU, muiU_in)]:
                r.append(g.dma_start(out=dst[:, :], in_=src[:, :]))
            r.append(g.dma_start(out=goff[:, :], in_=gather_off[:, :]))
            return r
        step("G", p0g, dma=True)

        step("S", lambda s: s.activation(cvec[:, 0:1], stats[:, 94:95], AF.Exp),
             deps=["L"])

        def p0v(v):
            v.memset(dbgb[:, :], 0.0)
            v.memset(spc[:, :], 0.0)
            v.memset(stats[:, 0:94], 0.0)
            v.memset(dump[:, :], 0.0)
            v.memset(ltile[:, :], 0.0)
            v.memset(ltile2[:, :], 0.0)
            v.tensor_scalar(cvec[:, 0:1], cvec[:, 0:1], -1.0, None, ALU.mult)
            v.tensor_copy(cvec[:, 1:2], stats[:, 95:96])
            v.memset(stats[:, W_COL:W_COL + 1], 0.75)
            v.tensor_scalar(stats[:, W_COL:W_COL + 1], stats[:, W_COL:W_COL + 1],
                            MAGIC64, None, ALU.add)
            v.tensor_scalar(stats[:, W_COL:W_COL + 1], stats[:, W_COL:W_COL + 1],
                            -MAGIC64, None, ALU.add)
            return v.tensor_scalar(stats[:, M_COL:M_COL + 1],
                                   stats[:, W_COL:W_COL + 1],
                                   MAGIC64, MAGIC, ALU.mult, ALU.add)
        step("V", p0v, deps=["S"])
        barrier()
        if upto <= 0:
            fin()
            return nc

        p1 = contextlib.ExitStack()
        with p1:
            qfm = sb(p1, "qfm", [128, DCH * T], BF16)
            pT1 = pst(p1, "pT1", [128, 1024], BF16)
            pA = pst(p1, "pA", [128, 512])
            pB = pst(p1, "pB", [128, 512])

            p1h = contextlib.ExitStack()
            with p1h:
                hfm = sb(p1h, "hfm", [128, DCH * T], BF16)

                def build_h(half):
                    src = x_prev if half == 0 else x_own
                    xr = src.rearrange("(t p) d -> t p d", p=128)
                    for t in range(NTI):
                        if half == 0:
                            def fl(sy, t=t, xr=xr):
                                return sy.dma_start(out=ltile[:, 0:D], in_=xr[t])
                            step("L", fl, deps=["V", "S"], dma=True)
                            src_ap, mu, rs = ltile[:, 0:D], ST_MU, ST_RS
                            ln_tile(src_ap, mu, rs, dep0=("L",))
                        else:
                            src_ap = res[:, t * D:(t + 1) * D]
                            mu, rs = ST_MU + t, ST_RS + t
                            ln_tile(src_ap, mu, rs)
                        def fap(v, src_ap=src_ap, mu=mu, rs=rs):
                            return v.tensor_scalar(stg[:, 0:D], src_ap,
                                                   stats[:, mu:mu + 1],
                                                   stats[:, rs:rs + 1],
                                                   ALU.subtract, ALU.mult)
                        step("V", fap, deps=["S"])
                        if half == 1 and t == 0:
                            DBG("h0", stg[:, 0:D])
                            DBG("h0stats", stats[:, 0:96])
                        def ftp(te, t=t):
                            for i in range(DCH):
                                r = te.transpose(pT1[:, i * 128:(i + 1) * 128],
                                                 stg[:, i * 128:(i + 1) * 128],
                                                 ident[:, :])
                            return r
                        step("Te", ftp, deps=["V"])
                        def fev(v, t=t):
                            for i in range(DCH):
                                r = v.tensor_copy(
                                    hfm[:, i * T + t * 128:i * T + (t + 1) * 128],
                                    pT1[:, i * 128:(i + 1) * 128])
                            return r
                        step("V", fev, deps=["Te"])

                def proj_kv(half):
                    key0 = 0 if half == 0 else T
                    pw2 = contextlib.ExitStack()
                    with pw2:
                        wkb = sb(pw2, "wkb%d" % half, [128, DCH * D], BF16)
                        wvb = sb(pw2, "wvb%d" % half, [128, DCH * D], BF16)
                        load_weight_bf16(wk0, wkb)
                        load_weight_bf16(wv0, wvb)
                        barrier()
                        for of in range(DCH):
                            for w in range(4):
                                def fm(te, of=of, w=w):
                                    for ci in range(DCH):
                                        r = te.matmul(
                                            pA[:, 0:512],
                                            wkb[:, ci * D + of * 128:ci * D + (of + 1) * 128],
                                            hfm[:, ci * T + w * 512:ci * T + (w + 1) * 512],
                                            start=(ci == 0), stop=(ci == DCH - 1))
                                    return r
                                step("Te", fm, deps=["V", "L"])
                                def fe(v):
                                    return v.tensor_copy(stg[:, 0:512], pA[:, 0:512])
                                step("V", fe, deps=["Te", "L"])
                                def fd(sy, of=of, w=w, key0=key0):
                                    return sy.dma_start(
                                        out=kT_d[of * 128:(of + 1) * 128,
                                                 key0 + w * 512:key0 + (w + 1) * 512],
                                        in_=stg[:, 0:512])
                                step("L", fd, deps=["V"], dma=True)
                        for t in range(NTI):
                            for nn2 in range(2):
                                def fm(te, t=t, nn2=nn2):
                                    for ci in range(DCH):
                                        r = te.matmul(
                                            pB[:, 0:512],
                                            hfm[:, ci * T + t * 128:ci * T + (t + 1) * 128],
                                            wvb[:, ci * D + nn2 * 512:ci * D + (nn2 + 1) * 512],
                                            start=(ci == 0), stop=(ci == DCH - 1))
                                    return r
                                step("Te", fm, deps=["V", "L"])
                                def fe(v, nn2=nn2):
                                    return v.tensor_copy(stg2[:, nn2 * 512:(nn2 + 1) * 512],
                                                         pB[:, 0:512])
                                step("V", fe, deps=["Te", "L"])
                            def fd(sy, t=t, key0=key0):
                                return sy.dma_start(
                                    out=v_d[key0 + t * 128:key0 + (t + 1) * 128, :],
                                    in_=stg2[:, 0:D])
                            step("L", fd, deps=["V"], dma=True)

                build_h(0)
                proj_kv(0)
                build_h(1)
                proj_kv(1)

                pq = contextlib.ExitStack()
                with pq:
                    wqb = sb(pq, "wqb", [128, DCH * D], BF16)
                    load_weight_bf16(wq0, wqb)
                    barrier()
                    for of in range(DCH):
                        for w in range(4):
                            def fm(te, of=of, w=w):
                                for ci in range(DCH):
                                    r = te.matmul(
                                        pA[:, 0:512],
                                        wqb[:, ci * D + of * 128:ci * D + (of + 1) * 128],
                                        hfm[:, ci * T + w * 512:ci * T + (w + 1) * 512],
                                        start=(ci == 0), stop=(ci == DCH - 1))
                                return r
                            step("Te", fm, deps=["V", "L"])
                            def fe(v, of=of, w=w):
                                return v.tensor_scalar(
                                    qfm[:, of * T + w * 512:of * T + (w + 1) * 512],
                                    pA[:, 0:512], 1.0 / 32.0, None, ALU.mult)
                            step("V", fe, deps=["Te"])
                barrier()

            fl1 = contextlib.ExitStack()
            with fl1:
                kst = sb(fl1, "kst", [128, 2 * DCH * 512], BF16)
                vst = sb(fl1, "vst", [128, 2 * 4 * D], BF16)
                sco = sb(fl1, "sco", [128, 4096])
                pbuf = sb(fl1, "pbuf", [128, 4096], BF16)
                pC = pst(fl1, "pC", [128, 512])
                pD = pst(fl1, "pD", [128, 512])

                for qt in range(NTI):
                    ext = T + (qt + 1) * 128
                    nwin = (ext + 511) // 512
                    for w in range(nwin):
                        k0w = w * 512
                        kn = min(ext, k0w + 512) - k0w
                        buf = w % 2
                        def fk(sy, k0w=k0w, kn=kn, buf=buf):
                            return sy.dma_start(
                                out=kst[:, buf * DCH * 512:buf * DCH * 512 + DCH * kn]
                                    .rearrange("p (c n) -> p c n", c=DCH),
                                in_=kT_d[:, k0w:k0w + kn]
                                    .rearrange("(c p) n -> c p n", p=128).transpose([1, 0, 2]))
                        step("L", fk, deps=["Te"], dma=True)
                        def fm(te, kn=kn, buf=buf, qt=qt):
                            for ci in range(DCH):
                                r = te.matmul(
                                    pA[:, 0:kn],
                                    qfm[:, ci * T + qt * 128:ci * T + (qt + 1) * 128],
                                    kst[:, buf * DCH * 512 + ci * kn:
                                        buf * DCH * 512 + (ci + 1) * kn],
                                    start=(ci == 0), stop=(ci == DCH - 1))
                            return r
                        step("Te", fm, deps=["L", "V"])
                        def fe(v, w=w, k0w=k0w, kn=kn, qt=qt):
                            r = None
                            col = k0w
                            while col < k0w + kn:
                                ce = min(col + 128, k0w + kn)
                                if ce <= T:
                                    r = v.tensor_scalar(sco[:, col:ce],
                                                        pA[:, col - k0w:ce - k0w],
                                                        mcA[:, 0:1], None, ALU.add)
                                elif (col - T) // 128 == qt:
                                    r = v.tensor_tensor(sco[:, col:ce],
                                                        pA[:, col - k0w:ce - k0w],
                                                        causal[:, 0:ce - col], ALU.add)
                                else:
                                    r = v.tensor_copy(sco[:, col:ce],
                                                      pA[:, col - k0w:ce - k0w])
                                col = ce
                            return r
                        step("V", fe, deps=["Te"])
                    if qt == 0:
                        DBG("sco0", sco[:, 0:1672])
                    def fmx(v, ext=ext):
                        return v.reduce_max(stats[:, ST_M:ST_M + 1], sco[:, 0:ext],
                                            AX.X, negate=True)
                    step("V", fmx)
                    def fex(s, ext=ext):
                        return s.activation(pbuf[:, 0:ext], sco[:, 0:ext], AF.Exp,
                                            bias=stats[:, ST_M:ST_M + 1],
                                            accum_out=stats[:, ST_L:ST_L + 1])
                    step("S", fex, deps=["V"])
                    def frc(v):
                        return v.reciprocal(stats[:, ST_L + 1:ST_L + 2],
                                            stats[:, ST_L:ST_L + 1])
                    step("V", frc, deps=["S"])
                    nkt = ext // 128
                    first = True
                    for g0 in range(0, nkt, 4):
                        g1 = min(nkt, g0 + 4)
                        buf = (g0 // 4) % 2
                        def fv(sy, g0=g0, g1=g1, buf=buf):
                            return sy.dma_start(
                                out=vst[:, buf * 4 * D:buf * 4 * D + (g1 - g0) * D]
                                    .rearrange("p (a b) -> p a b", b=D),
                                in_=v_d[g0 * 128:g1 * 128, :]
                                    .rearrange("(a p) b -> a p b", p=128).transpose([1, 0, 2]))
                        step("L", fv, deps=["Te"], dma=True)
                        for kt in range(g0, g1):
                            def ftp(te, kt=kt):
                                return te.transpose(pT1[:, 0:128],
                                                    pbuf[:, kt * 128:(kt + 1) * 128],
                                                    ident[:, :])
                            step("Te", ftp, deps=["V", "S"])
                            def fpe(v):
                                return v.tensor_copy(stg[:, 0:128], pT1[:, 0:128])
                            step("V", fpe, deps=["Te"])
                            def fav(te, kt=kt, buf=buf, g0=g0, first=first, nkt=nkt):
                                te.matmul(pC[:, 0:512], stg[:, 0:128],
                                          vst[:, buf * 4 * D + (kt - g0) * D:
                                              buf * 4 * D + (kt - g0) * D + 512],
                                          start=first, stop=(kt == nkt - 1))
                                return te.matmul(
                                    pD[:, 0:512], stg[:, 0:128],
                                    vst[:, buf * 4 * D + (kt - g0) * D + 512:
                                        buf * 4 * D + (kt - g0 + 1) * D],
                                    start=first, stop=(kt == nkt - 1))
                            step("Te", fav, deps=["V", "L"])
                            first = False
                    def ff(v):
                        v.tensor_scalar(ltile[:, 0:512], pC[:, 0:512],
                                        stats[:, ST_L + 1:ST_L + 2], None, ALU.mult)
                        return v.tensor_scalar(ltile[:, 512:1024], pD[:, 0:512],
                                               stats[:, ST_L + 1:ST_L + 2], None,
                                               ALU.mult)
                    step("V", ff, deps=["Te"])
                    if qt == 0:
                        DBG("acc0", ltile[:, 0:D])
                        DBG("pb0", pbuf[:, 0:1672])
                        DBG("l0", stats[:, 64:67])
                    residual_add_ln(ltile[:, 0:D], qt)
            barrier()
        if upto <= 1:
            fin()
            return nc

        p2 = contextlib.ExitStack()
        with p2:
            scratch = sb(p2, "scratch", [128, NTI * 1024])
            nfm = sb(p2, "nfm", [128, DCH * T], BF16)
            wstr = sb(p2, "wstr", [128, 2 * 512])
            tstr = sb(p2, "tstr", [128, 1024], BF16)
            s2bc = sb(p2, "s2bc", [128, 512])
            prow = sb(p2, "prow", [128, 512])
            n2t = sb(p2, "n2t", [128, 512], BF16)
            iv2 = sb(p2, "iv2", [128, 8])
            tbuf = sb(p2, "tbuf", [128, 2 * DCH * 128], BF16)
            tbuf2 = sb(p2, "tbuf2", [128, 2 * DCH * 128], BF16)
            pA2 = pst(p2, "pA2", [128, 512])
            pB2 = pst(p2, "pB2", [128, 512])
            pR2 = pst(p2, "pR2", [128, 512])
            pD2 = pst(p2, "pD2", [128, 512])
            pT2 = pst(p2, "pT2", [128, 1024], BF16)

            DBG("p2res", res[:, 0:1024], 0)
            DBG("p2res", res[:, 15 * D:15 * D + 512], 1024)
            for t in range(NTI):
                ln_tile(res[:, t * D:(t + 1) * D], ST_MU + t, ST_RS + t)
                def fap(v, t=t):
                    v.tensor_scalar(ltile[:, 0:D], res[:, t * D:(t + 1) * D],
                                    stats[:, ST_MU + t:ST_MU + t + 1],
                                    stats[:, ST_RS + t:ST_RS + t + 1],
                                    ALU.subtract, ALU.mult)
                    v.reduce_max(stats[:, ST_A:ST_A + 1], ltile[:, 0:D], AX.X,
                                 apply_absolute_value=True)
                    v.tensor_scalar(stats[:, ST_A:ST_A + 1], stats[:, ST_A:ST_A + 1],
                                    1e-5, None, ALU.max)
                    v.tensor_scalar(stats[:, SCL + t:SCL + t + 1],
                                    stats[:, ST_A:ST_A + 1], 1.0 / 127.0, None, ALU.mult)
                    v.reciprocal(stats[:, ST_B:ST_B + 1], stats[:, ST_A:ST_A + 1])
                    v.tensor_scalar(stats[:, ST_B:ST_B + 1], stats[:, ST_B:ST_B + 1],
                                    127.0, None, ALU.mult)
                    v.tensor_scalar(ltile2[:, 0:D], ltile[:, 0:D],
                                    stats[:, ST_B:ST_B + 1],
                                    stats[:, M_COL:M_COL + 1],
                                    ALU.mult, ALU.add)
                    return v.tensor_scalar(stg[:, 0:D], ltile2[:, 0:D],
                                           stats[:, M_COL:M_COL + 1], None,
                                           ALU.subtract)
                step("V", fap, deps=["S"])
                if t == 0:
                    DBG("p2res", stg[:, 0:128], 1536)
                def ftp(te, t=t):
                    for i in range(DCH):
                        r = te.transpose(pT2[:, i * 128:(i + 1) * 128],
                                         stg[:, i * 128:(i + 1) * 128], ident[:, :])
                    return r
                step("Te", ftp, deps=["V"])
                def fev(v, t=t):
                    for i in range(DCH):
                        r = v.tensor_copy(nfm[:, i * T + t * 128:i * T + (t + 1) * 128],
                                          pT2[:, i * 128:(i + 1) * 128])
                    return r
                step("V", fev, deps=["Te"])

            for t5 in range(16):
                DBG("p2r2", res[:, t5 * D:t5 * D + 64], t5 * 64)
            DBG("p2st", stats[:, 0:96], 0)
            for i5 in range(8):
                DBG("p2st", nfm[:, i5 * T + 15 * 128:i5 * T + 16 * 128], 96 + i5 * 128)
            DBG("p2st", nfm[:, 0:128], 1120)
            ab_tok = [None, None]
            fa_tok = [None, None]
            vt_tok = [None, None]
            stw_tok = [None, None]
            for wi, (wap, tdram) in enumerate([(wgT, tg_d), (wuT, tu_d), (wdT, td_d)]):
                rows, cols = wap.shape
                wr = wap.rearrange("(a p) (b q) -> a p b q", p=128, q=512)
                nb = cols // 512
                nch2 = (rows // 128) * nb
                step("V", lambda v: v.memset(stats[:, WS:WS + 1], 0.0), deps=["S"])
                for ch in range(nch2):
                    b = ch % 2
                    def fl(sy, ch=ch, wr=wr, b=b):
                        return sy.dma_start(
                            out=wstr[:, b * 512:(b + 1) * 512],
                            in_=wr[ch // nb, :, ch % nb, :])
                    lw = step("L" if b == 0 else "G", fl,
                              deps=[ab_tok[b], vt_tok[b]], dma=True,
                              sem="wf%d" % b)
                    def fs(s, b=b):
                        return s.activation(dump[:, b * 512:(b + 1) * 512],
                                            wstr[:, b * 512:(b + 1) * 512],
                                            AF.Abs,
                                            accum_out=stats[:, ST_A + b:ST_A + b + 1])
                    ab_tok[b] = step("S", fs, deps=[lw, fa_tok[b]])
                    def fa(v, b=b):
                        return v.tensor_tensor(stats[:, WS:WS + 1], stats[:, WS:WS + 1],
                                               stats[:, ST_A + b:ST_A + b + 1],
                                               ALU.add)
                    fa_tok[b] = step("V", fa, deps=[ab_tok[b]])
                def fbf_t(te):
                    return te.transpose(pA2[0:1, 0:128], stats[:, WS:WS + 1],
                                        identf[:, :])
                step("Te", fbf_t, deps=["S", "V"])
                def fbf_r(v, rows=rows, cols=cols):
                    v.reduce_sum(stats[0:1, 90:91], pA2[0:1, 0:128], AX.X)
                    v.tensor_scalar(stats[0:1, 92:93], stats[0:1, 90:91],
                                    1.0 / (rows * cols), None, ALU.mult)
                    v.tensor_scalar(stats[0:1, 92:93],
                                    stats[0:1, 92:93], 1e-5, None, ALU.max)
                    return v.reciprocal(stats[0:1, 93:94], stats[0:1, 92:93])
                step("V", fbf_r, deps=["Te"])
                def fbf_b(te):
                    te.matmul(pA2[:, 0:1], sel8[0:1, 0:128], stats[0:1, 93:94],
                              start=True, stop=True)
                    return te.matmul(pA2[:, 1:2], sel8[0:1, 0:128],
                                     stats[0:1, 92:93], start=True, stop=True)
                step("Te", fbf_b, deps=["V"])
                def fbf_e(v, wi=wi):
                    v.tensor_copy(stats[:, WS + 1:WS + 2], pA2[:, 0:1])
                    v.tensor_copy(stats[:, SWG + wi:SWG + wi + 1], pA2[:, 0:1])
                    return v.tensor_copy(stats[:, SWG + 3 + wi:SWG + 4 + wi],
                                         pA2[:, 1:2])
                step("V", fbf_e, deps=["Te"])
                DBG("p2w", stats[:, 80:96], wi * 16)
                DBG("p2w", pA2[:, 0:2], 48 + wi * 2)
                DBG("p2w", stats[0:1, WS:WS + 4], 54 + wi * 4)
                tdr = tdram.rearrange("(a p) (b q) -> a p b q", p=128, q=512)
                for ch in range(nch2):
                    b = ch % 2
                    def fl(sy, ch=ch, wr=wr, b=b):
                        return sy.dma_start(
                            out=wstr[:, b * 512:(b + 1) * 512],
                            in_=wr[ch // nb, :, ch % nb, :])
                    lw = step("L" if b == 0 else "G", fl,
                              deps=[ab_tok[b], vt_tok[b]], dma=True,
                              sem="wf%d" % b)
                    def ft(v, b=b):
                        sl = wstr[:, b * 512:(b + 1) * 512]
                        v.tensor_scalar(dump[:, b * 512:(b + 1) * 512], sl,
                                        stats[:, WS + 1:WS + 2],
                                        stats[:, M_COL:M_COL + 1],
                                        ALU.mult, ALU.add)
                        v.tensor_scalar(ltile[:, b * 512:(b + 1) * 512],
                                        dump[:, b * 512:(b + 1) * 512],
                                        stats[:, M_COL:M_COL + 1], 1.0,
                                        ALU.subtract, ALU.min)
                        return v.tensor_scalar(tstr[:, b * 512:(b + 1) * 512],
                                               ltile[:, b * 512:(b + 1) * 512],
                                               -1.0, None, ALU.max)
                    vt_tok[b] = step("V", ft, deps=[lw, stw_tok[b]])
                    def fst(g, ch=ch, tdr=tdr, b=b):
                        return g.dma_start(out=tdr[ch // nb, :, ch % nb, :],
                                           in_=tstr[:, b * 512:(b + 1) * 512])
                    stw_tok[b] = step("G" if b == 0 else "L", fst,
                                      deps=[vt_tok[b]], dma=True, sem="ws%d" % b)
            barrier()

            tgr_p = tg_d.rearrange("(c p) o -> c p o", p=128).transpose([1, 0, 2])
            tur_p = tu_d.rearrange("(c p) o -> c p o", p=128).transpose([1, 0, 2])
            tdr_p = td_d.rearrange("(c p) o -> c p o", p=128).transpose([1, 0, 2])
            use_g = [None, None]
            eg_tok = [None, None]
            eu_tok = [None, None]
            nq_war = [None, None]
            for tb in range(4):
                tok0 = tb * 512
                def fbt(te, tb=tb):
                    return te.transpose(pR2[0:4, 0:128],
                                        stats[:, SCL + 4 * tb:SCL + 4 * tb + 4],
                                        identf[:, :])
                step("Te", fbt, deps=["V", "S"])
                def fbe(v):
                    v.memset(ltile[0:8, 0:128], 0.0)
                    return v.tensor_copy(ltile[0:4, 0:128], pR2[0:4, 0:128])
                step("V", fbe, deps=["Te"])
                def fbm(te):
                    for i in range(4):
                        r = te.matmul(pR2[:, i * 128:(i + 1) * 128],
                                      sel8[:, i * 128:i * 128 + 128],
                                      ltile[0:8, 0:128],
                                      start=True, stop=True)
                    return r
                step("Te", fbm, deps=["V"])
                def fbc(v):
                    return v.tensor_copy(s2bc[:, 0:512], pR2[:, 0:512])
                fbc_t = step("V", fbc, deps=["Te"])

                for oc in range(ICH):
                    b = oc % 2
                    ps_g = [pA2, pR2][b]
                    ps_u = [pB2, pD2][b]
                    def flg(sy, oc=oc, b=b):
                        r1 = sy.dma_start(
                            out=tbuf[:, b * DCH * 128:(b + 1) * DCH * 128]
                                .rearrange("p (c n) -> p c n", c=DCH),
                            in_=tgr_p[:, :, oc * 128:(oc + 1) * 128])
                        r2 = sy.dma_start(
                            out=tbuf2[:, b * DCH * 128:(b + 1) * DCH * 128]
                                .rearrange("p (c n) -> p c n", c=DCH),
                            in_=tur_p[:, :, oc * 128:(oc + 1) * 128])
                        return [r1, r2]
                    lg = step("L", flg, deps=[use_g[b]], dma=True, sem="tg%d" % b)
                    def fmg(te, b=b, tok0=tok0, ps_g=ps_g, ps_u=ps_u):
                        for ci in range(DCH):
                            te.matmul(ps_g[:, 0:512],
                                      tbuf[:, b * DCH * 128 + ci * 128:
                                           b * DCH * 128 + (ci + 1) * 128],
                                      nfm[:, ci * T + tok0:ci * T + tok0 + 512],
                                      start=(ci == 0), stop=(ci == DCH - 1))
                        for ci in range(DCH):
                            r = te.matmul(ps_u[:, 0:512],
                                          tbuf2[:, b * DCH * 128 + ci * 128:
                                                b * DCH * 128 + (ci + 1) * 128],
                                          nfm[:, ci * T + tok0:ci * T + tok0 + 512],
                                          start=(ci == 0), stop=(ci == DCH - 1))
                        return r
                    mg = step("Te", fmg,
                              deps=[lg, eg_tok[b], eu_tok[b]]
                              + ([fbc_t] if oc < 2 else []))
                    use_g[b] = mg
                    def feg(v, b=b, ps_g=ps_g):
                        return v.scalar_tensor_tensor(ltile[:, b * 512:(b + 1) * 512],
                                                      ps_g[:, 0:512],
                                                      stats[:, SWG + 3:SWG + 4],
                                                      s2bc[:, 0:512],
                                                      ALU.mult, ALU.mult)
                    eg_tok[b] = step("V", feg, deps=[mg])
                    def fsi(s, b=b):
                        return s.activation(ltile2[:, b * 512:(b + 1) * 512],
                                            ltile[:, b * 512:(b + 1) * 512], AF.Silu)
                    si = step("S", fsi, deps=[eg_tok[b]])
                    def fup(v, oc=oc, b=b, ps_u=ps_u):
                        v.scalar_tensor_tensor(ltile[:, b * 512:(b + 1) * 512],
                                               ps_u[:, 0:512],
                                               stats[:, SWG + 4:SWG + 5],
                                               s2bc[:, 0:512], ALU.mult, ALU.mult)
                        return v.tensor_tensor(scratch[:, oc * 512:(oc + 1) * 512],
                                               ltile2[:, b * 512:(b + 1) * 512],
                                               ltile[:, b * 512:(b + 1) * 512],
                                               ALU.mult)
                    eu_tok[b] = step("V", fup, deps=[si])
                    if tb == 0 and oc == 0:
                        DBG("p2oc", ltile2[:, 0:512], 0)
                        DBG("p2oc", ltile[:, 0:512], 512)
                        DBG("p2oc", scratch[:, 0:512], 1024)
                        DBG("p2oc", stats[:, 80:96], 1536)
                        DBG("p2oc", s2bc[:, 0:64], 1552)
                def fq1(v):
                    v.tensor_tensor(prow[:, 0:512], scratch[:, 0:512],
                                    scratch[:, 512:1024], ALU.max)
                    v.tensor_tensor(ltile2[:, 0:512], scratch[:, 0:512],
                                    scratch[:, 512:1024], ALU.min)
                    for ic in range(2, ICH):
                        v.tensor_tensor(prow[:, 0:512], prow[:, 0:512],
                                        scratch[:, ic * 512:(ic + 1) * 512], ALU.max)
                        v.tensor_tensor(ltile2[:, 0:512], ltile2[:, 0:512],
                                        scratch[:, ic * 512:(ic + 1) * 512], ALU.min)
                    v.tensor_scalar(ltile2[:, 0:512], ltile2[:, 0:512], -1.0, None,
                                    ALU.mult)
                    return v.tensor_tensor(prow[:, 0:512], prow[:, 0:512],
                                           ltile2[:, 0:512], ALU.max)
                step("V", fq1)
                def fi2(te):
                    for i in range(4):
                        r = te.transpose(pR2[:, i * 128:(i + 1) * 128],
                                         prow[:, i * 128:(i + 1) * 128], identf[:, :])
                    return r
                step("Te", fi2, deps=["V"])
                def fi2e(v):
                    for i in range(4):
                        v.reduce_max(iv2[:, i:i + 1], pR2[:, i * 128:(i + 1) * 128],
                                     AX.X)
                    v.tensor_scalar(iv2[:, 0:4], iv2[:, 0:4], 1e-5, None, ALU.max)
                    v.reciprocal(iv2[:, 4:8], iv2[:, 0:4])
                    v.tensor_scalar(iv2[:, 0:4], iv2[:, 0:4], 1.0 / 127.0, None,
                                    ALU.mult)
                    return v.tensor_scalar(iv2[:, 4:8], iv2[:, 4:8], 127.0, None,
                                           ALU.mult)
                step("V", fi2e, deps=["Te"])
                def fi3(te):
                    for i in range(4):
                        r = te.transpose(pD2[0:1, i * 128:(i + 1) * 128],
                                         iv2[:, 4 + i:5 + i], identf[:, :])
                    return r
                step("Te", fi3, deps=["V"])
                def fi3e(v):
                    return v.tensor_copy(ltile[0:1, 0:512], pD2[0:1, 0:512])
                step("V", fi3e, deps=["Te"])
                def fi4(te):
                    for i in range(4):
                        r = te.matmul(pD2[:, i * 128:(i + 1) * 128],
                                      sel8[0:1, 0:128],
                                      ltile[0:1, i * 128:(i + 1) * 128],
                                      start=True, stop=True)
                    return r
                step("Te", fi4, deps=["V"])
                def fi4e(v):
                    return v.tensor_copy(s2bc[:, 0:512], pD2[:, 0:512])
                step("V", fi4e, deps=["Te"])
                if tb == 0:
                    DBG("p2mix", scratch[:, 0:512], 0)
                    DBG("p2mix", s2bc[:, 0:512], 512)
                    DBG("p2mix", iv2[:, 0:8], 1024)
                    DBG("p2mix", prow[:, 0:512], 1032)
                    DBG("p2mix", nfm[:, 0:128], 1544)
                dn_use = [use_g[0], use_g[1]]
                for ph in range(2):
                    c0 = ph * 256
                    for ic in range(ICH):
                        b = ic % 2
                        def fn2(v, ic=ic, c0=c0, b=b):
                            v.tensor_tensor(ltile2[:, b * 512:b * 512 + 256],
                                            scratch[:, ic * 512 + c0:ic * 512 + c0 + 256],
                                            s2bc[:, c0:c0 + 256], ALU.mult)
                            v.tensor_scalar(ltile2[:, b * 512 + 256:b * 512 + 512],
                                            ltile2[:, b * 512:b * 512 + 256],
                                            stats[:, M_COL:M_COL + 1], None, ALU.add)
                            return v.tensor_scalar(n2t[:, b * 256:(b + 1) * 256],
                                                   ltile2[:, b * 512 + 256:b * 512 + 512],
                                                   stats[:, M_COL:M_COL + 1], None,
                                                   ALU.subtract)
                        nq = step("V", fn2, deps=[nq_war[b]])
                        def fld(sy, ic=ic, b=b):
                            return sy.dma_start(
                                out=tbuf[:, b * 1024:(b + 1) * 1024],
                                in_=td_d[ic * 128:(ic + 1) * 128, :])
                        ld = step("L", fld, deps=[dn_use[b]], dma=True,
                                  sem="td%d" % b)
                        def fmd(te, ic=ic, b=b):
                            for ti in range(2):
                                for nn2 in range(2):
                                    pp = [pA2, pB2, pR2, pD2][ti * 2 + nn2]
                                    r = te.matmul(pp[:, 0:512],
                                                  n2t[:, b * 256 + ti * 128:
                                                      b * 256 + (ti + 1) * 128],
                                                  tbuf[:, b * 1024 + nn2 * 512:
                                                       b * 1024 + (nn2 + 1) * 512],
                                                  start=(ic == 0), stop=(ic == ICH - 1))
                            return r
                        mm = step("Te", fmd,
                                  deps=[ld, nq] + (["V", "S"] if ic == 0 else []))
                        dn_use[b] = mm
                        nq_war[b] = mm
                    for ti in range(2):
                        t_abs = tb * 4 + ph * 2 + ti
                        def fde(v, ti=ti, ph=ph, t_abs=t_abs):
                            v.tensor_scalar(ltile[:, 0:512],
                                            [pA2, pB2, pR2, pD2][ti * 2][:, 0:512],
                                            iv2[:, ph * 2 + ti:ph * 2 + ti + 1],
                                            stats[:, SWG + 5:SWG + 6],
                                            ALU.mult, ALU.mult)
                            return v.tensor_scalar(ltile[:, 512:1024],
                                                   [pA2, pB2, pR2, pD2][ti * 2 + 1][:, 0:512],
                                                   iv2[:, ph * 2 + ti:ph * 2 + ti + 1],
                                                   stats[:, SWG + 5:SWG + 6],
                                                   ALU.mult, ALU.mult)
                        step("V", fde, deps=["Te"])
                        residual_add_ln(ltile[:, 0:D], t_abs)
                use_g = [dn_use[0], dn_use[1]]
            barrier()
        if upto <= 2:
            fin()
            return nc

        p3 = contextlib.ExitStack()
        with p3:
            for t in range(NTI):
                ln_tile(res[:, t * D:(t + 1) * D], ST_MU + t, ST_RS + t)
            def frr(v):
                v.tensor_tensor(stats[:, ST_A:ST_A + NTI], stats[:, ST_RS:ST_RS + NTI],
                                stats[:, ST_RS:ST_RS + NTI], ALU.mult)
                return v.tensor_scalar(stats[:, ST_A:ST_A + NTI],
                                       stats[:, ST_A:ST_A + NTI],
                                       -1e-5, 1.000001, ALU.mult, ALU.add)
            step("V", frr, deps=["S"])
            def frr2(s):
                return s.activation(stats[:, ST_B:ST_B + NTI], stats[:, ST_A:ST_A + NTI],
                                    AF.Ln)
            step("S", frr2, deps=["V"])
            def frr2v(v):
                return v.tensor_scalar(stats[:, ST_A:ST_A + NTI],
                                       stats[:, ST_B:ST_B + NTI], -0.5, None, ALU.mult)
            step("V", frr2v, deps=["S"])
            def frr2b(s):
                return s.activation(stats[:, ST_B:ST_B + NTI], stats[:, ST_A:ST_A + NTI],
                                    AF.Exp)
            step("S", frr2b, deps=["V"])
            def frr3(v):
                return v.tensor_tensor(stats[:, RRC:RRC + NTI], stats[:, ST_RS:ST_RS + NTI],
                                       stats[:, ST_B:ST_B + NTI], ALU.mult)
            step("V", frr3, deps=["S"])

            ln1r = sb(p3, "ln1r", [128, D])
            zr = sb(p3, "zr", [8, T])
            zr2 = sb(p3, "zr2", [8, T])
            zrone = sb(p3, "zrone", [8, 128])
            acol = sb(p3, "acol", [128, NTI * H])
            lbacol = sb(p3, "lbacol", [128, NTI * H])
            lam = sb(p3, "lam", [128, NTI * H])
            kcoef = sb(p3, "kcoef", [128, NTI * H])
            lamC = sb(p3, "lamC", [128, NTI * H])
            bwb = sb(p3, "bwb", [128, DCH * H], BF16)
            awb = sb(p3, "awb", [128, DCH * H], BF16)
            step("L", lambda sy: sy.dma_start(out=ln1r[:, :], in_=ln1row[:, :]),
                 deps=[], dma=True)
            load_weight_bf16(bw, bwb)
            load_weight_bf16(aw, awb)
            barrier()

            p3a = contextlib.ExitStack()
            with p3a:
                hblk = sb(p3a, "hblk", [128, DCH * 512], BF16)
                wst3 = sb(p3a, "wst3", [128, DCH * D], BF16)
                qtm = sb(p3a, "qtm", [128, 4 * D], BF16)
                ktm = sb(p3a, "ktm", [128, 4 * D], BF16)
                vtm = sb(p3a, "vtm", [128, 4 * D], BF16)
                qfb = sb(p3a, "qfb", [128, DCH * 512], BF16)
                kfb = sb(p3a, "kfb", [128, DCH * 512], BF16)
                dT = sb(p3a, "dT", [128, 128], BF16)
                dU = sb(p3a, "dU", [128, 128], BF16)
                tpow = sb(p3a, "tpow", [128, 7 * 128], BF16)
                nbuf = sb(p3a, "nbuf", [128, 2 * 128], BF16)
                ybuf = sb(p3a, "ybuf", [128, 256], BF16)
                kp = sb(p3a, "kp", [128, 128], BF16)
                pmt = sb(p3a, "pmt", [128, 128], BF16)
                abuf = sb(p3a, "abuf", [128, 128], BF16)
                pA3 = pst(p3a, "pA3", [128, 512])
                pB3 = pst(p3a, "pB3", [128, 512])
                pC3 = pst(p3a, "pC3", [128, 512])
                pD3 = pst(p3a, "pD3", [128, 512])
                pE3 = pst(p3a, "pE3", [128, 512])
                pT3 = pst(p3a, "pT3", [128, 1024], BF16)

                def build_hblk(bk, dst):
                    for tt in range(4):
                        t = bk * 4 + tt
                        def fh(v, t=t):
                            v.tensor_scalar(ltile[:, 0:D], res[:, t * D:(t + 1) * D],
                                            stats[:, ST_MU + t:ST_MU + t + 1],
                                            stats[:, RRC + t:RRC + t + 1],
                                            ALU.subtract, ALU.mult)
                            return v.tensor_tensor(stg[:, 0:D], ltile[:, 0:D],
                                                   ln1r[:, :], ALU.mult)
                        step("V", fh, deps=["S", "Te"])
                        def fht(te, tt=tt):
                            for i in range(DCH):
                                r = te.transpose(pT3[:, i * 128:(i + 1) * 128],
                                                 stg[:, i * 128:(i + 1) * 128],
                                                 ident[:, :])
                            return r
                        step("Te", fht, deps=["V"])
                        def fhe(v, tt=tt, dst=dst):
                            for i in range(DCH):
                                r = v.tensor_copy(
                                    dst[:, i * 512 + tt * 128:i * 512 + (tt + 1) * 128],
                                    pT3[:, i * 128:(i + 1) * 128])
                            return r
                        step("V", fhe, deps=["Te"])

                for bk in range(4):
                    tok0 = bk * 512
                    build_hblk(bk, hblk)
                    def fz(te, tok0=tok0):
                        for ci in range(DCH):
                            te.matmul(pA3[0:H, 0:512],
                                      bwb[:, ci * H:(ci + 1) * H],
                                      hblk[:, ci * 512:(ci + 1) * 512],
                                      start=(ci == 0), stop=(ci == DCH - 1))
                        for ci in range(DCH):
                            r = te.matmul(pB3[0:H, 0:512],
                                          awb[:, ci * H:(ci + 1) * H],
                                          hblk[:, ci * 512:(ci + 1) * 512],
                                          start=(ci == 0), stop=(ci == DCH - 1))
                        return r
                    step("Te", fz, deps=["V"])
                    def fga0(v):
                        v.tensor_scalar(ltile[0:H, 0:512], pB3[0:H, 0:512],
                                        cvec[0:H, 1:2], None, ALU.add)
                        return v.tensor_scalar(ltile[0:H, 512:1024], pA3[0:H, 0:512],
                                               -1.0, None, ALU.mult)
                    step("V", fga0, deps=["Te"])
                    def fga1(s):
                        return s.activation(dump[0:H, 0:1024], ltile[0:H, 0:1024],
                                            AF.Abs)
                    step("S", fga1, deps=["V"])
                    def fga2(s):
                        return s.activation(ltile2[0:H, 0:1024], dump[0:H, 0:1024],
                                            AF.Exp, scale=-1.0)
                    step("S", fga2, deps=["S"])
                    def fga3(v):
                        return v.tensor_scalar(ltile2[0:H, 0:1024], ltile2[0:H, 0:1024],
                                               1.0, None, ALU.add)
                    step("V", fga3, deps=["S"])
                    def fga4(s):
                        return s.activation(dump[0:H, 0:1024], ltile2[0:H, 0:1024],
                                            AF.Ln)
                    step("S", fga4, deps=["V"])
                    def fga5(v):
                        v.tensor_scalar(ltile2[0:H, 0:1024], ltile[0:H, 0:1024],
                                        0.0, None, ALU.max)
                        return v.tensor_tensor(dump[0:H, 0:1024], ltile2[0:H, 0:1024],
                                               dump[0:H, 0:1024], ALU.add)
                    step("V", fga5, deps=["S"])
                    if bk == 0:
                        step("V", lambda v: v.memset(zrone[:, :], 1.0), deps=["S"])
                    def fgv2(v, tok0=tok0):
                        v.tensor_scalar(ltile[0:H, 0:512], dump[0:H, 0:512],
                                        cvec[0:H, 0:1], None, ALU.mult)
                        for cc2 in range(4):
                            v.tensor_tensor_scan(
                                zr[0:H, tok0 + cc2 * 128:tok0 + (cc2 + 1) * 128],
                                zrone[:, :],
                                ltile[0:H, cc2 * 128:(cc2 + 1) * 128],
                                0.0, ALU.mult, ALU.add)
                        v.tensor_tensor(ltile[0:H, 0:512], dump[0:H, 512:1024],
                                        zr[0:H, tok0:tok0 + 512], ALU.add)
                        return v.tensor_scalar(zr2[0:H, tok0:tok0 + 512],
                                               ltile[0:H, 0:512], -1.0, None, ALU.mult)
                    step("V", fgv2, deps=["S"])
                    for tt in range(4):
                        t = bk * 4 + tt
                        def fac(te, t=t, tok0=tok0, tt=tt):
                            te.transpose(pC3[:, 0:8],
                                         zr[0:8, tok0 + tt * 128:tok0 + (tt + 1) * 128],
                                         identf[0:8, 0:8])
                            return te.transpose(pC3[:, 8:16],
                                                zr2[0:8, tok0 + tt * 128:tok0 + (tt + 1) * 128],
                                                identf[0:8, 0:8])
                        step("Te", fac, deps=["V"])
                        def face(v, t=t):
                            v.tensor_copy(acol[:, t * H:(t + 1) * H], pC3[:, 0:8])
                            return v.tensor_copy(lbacol[:, t * H:(t + 1) * H], pC3[:, 8:16])
                        step("V", face, deps=["Te"])
                    for wv_i, (wap_, kind) in enumerate([(qw, "q"), (kw, "k"), (vw, "v")]):
                        load_weight_bf16(wap_, wst3, dep=("Te", "V"))
                        for tt in range(4):
                            t = bk * 4 + tt
                            for nn2 in range(2):
                                def fpm(te, tt=tt, nn2=nn2):
                                    for ci in range(DCH):
                                        r = te.matmul(
                                            pA3[:, 0:512],
                                            hblk[:, ci * 512 + tt * 128:ci * 512 + (tt + 1) * 128],
                                            wst3[:, ci * D + nn2 * 512:ci * D + (nn2 + 1) * 512],
                                            start=(ci == 0), stop=(ci == DCH - 1))
                                    return r
                                step("Te", fpm, deps=["G", "V", "S"])
                                def fpe(s, tt=tt, nn2=nn2, kind=kind):
                                    dst = {"q": qtm, "k": ktm, "v": vtm}[kind]
                                    return s.activation(
                                        dump[:, nn2 * 512:(nn2 + 1) * 512]
                                        if kind != "v" else
                                        dst[:, tt * D + nn2 * 512:tt * D + (nn2 + 1) * 512],
                                        pA3[:, 0:512], AF.Silu)
                                step("S", fpe, deps=["Te"])
                            if kind != "v":
                                def fl2(v, tt=tt, kind=kind):
                                    v.tensor_tensor(ltile[:, 0:D], dump[:, 0:D],
                                                    dump[:, 0:D], ALU.mult)
                                    for hh in range(H):
                                        v.reduce_sum(stats[:, ST_A + hh:ST_A + hh + 1],
                                                     ltile[:, hh * 128:(hh + 1) * 128],
                                                     AX.X)
                                    return v.tensor_scalar(stats[:, ST_A:ST_A + H],
                                                           stats[:, ST_A:ST_A + H],
                                                           1e-6, None, ALU.add)
                                step("V", fl2, deps=["S"])
                                def fl2b(s):
                                    return s.activation(stats[:, ST_B:ST_B + H],
                                                        stats[:, ST_A:ST_A + H], AF.Ln)
                                step("S", fl2b, deps=["V"])
                                def fl2bv(v):
                                    return v.tensor_scalar(stats[:, ST_B + 8:ST_B + 8 + H],
                                                           stats[:, ST_B:ST_B + H],
                                                           -0.5, None, ALU.mult)
                                step("V", fl2bv, deps=["S"])
                                def fl2bb(s):
                                    return s.activation(stats[:, ST_A + 8:ST_A + 8 + H],
                                                        stats[:, ST_B + 8:ST_B + 8 + H],
                                                        AF.Exp)
                                step("S", fl2bb, deps=["V"])
                                def fl2c(v, tt=tt, kind=kind):
                                    dst = qtm if kind == "q" else ktm
                                    for hh in range(H):
                                        r = v.tensor_scalar(
                                            dst[:, tt * D + hh * 128:tt * D + (hh + 1) * 128],
                                            dump[:, hh * 128:(hh + 1) * 128],
                                            stats[:, ST_A + 8 + hh:ST_A + 9 + hh], None,
                                            ALU.mult)
                                    return r
                                step("V", fl2c, deps=["S"])
                        if kind == "v":
                            continue
                        dstf = qfb if kind == "q" else kfb
                        srct = qtm if kind == "q" else ktm
                        for tt in range(4):
                            def ftf(te, tt=tt, srct=srct):
                                for i in range(DCH):
                                    r = te.transpose(pT3[:, i * 128:(i + 1) * 128],
                                                     srct[:, tt * D + i * 128:tt * D + (i + 1) * 128],
                                                     ident[:, :])
                                return r
                            step("Te", ftf, deps=["V", "S"])
                            def ftfe(v, tt=tt, dstf=dstf):
                                for i in range(DCH):
                                    r = v.tensor_copy(
                                        dstf[:, i * 512 + tt * 128:i * 512 + (tt + 1) * 128],
                                        pT3[:, i * 128:(i + 1) * 128])
                                return r
                            step("V", ftfe, deps=["Te", "L"])
                    def fqd(sy, tok0=tok0):
                        return sy.dma_start(
                            out=qf_d[:, tok0:tok0 + 512]
                                .rearrange("(c p) n -> c p n", p=128).transpose([1, 0, 2]),
                            in_=qfb[:, :].rearrange("p (c n) -> p c n", c=DCH))
                    step("L", fqd, deps=["V"], dma=True)

                    def fexp(s, bk=bk):
                        c0 = bk * 4 * H
                        return s.activation(lam[:, c0:c0 + 4 * H], acol[:, c0:c0 + 4 * H],
                                            AF.Exp)
                    step("S", fexp, deps=["V"])
                    def fac2(te, bk=bk, tok0=tok0):
                        for hh in range(H):
                            r = te.matmul(pC3[:, hh * 4:(hh + 1) * 4],
                                          sel8[:, hh * 128:hh * 128 + 128],
                                          zr[0:8, tok0 + 127:tok0 + 512:128],
                                          start=True, stop=True)
                        return r
                    step("Te", fac2, deps=["S", "V"])
                    def fac3(v, bk=bk):
                        c0 = bk * 4 * H
                        for hh in range(H):
                            for ct2 in range(4):
                                v.tensor_copy(ltile[:, ct2 * H + hh:ct2 * H + hh + 1],
                                              pC3[:, hh * 4 + ct2:hh * 4 + ct2 + 1])
                        return v.tensor_tensor(ltile[:, 32:64], lbacol[:, c0:c0 + 32],
                                               ltile[:, 0:32], ALU.add)
                    step("V", fac3, deps=["Te"])
                    def fac4(s, bk=bk):
                        c0 = bk * 4 * H
                        s.activation(kcoef[:, c0:c0 + 32], ltile[:, 32:64], AF.Exp)
                        return s.activation(lamC[:, c0:c0 + 32], ltile[:, 0:32], AF.Exp)
                    step("S", fac4, deps=["V"])

                    for ct in range(4):
                        c_abs = bk * 4 + ct
                        c0f = ct * 128
                        tok0c = bk * 512 + ct * 128
                        for hh in range(H):
                            ccol = c_abs * H + hh
                            def fx1(te, tok0c=tok0c, hh=hh):
                                te.matmul(pA3[:, 0:128],
                                          sel8[:, hh * 128:(hh + 1) * 128],
                                          zr[0:8, tok0c:tok0c + 128],
                                          start=True, stop=True)
                                return te.matmul(pB3[:, 0:128],
                                                 sel8[:, hh * 128:(hh + 1) * 128],
                                                 zr2[0:8, tok0c:tok0c + 128],
                                                 start=True, stop=True)
                            step("Te", fx1, deps=["V", "S"])
                            def fx2(v, ccol=ccol):
                                v.tensor_scalar(ltile[:, 0:128], pA3[:, 0:128],
                                                lbacol[:, ccol:ccol + 1], 0.0,
                                                ALU.add, ALU.min)
                                return v.tensor_scalar(ltile[:, 128:256], pB3[:, 0:128],
                                                       acol[:, ccol:ccol + 1], 0.0,
                                                       ALU.add, ALU.min)
                            step("V", fx2, deps=["Te"])
                            def fx3(s):
                                s.activation(dT[:, :], ltile[:, 0:128], AF.Exp)
                                return s.activation(dU[:, :], ltile[:, 128:256], AF.Exp)
                            step("S", fx3, deps=["V"])
                            def fkk(te, c0f=c0f, hh=hh):
                                te.matmul(pA3[:, 0:128],
                                          kfb[:, hh * 512 + c0f:hh * 512 + c0f + 128],
                                          kfb[:, hh * 512 + c0f:hh * 512 + c0f + 128],
                                          start=True, stop=True)
                                return te.matmul(pB3[:, 0:128],
                                                 kfb[:, hh * 512 + c0f:hh * 512 + c0f + 128],
                                                 qfb[:, hh * 512 + c0f:hh * 512 + c0f + 128],
                                                 start=True, stop=True)
                            step("Te", fkk, deps=["S", "V"])
                            def fnt(v):
                                v.tensor_tensor(tpow[:, 0:128], dT[:, :], pA3[:, 0:128],
                                                ALU.mult)
                                v.tensor_tensor(tpow[:, 0:128], tpow[:, 0:128],
                                                musU[:, :], ALU.mult)
                                v.tensor_tensor(nbuf[:, 0:128], dU[:, :], pA3[:, 0:128],
                                                ALU.mult)
                                v.tensor_tensor(nbuf[:, 0:128], nbuf[:, 0:128],
                                                musL[:, :], ALU.mult)
                                v.tensor_tensor(pmt[:, :], dT[:, :], pB3[:, 0:128],
                                                ALU.mult)
                                return v.tensor_tensor(pmt[:, :], pmt[:, :],
                                                       muiU[:, :], ALU.mult)
                            step("V", fnt, deps=["Te", "S"])
                            def fy(v, ct=ct, hh=hh, ccol=ccol):
                                v.tensor_copy(ybuf[:, 0:128],
                                              vtm[:, ct * D + hh * 128:ct * D + (hh + 1) * 128])
                                v.tensor_scalar(ybuf[:, 128:256],
                                                ktm[:, ct * D + hh * 128:ct * D + (hh + 1) * 128],
                                                lam[:, ccol:ccol + 1], None, ALU.mult)
                                return v.tensor_scalar(kp[:, :],
                                                       ktm[:, ct * D + hh * 128:ct * D + (hh + 1) * 128],
                                                       kcoef[:, ccol:ccol + 1], None,
                                                       ALU.mult)
                            step("V", fy, deps=["S"])
                            for lvl in range(1, 7):
                                cur = (lvl - 1) % 2
                                nxt = lvl % 2
                                def fsq(te, lvl=lvl, cur=cur):
                                    r = te.matmul(pC3[:, 0:128],
                                                  nbuf[:, cur * 128:(cur + 1) * 128],
                                                  tpow[:, (lvl - 1) * 128:lvl * 128],
                                                  start=True, stop=True)
                                    if lvl < 6:
                                        r = te.matmul(pD3[:, 0:128],
                                                      tpow[:, (lvl - 1) * 128:lvl * 128],
                                                      nbuf[:, cur * 128:(cur + 1) * 128],
                                                      start=True, stop=True)
                                    return r
                                step("Te", fsq, deps=["V"])
                                def fse(v, lvl=lvl, nxt=nxt):
                                    r = v.tensor_copy(tpow[:, lvl * 128:(lvl + 1) * 128],
                                                      pC3[:, 0:128])
                                    if lvl < 6:
                                        r = v.tensor_copy(nbuf[:, nxt * 128:(nxt + 1) * 128],
                                                          pD3[:, 0:128])
                                    return r
                                step("V", fse, deps=["Te"])
                            for lvl in range(6, -1, -1):
                                def fap2(te, lvl=lvl):
                                    return te.matmul(pC3[:, 0:256],
                                                     tpow[:, lvl * 128:(lvl + 1) * 128],
                                                     ybuf[:, 0:256], start=True, stop=True)
                                step("Te", fap2, deps=["V"])
                                def fae(v, lvl=lvl):
                                    return v.tensor_tensor(ybuf[:, 0:256], ybuf[:, 0:256],
                                                           pC3[:, 0:256],
                                                           ALU.add if lvl > 0 else ALU.subtract)
                                step("V", fae, deps=["Te"])
                            def fw2(te):
                                te.matmul(pA3[:, 0:128], ybuf[:, 128:256], kp[:, :],
                                          start=True, stop=True)
                                te.matmul(pB3[:, 0:128], ybuf[:, 128:256], pmt[:, :],
                                          start=True, stop=True)
                                te.matmul(pC3[:, 0:128], kp[:, :], ybuf[:, 0:128],
                                          start=True, stop=True)
                                return te.matmul(pD3[:, 0:128], pmt[:, :], ybuf[:, 0:128],
                                                 start=True, stop=True)
                            step("Te", fw2, deps=["V"])
                            def fw3(v, ccol=ccol):
                                v.tensor_scalar(stg[:, 0:128], ident[:, :],
                                                lamC[:, ccol:ccol + 1], None, ALU.mult)
                                v.tensor_scalar(abuf[:, :], pA3[:, 0:128],
                                                -1.0, None, ALU.mult)
                                v.tensor_tensor(abuf[:, :], abuf[:, :], stg[:, 0:128],
                                                ALU.add)
                                v.tensor_scalar(stg2[:, 0:128], pB3[:, 0:128],
                                                -1.0, None, ALU.mult)
                                v.tensor_copy(ltile[:, 0:128], pC3[:, 0:128])
                                return v.tensor_copy(ltile[:, 128:256], pD3[:, 0:128])
                            step("V", fw3, deps=["Te", "G"])
                            def fw4(g, ccol=ccol):
                                r = []
                                r0 = ccol * 128
                                r.append(g.dma_start(out=aT_d[r0:r0 + 128, :],
                                                     in_=abuf[:, :]))
                                r.append(g.dma_start(out=m2_d[r0:r0 + 128, :],
                                                     in_=stg2[:, 0:128]))
                                r.append(g.dma_start(out=bC_d[r0:r0 + 128, :],
                                                     in_=ltile[:, 0:128]))
                                r.append(g.dma_start(out=oT_d[r0:r0 + 128, :],
                                                     in_=ltile[:, 128:256]))
                                return r
                            step("G", fw4, deps=["V"], dma=True)
                            step("V", lambda v: None, deps=["G"])
            barrier()

            p3c = contextlib.ExitStack()
            with p3c:
                Sst = sb(p3c, "Sst", [128, H * 128], BF16)
                ach = sb(p3c, "ach", [128, H * 128], BF16)
                bch = sb(p3c, "bch", [128, H * 128])
                pA4 = pst(p3c, "pA4", [128, 512])
                step("V", lambda v: v.memset(Sst[:, :], 0.0), deps=["G", "S", "Te"])
                for c_abs in range(NCH):
                    def fca(sy, c_abs=c_abs):
                        r0 = c_abs * H * 128
                        r1 = sy.dma_start(out=ach[:, :].rearrange("p (h n) -> p h n", h=H),
                                          in_=aT_d[r0:r0 + H * 128, :]
                                          .rearrange("(h p) n -> h p n", p=128).transpose([1, 0, 2]))
                        r2 = sy.dma_start(out=bch[:, :].rearrange("p (h n) -> p h n", h=H),
                                          in_=bC_d[r0:r0 + H * 128, :]
                                          .rearrange("(h p) n -> h p n", p=128).transpose([1, 0, 2]))
                        return [r1, r2]
                    step("L", fca, deps=["V"], dma=True)
                    for hh in range(H):
                        def fcm(te, hh=hh):
                            return te.matmul(pA4[:, 0:128],
                                             ach[:, hh * 128:(hh + 1) * 128],
                                             Sst[:, hh * 128:(hh + 1) * 128],
                                             start=True, stop=True)
                        step("Te", fcm, deps=["L", "V"])
                        def fce(v, hh=hh):
                            return v.tensor_tensor(Sst[:, hh * 128:(hh + 1) * 128],
                                                   pA4[:, 0:128],
                                                   bch[:, hh * 128:(hh + 1) * 128],
                                                   ALU.add)
                        step("V", fce, deps=["Te"])
                def fsf(v):
                    return v.tensor_copy(ltile[:, 0:1024], Sst[:, 0:1024])
                step("V", fsf)
                def fsg(g):
                    return g.dma_start(out=cc_in.ap().rearrange("(h p) n -> h p n", p=128).transpose([1, 0, 2]),
                                       in_=ltile[:, 0:1024]
                                       .rearrange("p (h n) -> p h n", h=H))
                step("G", fsg, deps=["V"], dma=True)
                def fccw(g):
                    r = g.collective_compute(
                        "AllGather", ALU.bypass,
                        replica_groups=[list(range(8))],
                        ins=[cc_in.ap().opt()],
                        outs=[cc_out.ap().opt()])
                    r.then_inc(sems["Cc"][0], 1)
                    sems["Cc"][1] += 1
                    return None
                step("G", fccw, deps=["G"])
                IndirectOffsetOnAxis = bass.IndirectOffsetOnAxis
                def frd(g):
                    r = []
                    for hh in range(H):
                        r.append(g.indirect_dma_start(
                            ltile[:, hh * 128:(hh + 1) * 128], None,
                            cc_out.ap(),
                            IndirectOffsetOnAxis(goff[:, hh:hh + 1], 0)))
                    return r
                step("G", frd, deps=["Cc"], dma=True)
                def fmk(v):
                    return v.tensor_scalar(Sst[:, 0:1024], ltile[:, 0:1024],
                                           mcS[:, 0:1], None, ALU.mult)
                step("V", fmk, deps=["G"])

                p3d = contextlib.ExitStack()
                with p3d:
                    gwb = sb(p3d, "gwb", [128, DCH * D], BF16)
                    owb = sb(p3d, "owb", [128, DCH * D], BF16)
                    och = sb(p3d, "och", [128, H * 128])
                    mch = sb(p3d, "mch", [128, H * 128], BF16)
                    qch = sb(p3d, "qch", [128, H * 128], BF16)
                    hfc = sb(p3d, "hfc", [128, DCH * 128], BF16)
                    gatec = sb(p3d, "gatec", [128, D], BF16)
                    ofc = sb(p3d, "ofc", [128, H * 128], BF16)
                    onr = sb(p3d, "onr", [128, D])
                    pA5 = pst(p3d, "pA5", [128, 512])
                    pB5 = pst(p3d, "pB5", [128, 512])
                    pC5 = pst(p3d, "pC5", [128, 512])
                    pD5 = pst(p3d, "pD5", [128, 512])
                    pT5 = pst(p3d, "pT5", [128, 1024], BF16)
                    load_weight_bf16(gw, gwb, dep=("V",))
                    load_weight_bf16(oww, owb, dep=("V",))
                    step("L", lambda sy: sy.dma_start(out=onr[:, :], in_=onormrow[:, :]),
                         dma=True)
                    barrier()
                    for c_abs in range(NCH):
                        tok0c = c_abs * 128
                        def fld2(sy, c_abs=c_abs):
                            r0 = c_abs * H * 128
                            r = []
                            r.append(sy.dma_start(
                                out=ach[:, :].rearrange("p (h n) -> p h n", h=H),
                                in_=aT_d[r0:r0 + H * 128, :]
                                .rearrange("(h p) n -> h p n", p=128).transpose([1, 0, 2])))
                            r.append(sy.dma_start(
                                out=bch[:, :].rearrange("p (h n) -> p h n", h=H),
                                in_=bC_d[r0:r0 + H * 128, :]
                                .rearrange("(h p) n -> h p n", p=128).transpose([1, 0, 2])))
                            r.append(sy.dma_start(
                                out=och[:, :].rearrange("p (h n) -> p h n", h=H),
                                in_=oT_d[r0:r0 + H * 128, :]
                                .rearrange("(h p) n -> h p n", p=128).transpose([1, 0, 2])))
                            r.append(sy.dma_start(
                                out=mch[:, :].rearrange("p (h n) -> p h n", h=H),
                                in_=m2_d[r0:r0 + H * 128, :]
                                .rearrange("(h p) n -> h p n", p=128).transpose([1, 0, 2])))
                            r.append(sy.dma_start(
                                out=qch[:, :].rearrange("p (h n) -> p h n", h=H),
                                in_=qf_d[:, tok0c:tok0c + 128]
                                .rearrange("(h p) n -> h p n", p=128).transpose([1, 0, 2])))
                            return r
                        step("L", fld2, deps=["V", "Te"], dma=True)
                        def fh2(v, c_abs=c_abs):
                            v.tensor_scalar(ltile[:, 0:D],
                                            res[:, c_abs * D:(c_abs + 1) * D],
                                            stats[:, ST_MU + c_abs:ST_MU + c_abs + 1],
                                            stats[:, RRC + c_abs:RRC + c_abs + 1],
                                            ALU.subtract, ALU.mult)
                            return v.tensor_tensor(stg[:, 0:D], ltile[:, 0:D],
                                                   ln1r[:, :], ALU.mult)
                        step("V", fh2, deps=["S", "Te"])
                        def fh3(te):
                            for i in range(DCH):
                                r = te.transpose(pT5[:, i * 128:(i + 1) * 128],
                                                 stg[:, i * 128:(i + 1) * 128],
                                                 ident[:, :])
                            return r
                        step("Te", fh3, deps=["V"])
                        def fh4(v):
                            for i in range(DCH):
                                r = v.tensor_copy(hfc[:, i * 128:(i + 1) * 128],
                                                  pT5[:, i * 128:(i + 1) * 128])
                            return r
                        step("V", fh4, deps=["Te"])
                        def fg5(te):
                            for nn2 in range(2):
                                for ci in range(DCH):
                                    r = te.matmul(
                                        [pA5, pB5][nn2][:, 0:512],
                                        hfc[:, ci * 128:(ci + 1) * 128],
                                        gwb[:, ci * D + nn2 * 512:ci * D + (nn2 + 1) * 512],
                                        start=(ci == 0), stop=(ci == DCH - 1))
                            return r
                        step("Te", fg5, deps=["V", "L"])
                        def fg6(s):
                            s.activation(gatec[:, 0:512], pA5[:, 0:512], AF.Silu)
                            return s.activation(gatec[:, 512:1024], pB5[:, 0:512],
                                                AF.Silu)
                        step("S", fg6, deps=["Te"])
                        for hh in range(H):
                            ccol = c_abs * H + hh
                            def fo1(te, hh=hh):
                                te.matmul(pA5[:, 0:128], qch[:, hh * 128:(hh + 1) * 128],
                                          Sst[:, hh * 128:(hh + 1) * 128],
                                          start=True, stop=True)
                                te.matmul(pB5[:, 0:128], mch[:, hh * 128:(hh + 1) * 128],
                                          Sst[:, hh * 128:(hh + 1) * 128],
                                          start=True, stop=True)
                                return te.matmul(pC5[:, 0:128],
                                                 ach[:, hh * 128:(hh + 1) * 128],
                                                 Sst[:, hh * 128:(hh + 1) * 128],
                                                 start=True, stop=True)
                            step("Te", fo1, deps=["S", "V", "L"])
                            def fo2(v, hh=hh, ccol=ccol):
                                v.tensor_scalar(ltile[:, 0:128], pA5[:, 0:128],
                                                lam[:, ccol:ccol + 1], None, ALU.mult)
                                v.tensor_tensor(ltile[:, 128:256], ltile[:, 0:128],
                                                pB5[:, 0:128], ALU.add)
                                v.tensor_tensor(ltile[:, 0:128], ltile[:, 128:256],
                                                och[:, hh * 128:(hh + 1) * 128], ALU.add)
                                return v.tensor_tensor(Sst[:, hh * 128:(hh + 1) * 128],
                                                       pC5[:, 0:128],
                                                       bch[:, hh * 128:(hh + 1) * 128],
                                                       ALU.add)
                            step("V", fo2, deps=["Te"])
                            def fo3(s):
                                return s.activation(dump[:, 0:128], ltile[:, 0:128],
                                                    AF.Square,
                                                    accum_out=stats[:, ST_A:ST_A + 1])
                            step("S", fo3, deps=["V"])
                            def fo4(v):
                                return v.tensor_scalar(stats[:, ST_A + 2:ST_A + 3],
                                                       stats[:, ST_A:ST_A + 1],
                                                       1.0 / 128.0, 1e-6,
                                                       ALU.mult, ALU.add)
                            step("V", fo4, deps=["S"])
                            def fo5(s):
                                return s.activation(stats[:, ST_B:ST_B + 1],
                                                    stats[:, ST_A + 2:ST_A + 3], AF.Ln)
                            step("S", fo5, deps=["V"])
                            def fo5v(v):
                                return v.tensor_scalar(stats[:, ST_B + 1:ST_B + 2],
                                                       stats[:, ST_B:ST_B + 1],
                                                       -0.5, None, ALU.mult)
                            step("V", fo5v, deps=["S"])
                            def fo5b(s):
                                return s.activation(stats[:, ST_A + 1:ST_A + 2],
                                                    stats[:, ST_B + 1:ST_B + 2], AF.Exp)
                            step("S", fo5b, deps=["V"])
                            def fo6(v, hh=hh):
                                v.tensor_scalar(ltile[:, 128:256], ltile[:, 0:128],
                                                stats[:, ST_A + 1:ST_A + 2], None,
                                                ALU.mult)
                                v.tensor_tensor(ltile[:, 128:256], ltile[:, 128:256],
                                                onr[:, hh * 128:(hh + 1) * 128], ALU.mult)
                                return v.tensor_tensor(stg2[:, 0:128], ltile[:, 128:256],
                                                       gatec[:, hh * 128:(hh + 1) * 128],
                                                       ALU.mult)
                            step("V", fo6, deps=["S"])
                            def fo7(te):
                                return te.transpose(pT5[:, 0:128], stg2[:, 0:128],
                                                    ident[:, :])
                            step("Te", fo7, deps=["V"])
                            def fo8(v, hh=hh):
                                return v.tensor_copy(ofc[:, hh * 128:(hh + 1) * 128],
                                                     pT5[:, 0:128])
                            step("V", fo8, deps=["Te"])
                        def fow(te):
                            for nn2 in range(2):
                                for ci in range(DCH):
                                    r = te.matmul(
                                        [pA5, pB5][nn2][:, 0:512],
                                        ofc[:, ci * 128:(ci + 1) * 128],
                                        owb[:, ci * D + nn2 * 512:ci * D + (nn2 + 1) * 512],
                                        start=(ci == 0), stop=(ci == DCH - 1))
                            return r
                        step("Te", fow, deps=["V"])
                        def fx1(v, c_abs=c_abs):
                            v.tensor_scalar(ltile[:, 0:D],
                                            res[:, c_abs * D:(c_abs + 1) * D],
                                            stats[:, ST_MU + c_abs:ST_MU + c_abs + 1],
                                            stats[:, ST_RS + c_abs:ST_RS + c_abs + 1],
                                            ALU.subtract, ALU.mult)
                            v.scalar_tensor_tensor(ltile2[:, 0:512], pA5[:, 0:512],
                                                   0.5, ltile[:, 0:512],
                                                   ALU.mult, ALU.add)
                            return v.scalar_tensor_tensor(ltile2[:, 512:1024],
                                                          pB5[:, 0:512],
                                                          0.5, ltile[:, 512:1024],
                                                          ALU.mult, ALU.add)
                        step("V", fx1, deps=["Te"])
                        def fx2(sy, c_abs=c_abs):
                            return sy.dma_start(
                                out=x1_d[c_abs * 128:(c_abs + 1) * 128, :],
                                in_=ltile2[:, 0:D])
                        step("L", fx2, deps=["V"], dma=True)
                        step("V", lambda v: None, deps=["L"])
            barrier()

            p3f = contextlib.ExitStack()
            with p3f:
                h2fm = sb(p3f, "h2fm", [128, DCH * T], BF16)
                sil2 = sb(p3f, "sil2", [128, 2 * 1024], BF16)
                acc = sb(p3f, "acc", [128, 8 * 1024])
                fstr = sb(p3f, "fstr", [128, 2 * DCH * 128], BF16)
                pwst = sb(p3f, "pwst", [128, 2 * D], BF16)
                pwsf = sb(p3f, "pwsf", [128, 2 * D])
                ln2r = sb(p3f, "ln2r", [128, D])
                x1t = ltile2
                pp6 = [pst(p3f, "pp6%d" % i, [128, 512]) for i in range(6)]
                p3fb = contextlib.ExitStack()
                pT6 = pst(p3fb, "pT6", [128, 1024], BF16)
                step("L", lambda sy: sy.dma_start(out=ln2r[:, :], in_=ln2row[:, :]),
                     dma=True)
                barrier()
                for t in range(NTI):
                    def fx3(sy, t=t):
                        return sy.dma_start(out=x1t[:, :],
                                            in_=x1_d[t * 128:(t + 1) * 128, :])
                    step("L", fx3, deps=["V", "S"], dma=True)
                    def fr1(s, t=t):
                        return s.activation(dump[:, 0:D], x1t[:, :], AF.Square,
                                            accum_out=stats[:, ST_A:ST_A + 1])
                    step("S", fr1, deps=["L"])
                    def fr2(v):
                        return v.tensor_scalar(stats[:, ST_A + 2:ST_A + 3],
                                               stats[:, ST_A:ST_A + 1],
                                               1.0 / D, 1e-6, ALU.mult, ALU.add)
                    step("V", fr2, deps=["S"])
                    def fr3(s):
                        return s.activation(stats[:, ST_B:ST_B + 1],
                                            stats[:, ST_A + 2:ST_A + 3], AF.Ln)
                    step("S", fr3, deps=["V"])
                    def fr3v(v):
                        return v.tensor_scalar(stats[:, ST_B + 1:ST_B + 2],
                                               stats[:, ST_B:ST_B + 1], -0.5, None,
                                               ALU.mult)
                    step("V", fr3v, deps=["S"])
                    def fr3b(s):
                        return s.activation(stats[:, ST_A + 1:ST_A + 2],
                                            stats[:, ST_B + 1:ST_B + 2], AF.Exp)
                    step("S", fr3b, deps=["V"])
                    def fr4(v):
                        v.tensor_scalar(ltile[:, 0:D], x1t[:, :],
                                        stats[:, ST_A + 1:ST_A + 2], None, ALU.mult)
                        return v.tensor_tensor(stg[:, 0:D], ltile[:, 0:D], ln2r[:, :],
                                               ALU.mult)
                    step("V", fr4, deps=["S"])
                    def fr5(te):
                        for i in range(DCH):
                            r = te.transpose(pT6[:, i * 128:(i + 1) * 128],
                                             stg[:, i * 128:(i + 1) * 128], ident[:, :])
                        return r
                    step("Te", fr5, deps=["V"])
                    def fr6(v, t=t):
                        for i in range(DCH):
                            r = v.tensor_copy(h2fm[:, i * T + t * 128:i * T + (t + 1) * 128],
                                              pT6[:, i * 128:(i + 1) * 128])
                        return r
                    step("V", fr6, deps=["Te"])
                barrier()
                p3fb.close()
                pp7 = [pst(p3f, "pp7%d" % i, [128, 512]) for i in range(2)]
                fcr_p = fcw.rearrange("(c p) o -> c p o", p=128).transpose([1, 0, 2])
                pwr = pww.rearrange("(c p) d -> c p d", p=128)
                use_fc = [None, None]
                sil_tok = [None, None]
                use_pw = [None, None]
                cst_tok = [None, None]
                ac_ev = [None, None]
                fy_tok = None
                rs3_tok = None
                fcps = [[pp6[4], pp6[5]], [pp7[0], pp7[1]]]
                for tbp in range(2):
                    az = step("V", lambda v: v.memset(acc[:, :], 0.0))
                    for icc in range(ICH):
                        b = icc % 2
                        def ffl(g, icc=icc, b=b):
                            return g.dma_start(
                                out=fstr[:, b * DCH * 128:(b + 1) * DCH * 128]
                                    .rearrange("p (c n) -> p c n", c=DCH),
                                in_=fcr_p[:, :, icc * 128:(icc + 1) * 128])
                        lt = step("G", ffl, deps=[use_fc[b]], dma=True,
                                  sem="fc%d" % b)
                        def fpl2(sy, icc=icc, b=b):
                            return sy.dma_start(
                                out=pwsf[:, b * D:(b + 1) * D], in_=pwr[icc])
                        lt2 = step("L", fpl2, deps=[cst_tok[b]], dma=True,
                                   sem="pw%d" % b)
                        def fpc(v, b=b):
                            return v.tensor_copy(pwst[:, b * D:(b + 1) * D],
                                                 pwsf[:, b * D:(b + 1) * D])
                        ct = step("V", fpc, deps=[lt2, use_pw[b]])
                        cst_tok[b] = ct
                        def ffm(te, icc=icc, b=b, tbp=tbp):
                            for sb2 in range(2):
                                tok0 = tbp * 1024 + sb2 * 512
                                for ci in range(DCH):
                                    r = te.matmul(
                                        fcps[b][sb2][:, 0:512],
                                        fstr[:, b * DCH * 128 + ci * 128:
                                             b * DCH * 128 + (ci + 1) * 128],
                                        h2fm[:, ci * T + tok0:ci * T + tok0 + 512],
                                        start=(ci == 0), stop=(ci == DCH - 1))
                            return r
                        mt = step("Te", ffm, deps=[lt, sil_tok[b]])
                        use_fc[b] = mt
                        def ffs(s, b=b):
                            s.activation(sil2[:, b * 1024:b * 1024 + 512],
                                         fcps[b][0][:, 0:512], AF.Silu)
                            return s.activation(sil2[:, b * 1024 + 512:(b + 1) * 1024],
                                                fcps[b][1][:, 0:512], AF.Silu)
                        sil_tok[b] = step("S", ffs, deps=[mt])
                        for tq in range(8):
                            pq = tq % 2
                            deps0 = [ct, sil_tok[b]]
                            if icc == 0 and tq == 0:
                                deps0 += [az]
                            def fpm2(te, icc=icc, b=b, tq=tq, pq=pq):
                                for nn2 in range(2):
                                    r = te.matmul(
                                        pp6[pq * 2 + nn2][:, 0:512],
                                        sil2[:, b * 1024 + tq * 128:
                                             b * 1024 + (tq + 1) * 128],
                                        pwst[:, b * D + nn2 * 512:
                                             b * D + (nn2 + 1) * 512],
                                        start=True, stop=True)
                                return r
                            mtq = step("Te", fpm2, deps=deps0 + [ac_ev[pq]])
                            use_pw[b] = mtq
                            def facc(v, tq=tq, pq=pq):
                                v.tensor_tensor(acc[:, tq * 1024:tq * 1024 + 512],
                                                acc[:, tq * 1024:tq * 1024 + 512],
                                                pp6[pq * 2][:, 0:512], ALU.add)
                                return v.tensor_tensor(
                                    acc[:, tq * 1024 + 512:(tq + 1) * 1024],
                                    acc[:, tq * 1024 + 512:(tq + 1) * 1024],
                                    pp6[pq * 2 + 1][:, 0:512], ALU.add)
                            ac_ev[pq] = step("V", facc, deps=[mtq])
                    for tq in range(8):
                        t_abs = tbp * 8 + tq
                        def fye(sy, t_abs=t_abs):
                            return sy.dma_start(
                                out=x1t[:, 0:D],
                                in_=x1_d[t_abs * 128:(t_abs + 1) * 128, :])
                        xlt = step("L", fye,
                                   deps=[fy_tok] if fy_tok else ["V", "S"],
                                   dma=True, sem="x1t")
                        def fy2(v, tq=tq):
                            v.scalar_tensor_tensor(ltile[:, 0:512],
                                                   acc[:, tq * 1024:tq * 1024 + 512],
                                                   0.5, x1t[:, 0:512],
                                                   ALU.mult, ALU.add)
                            return v.scalar_tensor_tensor(
                                ltile[:, 512:1024],
                                acc[:, tq * 1024 + 512:(tq + 1) * 1024],
                                0.5, x1t[:, 512:1024], ALU.mult, ALU.add)
                        fy_tok = step("V", fy2, deps=[xlt, rs3_tok])
                        residual_add_ln(ltile[:, 0:D], t_abs)
                        rs3_tok = ln_tile.end_stok
            barrier()
        if upto <= 3:
            fin()
            return nc

        p4 = contextlib.ExitStack()
        with p4:
            xfm = sb(p4, "xfm", [128, DCH * T], BF16)
            w1b = sb(p4, "w1b", [128, 2 * DCH * 512], BF16)
            hq = sb(p4, "hq", [128, ICH * 1024], F8E4)
            w2q = sb(p4, "w2q", [128, ICH * 1024], F8E4)
            w2s = sb(p4, "w2s", [128, 2 * 512])
            b2r = sb(p4, "b2r", [128, D])
            b1c = sb(p4, "b1c", [128, ICH])
            yq = sb(p4, "yq", [128, 512], F16)
            pA7 = pst(p4, "pA7", [128, 512])
            pB7 = pst(p4, "pB7", [128, 512])
            pT7 = pst(p4, "pT7", [128, 1024], BF16)

            def p4l(sy):
                r1 = sy.dma_start(out=b2r[:, :], in_=b2row_in[:, :])
                r2 = sy.dma_start(out=b1c[:, :], in_=b1fm_in[:, :])
                return [r1, r2]
            step("L", p4l, dma=True)
            w2r = w2T.rearrange("(c p) o -> c p o", p=128)
            w2ct = [None, None]
            for hc2 in range(2 * ICH):
                hc, hf2 = hc2 // 2, hc2 % 2
                b = hc2 % 2
                def fwl(sy, hc=hc, hf2=hf2, b=b):
                    return sy.dma_start(out=w2s[:, b * 512:(b + 1) * 512],
                                        in_=w2r[hc][:, hf2 * 512:(hf2 + 1) * 512])
                lt = step("L", fwl, deps=[w2ct[b]], dma=True, sem="w2%d" % b)
                def fwc(v, hc=hc, hf2=hf2, b=b):
                    return v.tensor_copy(
                        w2q[:, hc * 1024 + hf2 * 512:hc * 1024 + (hf2 + 1) * 512],
                        w2s[:, b * 512:(b + 1) * 512])
                w2ct[b] = step("V", fwc, deps=[lt])
            for t in range(NTI):
                ln_tile(res[:, t * D:(t + 1) * D], ST_MU + t, ST_RS + t)
                def fap4(v, t=t):
                    return v.tensor_scalar(stg[:, 0:D], res[:, t * D:(t + 1) * D],
                                           stats[:, ST_MU + t:ST_MU + t + 1],
                                           stats[:, ST_RS + t:ST_RS + t + 1],
                                           ALU.subtract, ALU.mult)
                step("V", fap4, deps=["S"])
                def ftp4(te):
                    for i in range(DCH):
                        r = te.transpose(pT7[:, i * 128:(i + 1) * 128],
                                         stg[:, i * 128:(i + 1) * 128], ident[:, :])
                    return r
                step("Te", ftp4, deps=["V"])
                def fev4(v, t=t):
                    for i in range(DCH):
                        r = v.tensor_copy(xfm[:, i * T + t * 128:i * T + (t + 1) * 128],
                                          pT7[:, i * 128:(i + 1) * 128])
                    return r
                step("V", fev4, deps=["Te"])
            barrier()
            w1r4 = w1T.rearrange("(c p) (q o) -> q c p o", p=128, o=1024)
            fe_tok = None
            rs_tok = None
            ld_tok = [None, None]
            use_tok = [None, None]
            ev_tok = [None, None]
            for tbp in range(2):
                for qtr in range(4):
                    for hf in range(2):
                        b = (qtr * 2 + hf) % 2
                        def fwl4(g, qtr=qtr, hf=hf, b=b):
                            return [g.dma_start(
                                out=w1b[:, b * DCH * 512 + ci * 512:
                                        b * DCH * 512 + (ci + 1) * 512],
                                in_=w1r4[qtr, ci][:, hf * 512:(hf + 1) * 512])
                                for ci in range(DCH)]
                        ld_tok[b] = step("G", fwl4, deps=[use_tok[b]], dma=True,
                                         sem="w1%d" % b)
                        for tbi in range(2):
                            tok0 = tbp * 1024 + tbi * 512
                            for h4 in range(4):
                                hh2 = hf * 4 + h4
                                hc = qtr * DCH + hh2
                                pb = (tbi * 4 + h4) % 2
                                def fm4(te, h4=h4, b=b, tok0=tok0, pb=pb):
                                    for ci in range(DCH):
                                        r = te.matmul(
                                            [pA7, pB7][pb][:, 0:512],
                                            w1b[:, b * DCH * 512 + ci * 512 + h4 * 128:
                                                b * DCH * 512 + ci * 512 + (h4 + 1) * 128],
                                            xfm[:, ci * T + tok0:ci * T + tok0 + 512],
                                            start=(ci == 0), stop=(ci == DCH - 1))
                                    return r
                                mt = step("Te", fm4,
                                          deps=[ld_tok[b], ev_tok[pb], fe_tok])
                                fe_tok = None
                                def fe4(s, hc=hc, tbi=tbi, pb=pb):
                                    return s.activation(
                                        hq[:, hc * 1024 + tbi * 512:
                                            hc * 1024 + (tbi + 1) * 512],
                                        [pA7, pB7][pb][:, 0:512], AF.Relu,
                                        bias=b1c[:, hc:hc + 1])
                                ev_tok[pb] = step("S", fe4, deps=[mt])
                                use_tok[b] = mt
                for tbi in range(2):
                    for ti in range(4):
                        t_abs = tbp * 8 + tbi * 4 + ti
                        mts = []
                        for nn2 in range(2):
                            def fm5(te, tbi=tbi, ti=ti, nn2=nn2):
                                pp = [pA7, pB7][nn2]
                                for hc in range(ICH):
                                    r = te.matmul(
                                        pp[:, 0:512],
                                        hq[:, hc * 1024 + tbi * 512 + ti * 128:
                                            hc * 1024 + tbi * 512 + (ti + 1) * 128],
                                        w2q[:, hc * 1024 + nn2 * 512:
                                            hc * 1024 + (nn2 + 1) * 512],
                                        start=(hc == 0), stop=(hc == ICH - 1))
                                return r
                            mts.append(step("Te", fm5,
                                            deps=[ev_tok[0], ev_tok[1], fe_tok]))
                        def fe5(v):
                            v.tensor_tensor(ltile[:, 0:512], pA7[:, 0:512],
                                            b2r[:, 0:512], ALU.add)
                            v.tensor_tensor(ltile[:, 512:1024], pB7[:, 0:512],
                                            b2r[:, 512:1024], ALU.add)
                            v.tensor_copy(yq[:, :], ltile[:, 0:512])
                            v.tensor_copy(ltile[:, 0:512], yq[:, :])
                            v.tensor_copy(yq[:, :], ltile[:, 512:1024])
                            return v.tensor_copy(ltile[:, 512:1024], yq[:, :])
                        fe_tok = step("V", fe5, deps=mts + [rs_tok])
                        residual_add_ln(ltile[:, 0:D], t_abs)
                        rs_tok = ln_tile.end_stok
            barrier()
        fin()
    return nc


_CACHE = {}


def kernel(**inputs):
    import os
    upto = int(os.environ.get("KMK_UPTO", "9"))
    dbg = os.environ.get("KMK_DBG", "")
    if _CACHE.get("key") != (upto, dbg):
        _CACHE["nc"] = build_program(upto, dbg)
        _CACHE["key"] = (upto, dbg)
    nc = _CACHE["nc"]
    f32 = np.float32
    x = np.ascontiguousarray(inputs["L0_attn_x"], dtype=f32)

    ident = np.eye(128, dtype=f32)
    causal = np.where(np.tril(np.ones((128, 128), bool)), 0.0, NEG).astype(f32)
    musU_ = np.triu(np.ones((128, 128), f32), 1)
    musL_ = np.tril(np.ones((128, 128), f32), -1)
    muiU_ = np.triu(np.ones((128, 128), f32), 0)
    pad = lambda a: np.pad(np.asarray(a, f32), (0, 128 - len(a)))[:, None]
    shared = {
        "wq0": np.ascontiguousarray(inputs["L0_attn_Wq"], f32),
        "wk0": np.ascontiguousarray(inputs["L0_attn_Wk"], f32),
        "wv0": np.ascontiguousarray(inputs["L0_attn_Wv"], f32),
        "wgT": np.ascontiguousarray(np.asarray(inputs["L0_ffn_w_gate"], f32).T),
        "wuT": np.ascontiguousarray(np.asarray(inputs["L0_ffn_w_up"], f32).T),
        "wdT": np.ascontiguousarray(np.asarray(inputs["L0_ffn_w_down"], f32).T),
        "qw": np.ascontiguousarray(inputs["L1_attn_qw"], f32),
        "kw": np.ascontiguousarray(inputs["L1_attn_kw"], f32),
        "vw": np.ascontiguousarray(inputs["L1_attn_vw"], f32),
        "gw": np.ascontiguousarray(inputs["L1_attn_gw"], f32),
        "oww": np.ascontiguousarray(inputs["L1_attn_ow"], f32),
        "bw": np.ascontiguousarray(inputs["L1_attn_bw"], f32),
        "aw": np.ascontiguousarray(inputs["L1_attn_aw"], f32),
        "fcw": np.ascontiguousarray(inputs["L1_attn_fcw"], f32),
        "pww": np.ascontiguousarray(inputs["L1_attn_pw"], f32),
        "w1T": np.ascontiguousarray(np.asarray(inputs["L1_ffn_w1"], f32).T),
        "w2T": np.ascontiguousarray(np.asarray(inputs["L1_ffn_w2"], f32).T),
        "ln1row": np.broadcast_to(np.asarray(inputs["L1_attn_ln1_w"], f32), (128, D)).copy(),
        "ln2row": np.broadcast_to(np.asarray(inputs["L1_attn_ln2_w"], f32), (128, D)).copy(),
        "onormrow": np.broadcast_to(np.tile(np.asarray(inputs["L1_attn_onorm_w"], f32), H),
                                    (128, D)).copy(),
        "b1fm_in": np.ascontiguousarray(np.asarray(inputs["L1_ffn_b1"], f32)
                                        .reshape(ICH, 128).T),
        "b2row_in": np.broadcast_to(np.asarray(inputs["L1_ffn_b2"], f32), (128, D)).copy(),
        "alog_col": pad(inputs["L1_attn_A_log"]),
        "dtb_col": pad(inputs["L1_attn_dt_bias"]),
        "identf_in": ident,
        "causal_in": causal,
        "musU_in": musU_, "musL_in": musL_, "muiU_in": muiU_,
        "sel8_in": np.concatenate(
            [np.where(np.arange(8)[:, None] == i, 1.0, 0.0)
             * np.ones((1, 128)) for i in range(8)], axis=1).astype(f32),
    }
    in_maps = []
    for c in range(8):
        s, hf = c // 2, c % 2
        m = dict(shared)
        m["x_own"] = np.ascontiguousarray(x[s, hf * T:(hf + 1) * T])
        m["x_prev"] = (np.ascontiguousarray(x[s, 0:T]) if hf == 1
                       else np.zeros((T, D), f32))
        m["maskattn"] = np.full((128, 1), 0.0 if hf == 1 else NEG, f32)
        m["maskstate"] = np.full((128, 1), 1.0 if hf == 1 else 0.0, f32)
        slot = c - 1 if hf == 1 else c
        gidx = (slot * H * HD + np.arange(H)[None, :] * HD
                + np.arange(128)[:, None]).astype(np.int32)
        m["gather_off"] = gidx
        in_maps.append(m)
    r = run_bass_kernel_spmd(nc, in_maps, core_ids=list(range(8)))
    kernel._last = r
    outa = np.empty((4, 4096, D), f32)
    for c in range(8):
        outa[c // 2, (c % 2) * T:(c % 2 + 1) * T] = r.results[c]["out"]
    return outa
